# version 04
# baseline (speedup 1.0000x reference)
import os
import numpy as np
import ml_dtypes
import concourse.bass as bass
import concourse.mybir as mybir
from concourse.bass_utils import run_bass_kernel_spmd

F32 = mybir.dt.float32
BF16 = mybir.dt.bfloat16
AF = mybir.ActivationFunctionType
ALU = mybir.AluOpType
AX = mybir.AxisListType

T = 2048
D = 1024
NCORE = 8
BPC = 2
NC = 8
NKC = 8
NH = 16

bf16 = ml_dtypes.bfloat16


def build_w1():
    v = np.arange(32)
    km = np.arange(64)
    th = 2 * np.pi * np.outer(v, km) / 64.0
    W = np.zeros((64, 128), np.float64)
    W[0:32, 0:64] = np.cos(th)
    W[32:64, 0:64] = np.sin(th)
    W[0:32, 64:128] = -np.sin(th)
    W[32:64, 64:128] = np.cos(th)
    return np.concatenate([W, W], axis=0).astype(bf16)


def build_w2():
    u = np.arange(64)
    kd = np.arange(64)
    out = np.zeros((64, 64, 2, 2, 64), np.float64)
    for km in range(64):
        phi = 2 * np.pi * np.outer(u, (km + 64 * kd)) / 4096.0
        out[:, km, 0, 0, :] = np.cos(phi)
        out[:, km, 0, 1, :] = -np.sin(phi)
        out[:, km, 1, 0, :] = np.sin(phi)
        out[:, km, 1, 1, :] = np.cos(phi)
    flat = out.reshape(64, 64 * 256)
    return np.concatenate([flat, flat], axis=0).astype(bf16)


def build_wi1():
    kd = np.arange(64)
    tu = np.arange(64)
    psi = 2 * np.pi * np.outer(kd, tu) / 64.0
    W = np.zeros((64, 192), np.float64)
    W[:, 0:64] = np.cos(psi)
    W[:, 64:128] = np.sin(psi)
    W[:, 128:192] = -np.sin(psi)
    return np.concatenate([W, W], axis=0).astype(bf16)


def build_wi2():
    km = np.arange(64)
    tv = np.arange(32)
    out = np.zeros((2, 64, 64, 2, 32), np.float64)
    for tu in range(64):
        chi = 2 * np.pi * np.outer(km, (64 * tv + tu)) / 4096.0
        out[0, :, tu, 0, :] = np.cos(chi)
        out[1, :, tu, 0, :] = -np.sin(chi)
        out[0, :, tu, 1, :] = np.sin(chi)
        out[1, :, tu, 1, :] = np.cos(chi)
    return out.reshape(128, 4096).astype(bf16)


def build_pos_scrambled():
    pos_t = np.arange(T, dtype=np.float64)[:, None]
    i = np.arange(D)[None, :]
    angle = pos_t / np.power(10000.0, (2 * (i // 2)).astype(np.float64) / D)
    pos = np.where(i % 2 == 0, np.sin(angle), np.cos(angle))
    t = np.arange(T)
    v, u = t // 64, t % 64
    uh, u32 = u // 32, u % 32
    f = u32 * 64 + uh * 32 + v
    pos_sc = np.zeros((D, T), np.float32)
    pos_sc[:, f] = pos.T.astype(np.float32)
    return pos_sc


EVD = set(os.environ.get("MK_EVD", "azi"))


def build_nc(dbg=()):
    dbg = set(dbg)
    nc = bass.Bass("TRN2", target_bir_lowering=False)

    def din(name, shape, dtype=F32):
        return nc.dram_tensor(name, shape, dtype, kind="ExternalInput").ap()

    x_nat = din("x_nat", [D, BPC * T])
    x_til = din("x_til", [NC * 128, 4096])
    wp = din("wp", [D, 4 * D])
    bp_d = din("bp_d", [1, 4 * D])
    wf_sl = din("wf_sl", [D, 384])
    bf_sl = din("bf_sl", [1, 384])
    pos_d = din("pos_d", [D, T])
    wq_d = din("wq_d", [D, D])
    wk_d = din("wk_d", [D, D])
    wv_d = din("wv_d", [D, D])
    wo_d = din("wo_d", [D, D])
    ws1_d = din("ws1_d", [D, 256])
    bs1_d = din("bs1_d", [1, 256])
    ws2t_d = din("ws2t_d", [128, 2])
    bs2_d = din("bs2_d", [1, 1])
    w1c = din("w1c", [128, 128], BF16)
    w2c = din("w2c", [128, 16384], BF16)
    wi1c = din("wi1c", [128, 192], BF16)
    wi2c = din("wi2c", [128, 4096], BF16)
    identc = din("identc", [128, 128], F32)

    out_t = nc.dram_tensor("out_t", [NC * 128, 4096], F32, kind="ExternalOutput").ap()
    f_slice = nc.dram_tensor("f_slice", [384, 8192], BF16).ap()
    f_full = nc.dram_tensor("f_full", [3072, 8192], BF16).ap()
    y3_dram = nc.dram_tensor("y3_dram", [NC * 128, 4096], BF16).ap()

    dbg_t = {}
    for name, shape in [("xtilde", [128, NKC * 4096]), ("tile0", [128, 4096]),
                        ("tile1", [128, 4096]), ("asb", [128, 8192]),
                        ("wspec", [128, 8192]), ("y1", [128, 4096]),
                        ("y3", [128, 4096]), ("groups", [128, 2048]),
                        ("scores", [1, 256]), ("mask", [128, 2]),
                        ("outg", [128, 2048]), ("x1c0", [128, 4096]),
                        ("fsl", [384, 8192])]:
        if name in dbg:
            dbg_t[name] = nc.dram_tensor("dbg_" + name, shape, F32,
                                         kind="ExternalOutput").ap()

    ex = []

    def sbuf(name, shape, dtype):
        h = nc.sbuf_tensor(name, shape, dtype)
        tt = h.__enter__()
        ex.append(h)
        return tt

    def psumt(name, shape):
        h = nc.psum_tensor(name, shape, F32)
        tt = h.__enter__()
        ex.append(h)
        return tt

    xt = sbuf("xt", [128, NKC * 4096], BF16)
    w1s = sbuf("w1s", [128, 128], BF16)
    w2s = sbuf("w2s", [128, 16384], BF16)
    wi1s = sbuf("wi1s", [128, 192], BF16)
    wi2s = sbuf("wi2s", [128, 4096], BF16)
    idents = sbuf("idents", [128, 128], F32)
    tile = [sbuf(f"tile{i}", [128, 4096], BF16) for i in range(4)]
    absb = sbuf("absb", [128, 8192], BF16)
    wsb = sbuf("wsb", [128, 8192], BF16)
    fsb = [sbuf(f"fsb{i}", [128, 1024], BF16) for i in range(2)]
    wps = [sbuf(f"wps{i}", [128, 2048], BF16) for i in range(2)]
    m128 = sbuf("m128", [128, 4096], BF16)
    r128 = sbuf("r128", [128, 4096], BF16)
    onesb = sbuf("onesb", [128, 256], BF16)
    ones32 = sbuf("ones32", [128, 128], F32)
    epsc = sbuf("epsc", [128, 1], F32)
    rowf = sbuf("rowf", [1, 512], BF16)
    ws2s = sbuf("ws2s", [128, 2], BF16)
    st12 = sbuf("st12", [128, 128], F32)

    pp = [psumt(f"pp{i}", [128, 512]) for i in range(2)]
    pa = [psumt(f"pa{i}", [128, 512]) for i in range(2)]
    px = [psumt(f"px{d}", [128, 512]) for d in range(2)]
    pyp = [psumt(f"py{d}", [128, 256]) for d in range(2)]

    xv = [absb.ap().bitcast(F32), wsb.ap().bitcast(F32)]

    EV = {}
    cnt = {}
    streams = {k: [] for k in ("pe", "act", "dve", "pool", "sp")}

    anon = [0]
    WAITLOG = []

    def emit(engine, fn, waits=(), inc=None, n=1, fam=None, serial=False):
        w = []
        for x in waits:
            if x is None:
                continue
            WAITLOG.append(x if isinstance(x, str) else None)
            w.append(EV[x] if isinstance(x, str) else x)
        rec = None
        if inc == "cc_done":
            cnt["s_cc"] = 1
            EV[inc] = ("s_cc", 1)
            streams[engine].append((fn, w, ("s_cc", 1), inc))
            return
        if n == 16:
            if inc is None:
                anon[0] += 1
                inc = "_anon%d" % anon[0]
            s = "d_" + (fam if fam is not None else engine + "_anon")
            cnt[s] = cnt.get(s, 0) + 16
            EV[inc] = (s, cnt[s])
            streams[engine].append((fn, w, (s, 16), inc))
            return
        if engine in ("dve", "act"):
            s = "s_" + engine
            prev = cnt.get(s, 0)
            tgt = prev if serial else prev - 1
            if tgt >= 1:
                w.append((s, tgt))
            if inc is None:
                anon[0] += 1
                inc = "_anon%d" % anon[0]
        if inc is not None:
            s = "s_" + engine
            cnt[s] = cnt.get(s, 0) + n
            EV[inc] = (s, cnt[s])
            rec = (s, n)
        streams[engine].append((fn, w, rec, inc))

    def ev_now(engine):
        s = "s_" + engine
        return (s, cnt.get(s, 0))

    def dma(engine, dst, src, waits=(), inc=None, fam=None):
        emit(engine, (lambda d=dst, s=src: (lambda eng: eng.dma_start(d, s)))(),
             waits, inc, n=16, fam=fam)

    emit("dve", lambda eng: eng.memset(onesb[:, :], 1.0), [], None)
    emit("dve", lambda eng: eng.memset(ones32[:, :], 1.0), [], None)
    emit("dve", lambda eng: eng.memset(epsc[:, :], 1e-5), [], "ones_set")
    emit("dve", lambda eng: eng.memset(
        bass.AP(xt, 0, [[NKC * 4096, 1], [1, 8192]]), 0), [], "stz0")
    dma("sp", w1s[:, :], w1c, [], "ld_w1", fam="c0")
    dma("sp", wi1s[:, :], wi1c, [], "ld_wi1", fam="c0")
    dma("sp", wi2s[:, :], wi2c, [], "ld_wi2", fam="c0")
    dma("sp", idents[:, :], identc, [], "ld_id", fam="c0")
    for _nm in ("ld_w1", "ld_wi1", "ld_wi2", "ld_id"):
        EV[_nm] = EV["ld_id"]
    dma("pool", rowf[0:1, 0:384], bf_sl, [], "ld_bf", fam="bf")
    dma("pool", ws2s[:, :], ws2t_d, [], "ld_ws2", fam="ws2")
    for kc in range(NKC):
        dma("pool", bass.AP(w2s, kc * 2048, [[16384, 128], [1, 2048]]),
            pos_d[kc * 128:(kc + 1) * 128, :], [], "ld_pos%d" % kc, fam="pos")
        dma("pool", bass.AP(tile[3], kc * 384, [[4096, 128], [1, 384]]),
            wf_sl[kc * 128:(kc + 1) * 128, :], [], "ld_wf%d" % kc, fam="wf")

    rowx = lambda off, n: bass.AP(xt, off, [[NKC * 4096, 1], [1, n]])
    strot = [pa[0], pa[1], px[0], px[1]]
    for kc in range(NKC):
        bufi = kc % 2
        w = ["st_dv_%d" % (kc - 2)] if kc >= 2 else []
        dma("sp", xv[bufi][:, :], x_nat[kc * 128:(kc + 1) * 128, :], w, "ld_x%d" % kc,
            fam="xa" if bufi == 0 else "xw")
        emit("dve",
             (lambda b=bufi: (lambda eng: eng.tensor_copy(tile[0][:, :], xv[b][:, :])))(),
             ["ld_x%d" % kc, "ones_set", "stz0"]
             + (["st_pe_%d" % (kc - 1)] if kc >= 1 else []), "cast_%d" % kc)
        emit("act",
             (lambda b=bufi: (lambda eng: eng.activation(tile[1][:, :], xv[b][:, :],
                                                         AF.Square)))(),
             ["ld_x%d" % kc] + (["st_pe_%d" % (kc - 1)] if kc >= 1 else []),
             "sqr_%d" % kc)
        for nsl in range(8):
            for mom in range(2):
                ridx = (nsl * 2 + mom) % 4
                def mk(kc=kc, nsl=nsl, mom=mom, ridx=ridx):
                    def fn(eng):
                        return eng.matmul(strot[ridx][:, :], onesb[:, 0:128],
                                          tile[mom][:, nsl * 512:(nsl + 1) * 512],
                                          start=True, stop=True)
                    return fn
                w = ["cast_%d" % kc, "sqr_%d" % kc] if (nsl == 0 and mom == 0) else []
                if (kc, nsl, mom) != (0, 0, 0):
                    pr = (kc * 16 + nsl * 2 + mom) - 4
                    if pr >= 0:
                        w.append("st_add_%d" % pr)
                emit("pe", mk(), w, "st_mm_%d" % (kc * 16 + nsl * 2 + mom))
                def mka(kc=kc, nsl=nsl, mom=mom, ridx=ridx):
                    def fn(eng):
                        return eng.tensor_add(rowx(mom * 4096 + nsl * 512, 512),
                                              rowx(mom * 4096 + nsl * 512, 512),
                                              strot[ridx][0:1, :])
                    return fn
                emit("dve", mka(), ["st_mm_%d" % (kc * 16 + nsl * 2 + mom)],
                     "st_add_%d" % (kc * 16 + nsl * 2 + mom))
        EV["st_pe_%d" % kc] = ev_now("pe")
        EV["st_dv_%d" % kc] = ev_now("dve")
    EV["stats_done"] = ev_now("dve")
    trow = lambda off, n: bass.AP(tile[0], off, [[4096, 1], [1, n]])
    emit("dve", lambda eng: eng.tensor_scalar_mul(rowx(0, 4096), rowx(0, 4096),
                                                  -1.0 / 1024.0), ["stats_done"], None,
         serial=True)
    emit("dve", lambda eng: eng.tensor_scalar_mul(rowx(4096, 4096), rowx(4096, 4096),
                                                  1.0 / 1024.0), [], None, serial=True)
    emit("dve", lambda eng: eng.tensor_mul(trow(0, 4096), rowx(0, 4096), rowx(0, 4096)),
         [], None, serial=True)
    emit("dve", lambda eng: eng.tensor_sub(rowx(4096, 4096), rowx(4096, 4096),
                                           trow(0, 4096)), [], "var_done", serial=True)
    emit("act", lambda eng: eng.activation(rowx(4096, 4096), rowx(4096, 4096), AF.Sqrt,
                                           bias=epsc[0:1, 0:1]), ["var_done"], "sd_done")
    emit("dve", lambda eng: eng.reciprocal(rowx(4096, 4096), rowx(4096, 4096)),
         ["sd_done"], "r_done")
    for half, dst in ((0, m128), (1, r128)):
        for nsl in range(8):
            def mkb(half=half, nsl=nsl):
                def fn(eng):
                    return eng.matmul(pp[nsl % 2][:, :], onesb[0:1, 0:128],
                                      rowx(half * 4096 + nsl * 512, 512),
                                      start=True, stop=True)
                return fn
            w = ["r_done"] if (half == 0 and nsl == 0) else []
            if nsl >= 2:
                w.append("bc_%d_%d" % (half, nsl - 2))
            elif half == 1:
                w.append("bc_0_%d" % (6 + nsl))
            emit("pe", mkb(), w, "bcm_%d_%d" % (half, nsl))
            def mkbe(half=half, nsl=nsl, dst=dst):
                def fn(eng):
                    return eng.activation(dst[:, nsl * 512:(nsl + 1) * 512],
                                          pp[nsl % 2][:, :], AF.Identity)
                return fn
            emit("act", mkbe(), ["bcm_%d_%d" % (half, nsl)], "bc_%d_%d" % (half, nsl))
    emit("dve", lambda eng: eng.tensor_mul(m128[:, :], m128[:, :], r128[:, :]),
         ["bc_1_7"], "mr_mul")
    scr_ap = lambda base: bass.AP(xt, base,
                                  [[NKC * 4096, 128], [32, 2], [1, 32], [64, 2], [128, 32]])
    for kc in range(NKC):
        bufi = kc % 2
        w = ["st_pe_%d" % (NKC - 1)] if kc < 2 else ["xt_%d" % (kc - 2)]
        if kc == 0:
            w = w + ["bcm_1_7", "stats_done"]
        dma("sp", xv[bufi][:, :], x_nat[kc * 128:(kc + 1) * 128, :], w, "l2x_%d" % kc,
            fam="xa" if bufi == 0 else "xw")
        emit("dve", (lambda kc=kc, b=bufi: (lambda eng: eng.tensor_mul(
            scr_ap(kc * 4096), xv[b][:, :], r128.ap())))(),
            ["l2x_%d" % kc, "mr_mul"], None)
        emit("dve", (lambda kc=kc: (lambda eng: eng.tensor_add(
            scr_ap(kc * 4096), scr_ap(kc * 4096), m128.ap())))(), [], "xt_%d" % kc,
            serial=True)
    EV["xt_all"] = ev_now("dve")
    dma("pool", bass.AP(m128, 64 * 4096, [[4096, 1], [1, 4096]]), bp_d,
        ["xt_all"], "ld_bp", fam="bp")
    if "xtilde" in dbg_t:
        dma("pool", dbg_t["xtilde"], xt.ap(), ["xt_all"], "d_xt", fam="dbgxt")

    for s in range(3):
        emit("dve", (lambda s=s: (lambda eng: eng.memset(tile[s][:, :], 0)))(),
             ["st_pe_%d" % (NKC - 1), "stats_done"], "fz_%d" % s)
    for seti in range(64):
        u32, uh = seti // 2, seti % 2
        rot = seti % 2
        for kc in range(NKC):
            def mk(u32=u32, uh=uh, kc=kc, rot=rot):
                def fn(eng):
                    return eng.matmul(
                        pp[rot][uh * 64:uh * 64 + 32, 0:384],
                        bass.AP(w2s, kc * 2048 + u32 * 64 + uh * 32,
                                [[16384, 128], [1, 32]]),
                        bass.AP(tile[3], kc * 384, [[4096, 128], [1, 384]]),
                        start=(kc == 0), stop=False, tile_position=(0, uh * 64))
                return fn
            w = []
            if kc == 0:
                if seti == 0:
                    w = ["ld_pos%d" % (NKC - 1), "ld_wf%d" % (NKC - 1), "bc_1_7"]
                if seti >= 2:
                    w.append("fpe_%d" % (seti - 2))
            emit("pe", mk(), w, None)
        def mkb(uh=uh, rot=rot):
            def fn(eng):
                return eng.matmul(pp[rot][uh * 64:uh * 64 + 32, 0:384],
                                  onesb[0:1, 0:32], rowf[0:1, 0:384],
                                  start=False, stop=True, tile_position=(0, uh * 64))
            return fn
        emit("pe", mkb(), ["ld_bf"] if seti == 0 else [], "fpm_%d" % seti)
        for s in range(3):
            def mke(u32=u32, uh=uh, s=s, rot=rot):
                def fn(eng):
                    dst = bass.AP(tile[s], (uh * 64) * 4096 + u32, [[4096, 32], [32, 128]])
                    src = bass.AP(pp[rot], (uh * 64) * 512 + s * 128, [[512, 32], [1, 128]])
                    return eng.activation(dst, src, AF.Identity)
                return fn
            emit("act", mke(), ["fpm_%d" % seti, "fz_%d" % s],
                 "fpe_%d" % seti if s == 2 else None)
    EV["fproj_done"] = ev_now("act")
    dma("sp", w2s[:, :], w2c, ["fproj_done"], "ld_w2", fam="w2")

    def emit_fwd(name, ytile, l1_waits, fscale):
        for dd in range(128):
            rot = (dd // 8) % 2
            blk = (dd // 2) % 4
            for uh in range(2):
                base = (dd % 2) * 64 + uh * 32
                def mk(dd=dd, uh=uh, base=base, rot=rot, blk=blk, yt=ytile):
                    def fn(eng):
                        return eng.matmul(
                            pa[rot][base:base + 32, blk * 128:blk * 128 + 128],
                            yt[uh * 64:(uh + 1) * 64, dd * 32:(dd + 1) * 32],
                            w1s[uh * 64:(uh + 1) * 64, :],
                            start=True, stop=True, tile_position=(uh * 64, base))
                    return fn
                w = []
                if uh == 0 and dd % 8 == 0:
                    if dd == 0:
                        w = list(l1_waits) + ["ld_w1"]
                    if dd >= 16:
                        w.append("%s_ae_%d" % (name, dd // 8 - 2))
                emit("pe", mk(), w,
                     "%s_l1_%d" % (name, dd // 8) if (dd % 8 == 7 and uh == 1) else None)
            if dd % 8 == 7:
                g = dd // 8
                def mke(g=g, rot=(g % 2)):
                    def fn(eng):
                        dst = bass.AP(absb, g * 4, [[8192, 128], [1, 4], [64, 128]])
                        src = bass.AP(pa[rot], 0, [[512, 128], [128, 4], [1, 128]])
                        if g % 2 == 0 or "a" not in EVD:
                            return eng.activation(dst, src, AF.Identity)
                        return eng.tensor_copy(dst, src)
                    return fn
                emit("act" if (g % 2 == 0 or "a" not in EVD) else "dve", mke(),
                     ["%s_l1_%d" % (name, g)], "%s_ae_%d" % (name, g))
        for kmq in range(32):
            rot = kmq % 2
            for dpar in range(2):
                for kmi in range(2):
                    km = kmq * 2 + kmi
                    for riout in range(2):
                        for ri in range(2):
                            def mk(km=km, dpar=dpar, ri=ri, riout=riout,
                                   rot=rot, kmi=kmi):
                                def fn(eng):
                                    return eng.matmul(
                                        px[rot][dpar * 64:(dpar + 1) * 64,
                                                (kmi * 2 + riout) * 64:(kmi * 2 + riout) * 64 + 64],
                                        w2s[dpar * 64:(dpar + 1) * 64,
                                            km * 256 + ri * 128 + riout * 64:
                                            km * 256 + ri * 128 + riout * 64 + 64],
                                        absb[dpar * 64:(dpar + 1) * 64,
                                             (ri * 64 + km) * 64:(ri * 64 + km) * 64 + 64],
                                        start=(ri == 0), stop=(ri == 1),
                                        tile_position=(dpar * 64, dpar * 64))
                                return fn
                            w = []
                            if kmi == 0 and ri == 0 and riout == 0 and dpar == 0:
                                if kmq == 0:
                                    w = ["%s_ae_%d" % (name, g) for g in (14, 15)] + ["ld_w2"]
                                if kmq >= 2:
                                    w += ["%s_ze_%d" % (name, kmq - 2)]
                            emit("pe", mk(), w,
                                 "%s_l2_%d" % (name, kmq)
                                 if (kmi == 1 and riout == 1 and ri == 1 and dpar == 1) else None)
            def mkz(kmq=kmq, rot=rot, fs=fscale, kmi=None):
                def fn(eng):
                    km0 = kmq * 2
                    if kmi is None:
                        dst = bass.AP(wsb, km0,
                                      [[8192, 128], [1, 2], [64, 2], [128, 64]])
                        src = bass.AP(px[rot], 0,
                                      [[512, 128], [128, 2], [64, 2], [1, 64]])
                        if fs:
                            return eng.activation(dst, src, AF.Identity,
                                                  scale=1.0 / 4096.0)
                        return eng.activation(dst, src, AF.Identity)
                    dst = bass.AP(wsb, km0 + kmi, [[8192, 128], [64, 2], [128, 64]])
                    src = bass.AP(px[rot], kmi * 128, [[512, 128], [64, 2], [1, 64]])
                    if fs:
                        return eng.tensor_scalar_mul(dst, src, 1.0 / 4096.0)
                    return eng.tensor_copy(dst, src)
                return fn
            if kmq % 2 == 0 or "z" not in EVD:
                emit("act", mkz(), ["%s_l2_%d" % (name, kmq)], "%s_ze_%d" % (name, kmq))
            else:
                emit("dve", mkz(kmi=0), ["%s_l2_%d" % (name, kmq)], None)
                emit("dve", mkz(kmi=1), [], "%s_ze_%d" % (name, kmq))
        EV["%s_zdone" % name] = EV["%s_ze_30" % name]
        EV["%s_zdoned" % name] = EV["%s_ze_31" % name]

    def emit_filter(name, c, s, first_waits):
        def zre(e):
            return bass.AP(wsb, e * 1024, [[8192, 128], [128, 8], [1, 64]])
        def zim(e):
            return bass.AP(wsb, e * 1024 + 64, [[8192, 128], [128, 8], [1, 64]])
        def fre(fb):
            return bass.AP(fsb[fb], 0, [[1024, 128], [128, 8], [1, 64]])
        def fim(fb):
            return bass.AP(fsb[fb], 64, [[1024, 128], [128, 8], [1, 64]])
        t1 = bass.AP(r128, 0, [[4096, 128], [1, 512]])
        t2 = bass.AP(r128, 512, [[4096, 128], [1, 512]])
        for e in range(8):
            fb = e % 2
            w = []
            if e < 2:
                w = list(first_waits)
                if not (c == 0 and s == 0):
                    pcs = (c, s - 1) if s > 0 else (c - 1, 2)
                    w.append("m%d_%d_fi_%d" % (pcs[0], pcs[1], 6 + e))
            else:
                w.append("%s_fi_%d" % (name, e - 2))
            def mkd(c=c, s=s, e=e, fb=fb):
                def fn(eng):
                    rows = f_full[(c * 3 + s) * 128:(c * 3 + s) * 128 + 128,
                                  e * 1024:(e + 1) * 1024]
                    return eng.dma_start(fsb[fb][:, :], rows)
                return fn
            emit("sp", mkd(), w, "%s_fl_%d" % (name, e), n=16, fam="fb%d" % fb)
            w = ["%s_zdone" % name, "%s_zdoned" % name, "%s_fl_%d" % (name, e)]
            emit("dve", (lambda fb=fb, e=e, t1=t1: (lambda eng: eng.tensor_mul(
                t1, zim(e), fim(fb))))(), w, None, serial=True)
            emit("dve", (lambda fb=fb, e=e, t2=t2: (lambda eng: eng.tensor_mul(
                t2, zim(e), fre(fb))))(), [], None)
            emit("dve", (lambda fb=fb, e=e: (lambda eng: eng.tensor_mul(
                zim(e), zre(e), fim(fb))))(), [], None, serial=True)
            emit("dve", (lambda e=e, t2=t2: (lambda eng: eng.tensor_add(
                zim(e), zim(e), t2)))(), [], None, serial=True)
            emit("dve", (lambda fb=fb, e=e: (lambda eng: eng.tensor_mul(
                zre(e), zre(e), fre(fb))))(), [], None, serial=True)
            emit("dve", (lambda e=e, t1=t1: (lambda eng: eng.tensor_sub(
                zre(e), zre(e), t1)))(), [], "%s_fi_%d" % (name, e), serial=True)
        EV["%s_filt" % name] = ev_now("dve")

    def emit_inv(name, ytile_next, gtile, gate_ready=None):
        part_of = {(0, 0): 0, (0, 1): 64, (1, 0): 128, (1, 1): 0}
        for dd in range(128):
            rot = (dd // 8) % 2
            blk = (dd // 2) % 4
            dpar = dd % 2
            tq = dd // 2
            for riout in range(2):
                for riin in range(2):
                    def mk(dd=dd, rot=rot, blk=blk, dpar=dpar, tq=tq,
                           riin=riin, riout=riout):
                        def fn(eng):
                            pt = pa[rot] if dpar == 0 else pyp[rot]
                            return eng.matmul(
                                pt[riout * 64:riout * 64 + 64,
                                   blk * 64:blk * 64 + 64],
                                bass.AP(wsb, tq * 128 + riin * 64 + dpar * 64 * 8192,
                                        [[8192, 64], [1, 64]]),
                                wi1s[dpar * 64:(dpar + 1) * 64,
                                     part_of[(riin, riout)]:part_of[(riin, riout)] + 64],
                                start=(riin == 0), stop=(riin == 1),
                                tile_position=(dpar * 64, riout * 64))
                        return fn
                    w = []
                    if dd % 8 == 0 and riout == 0 and riin == 0:
                        w = ["%s_fi_%d" % (name, dd // 16), "ld_wi1"]
                        if dd >= 16:
                            w.append("%s_be_%d" % (name, dd // 8 - 2))
                            w.append("%s_b0_%d" % (name, dd // 8 - 2))
                    emit("pe", mk(), w,
                         "%s_i1_%d" % (name, dd // 8)
                         if (dd % 8 == 7 and riout == 1 and riin == 1) else None)
            if dd % 8 == 7:
                g = dd // 8
                def mke0(g=g, rot=(g % 2)):
                    def fn(eng):
                        dst = bass.AP(absb, g * 8, [[8192, 128], [2, 4], [128, 64]])
                        src = bass.AP(pa[rot], 0, [[512, 128], [64, 4], [1, 64]])
                        return eng.activation(dst, src, AF.Identity)
                    return fn
                def mke1(g=g, rot=(g % 2)):
                    def fn(eng):
                        dst = bass.AP(absb, g * 8 + 1, [[8192, 128], [2, 4], [128, 64]])
                        src = bass.AP(pyp[rot], 0, [[256, 128], [64, 4], [1, 64]])
                        if "i" not in EVD:
                            return eng.activation(dst, src, AF.Identity)
                        return eng.tensor_copy(dst, src)
                    return fn
                emit("act", mke0(), ["%s_i1_%d" % (name, g)], "%s_b0_%d" % (name, g))
                emit("dve" if "i" in EVD else "act", mke1(),
                     ["%s_i1_%d" % (name, g)], "%s_be_%d" % (name, g))
        for tpq in range(16):
            rot = tpq % 2
            for tpi in range(2):
                tp = tpq * 2 + tpi
                for member in range(2):
                    tum = tp + 32 * member
                    def mk(tpi=tpi, member=member, tum=tum, rot=rot):
                        def fn(eng):
                            return eng.matmul(
                                pyp[rot][member * 64:member * 64 + 64,
                                         tpi * 128:tpi * 128 + 128],
                                wi2s[:, tum * 64:tum * 64 + 64],
                                bass.AP(absb, tum * 128, [[8192, 128], [1, 128]]),
                                start=True, stop=True,
                                tile_position=(0, member * 64))
                        return fn
                    w = []
                    if tpi == 0 and member == 0:
                        if tpq == 0:
                            w = (["%s_be_%d" % (name, g) for g in (14, 15)]
                                 + ["%s_b0_%d" % (name, g) for g in (14, 15)]
                                 + ["ld_wi2"])
                        if tpq >= 2:
                            w.append("%s_ye_%d" % (name, tpq - 2))
                    emit("pe", mk(), w,
                         "%s_i2_%d" % (name, tpq)
                         if (tpi == 1 and member == 1) else None)
            def mkg(tpq=tpq, rot=rot, yt=ytile_next, gt=gtile):
                def fn(eng):
                    tp0 = tpq * 2
                    src = bass.AP(pyp[rot], 0, [[256, 128], [128, 2], [1, 128]])
                    dstv = bass.AP(yt, tp0, [[4096, 128], [1, 2], [32, 128]])
                    gv = bass.AP(gt, tp0, [[4096, 128], [1, 2], [32, 128]])
                    return eng.tensor_mul(dstv, src, gv)
                return fn
            gw = ["%s_i2_%d" % (name, tpq)]
            if tpq == 0 and gate_ready:
                gw += list(gate_ready)
            emit("dve", mkg(), gw, "%s_ye_%d" % (name, tpq))
        EV["%s_done" % name] = ev_now("dve")

    proj_last_mm = {}
    proj_last_ev = {}

    def emit_proj(c, sp, dst0, dst1, mm_waits, evac_waits):
        wbuf = (c * 2 + sp) % 2
        for kc in range(NKC):
            def mkd(c=c, sp=sp, kc=kc, wbuf=wbuf):
                def fn(eng):
                    cols = bass.AP(wp.tensor, (kc * 128) * 4096 + (sp * 2) * 1024 + c * 128,
                                   [[4096, 128], [1024, 2], [1, 128]])
                    dst = bass.AP(wps[wbuf], kc * 256, [[2048, 128], [128, 2], [1, 128]])
                    return eng.dma_start(dst, cols)
                return fn
            w = []
            if kc == 0:
                prev = proj_last_mm.get((c * 2 + sp) - 2)
                if prev:
                    w.append(prev)
            emit("pool", mkd(), w, "wpl_%d_%d" % (c, sp) if kc == NKC - 1 else None,
                 n=16, fam="wps%d" % wbuf)
        for q in range(16):
            rot = q % 2
            for si in range(2):
                u32 = q * 2 + si
                for kc in range(NKC):
                    def mk(u32=u32, kc=kc, si=si, rot=rot, wbuf=wbuf):
                        def fn(eng):
                            return eng.matmul(
                                pp[rot][:, si * 256:si * 256 + 256],
                                bass.AP(xt, kc * 4096 + u32 * 128,
                                        [[NKC * 4096, 128], [1, 128]]),
                                bass.AP(wps[wbuf], kc * 256, [[2048, 128], [1, 256]]),
                                start=(kc == 0), stop=False)
                        return fn
                    w = []
                    if kc == 0 and si == 0:
                        if q == 0:
                            w = list(mm_waits) + ["wpl_%d_%d" % (c, sp), "xt_all", "ld_bp"]
                            prev = proj_last_ev.get((c * 2 + sp) - 1)
                            if prev:
                                w += list(prev)
                            else:
                                w.append("fpe_63")
                        if q >= 2:
                            w.append("pq_%d_%d_%d_0" % (c, sp, q - 2))
                            w.append("pq_%d_%d_%d_1" % (c, sp, q - 2))
                    emit("pe", mk(), w, None)
                def mkb(si=si, rot=rot, c=c, sp=sp):
                    def fn(eng):
                        bcols = bass.AP(m128, 64 * 4096 + (sp * 2) * 1024 + c * 128,
                                        [[4096, 1], [1024, 2], [1, 128]])
                        return eng.matmul(pp[rot][:, si * 256:si * 256 + 256],
                                          onesb[64:65, 0:128], bcols,
                                          start=False, stop=True)
                    return fn
                emit("pe", mkb(), [], "pj_%d_%d_%d" % (c, sp, q) if si == 1 else None)
            for s_i, dstt in ((0, dst0), (1, dst1)):
                def mke(q=q, s_i=s_i, dstt=dstt, rot=rot, h=None):
                    def fn(eng):
                        if h is None:
                            dst = bass.AP(dstt, q * 2, [[4096, 128], [1, 2], [32, 128]])
                            src = bass.AP(pp[rot], s_i * 128,
                                          [[512, 128], [256, 2], [1, 128]])
                            return eng.activation(dst, src, AF.Identity)
                        dst = bass.AP(dstt, q * 2 + h, [[4096, 128], [32, 128]])
                        src = bass.AP(pp[rot], s_i * 128 + h * 256,
                                      [[512, 128], [1, 128]])
                        return eng.tensor_copy(dst, src)
                    return fn
                w = ["pj_%d_%d_%d" % (c, sp, q)]
                if q == 0:
                    w += list(evac_waits)
                if s_i == 0 or "p" not in EVD:
                    emit("act", mke(), w, "pq_%d_%d_%d_%d" % (c, sp, q, s_i))
                else:
                    emit("dve", mke(h=0), w, None)
                    emit("dve", mke(h=1), [], "pq_%d_%d_%d_%d" % (c, sp, q, s_i))
        proj_last_mm[c * 2 + sp] = "pj_%d_%d_15" % (c, sp)
        proj_last_ev[c * 2 + sp] = ("pq_%d_%d_15_0" % (c, sp),
                                    "pq_%d_%d_15_1" % (c, sp))
        EV["proj_%d_%d" % (c, sp)] = ev_now("act")
        EV["projd_%d_%d" % (c, sp)] = ev_now("dve")

    for s in range(3):
        l1w = ["fpe_63"]
        if s > 0:
            l1w.append("fsl_%d" % (s - 1))
        emit_fwd("f%d" % s, tile[s], l1w, True)
        dma("sp", f_slice[s * 128:(s + 1) * 128, :], wsb.ap(),
            ["f%d_zdone" % s, "f%d_zdoned" % s], "fsl_%d" % s, fam="fsl")
    emit("pool", lambda eng: eng.collective_compute(
        "AllGather", ALU.bypass, replica_groups=[list(range(NCORE))],
        ins=[f_slice], outs=[f_full]),
        ["fsl_%d" % s for s in range(3)], "cc_done", n=16)
    if "fsl" in dbg_t:
        dma("pool", dbg_t["fsl"], f_slice, ["cc_done"], "d_fsl", fam="dbgfsl")

    for c in range(NC):
        evw = ["f2_l1_15"] if c == 0 else ["y3s_%d" % (c - 1)]
        emit_proj(c, 0, tile[0], tile[1], [], evw)
        ytiles = [tile[0], tile[0], tile[1], tile[0]]
        gtiles = [None, tile[1], tile[2], tile[3]]
        for s in range(3):
            nm = "m%d_%d" % (c, s)
            if s == 0:
                l1w = ["proj_%d_0" % c, "projd_%d_0" % c]
                if c == 0:
                    l1w.append("f2_zdone")
                    l1w.append("f2_zdoned")
                else:
                    l1w.append("m%d_2_done" % (c - 1))
                    l1w.append("m%d_2_b0_15" % (c - 1))
            else:
                l1w = ["m%d_%d_done" % (c, s - 1), "m%d_%d_b0_15" % (c, s - 1)]
            emit_fwd(nm, ytiles[s], l1w, False)
            fw0 = ["cc_done", "xt_all"] if (c == 0 and s == 0) else []
            emit_filter(nm, c, s, fw0)
            emit_inv(nm, ytiles[s + 1], gtiles[s + 1],
                     gate_ready=[("proj_%d_0" % c, "projd_%d_0" % c),
                                 ("proj_%d_1" % c, "projd_%d_1" % c)][0 if s == 0 else 1])
            if s == 0:
                ew = ["bc_1_7"] if c == 0 else [
                    "m%d_1_ye_15" % (c - 1), "m%d_2_ye_15" % (c - 1)]
                emit_proj(c, 1, tile[2], tile[3], ["m%d_0_ae_15" % c], ew)
        def mky3(c=c):
            def fn(eng):
                return eng.dma_start(y3_dram[c * 128:(c + 1) * 128, :], tile[0].ap())
            return fn
        emit("sp", mky3(), ["m%d_2_done" % c], "y3s_%d" % c, n=16, fam="y3s")
    if "y3" in dbg_t:
        dma("pool", dbg_t["y3"], tile[0].ap(), ["y3s_%d" % (NC - 1)], "d_y3", fam="dbgy3")

    FDN = "m%d_2_done" % (NC - 1)
    emit("dve", lambda eng: eng.memset(st12[:, :], 0), [FDN], "stz")
    for c in range(NC):
        w = ["proj_%d_1" % (NC - 1), "projd_%d_1" % (NC - 1)]
        if c == 0:
            w.append("y3s_%d" % (NC - 1))
        w.append(FDN if c == 0 else "stp_%d" % (c - 1))
        dma("sp", xv[0][:, :], x_til[c * 128:(c + 1) * 128, :], w, "ltx_%d" % c,
            fam="xa")
        def mkyl(c=c):
            def fn(eng):
                return eng.dma_start(bass.AP(wsb, 0, [[8192, 128], [1, 4096]]),
                                     y3_dram[c * 128:(c + 1) * 128, :])
            return fn
        wy = ["m%d_2_i1_15" % (NC - 1), "m%d_2_fi_7" % (NC - 1),
              "y3s_%d" % (NC - 1)] if c == 0 else ["x1_%d" % (c - 1)]
        emit("sp", mkyl(), wy, "lty_%d" % c, n=16, fam="y3l")
        def mkx1(c=c):
            def fn(eng):
                dst = bass.AP(xt, c * 4096, [[NKC * 4096, 128], [1, 4096]])
                return eng.tensor_add(dst, xv[0][:, :],
                                      bass.AP(wsb, 0, [[8192, 128], [1, 4096]]))
            return fn
        wx1 = ["ltx_%d" % c, "lty_%d" % c, "stz"]
        if c == 0 and "xtilde" in dbg_t:
            wx1.append("d_xt")
        emit("dve", mkx1(), wx1, "x1_%d" % c)
        emit("dve", (lambda c=c: (lambda eng: eng.tensor_reduce(
            m128[:, 0:32], bass.AP(xt, c * 4096, [[NKC * 4096, 128], [1, 32], [32, 128]]),
            AX.X, ALU.add)))(), [], None, serial=True)
        emit("dve", lambda eng: eng.tensor_add(st12[:, 0:32], st12[:, 0:32],
                                               m128[:, 0:32]), [], None, serial=True)
        emit("act", (lambda c=c: (lambda eng: eng.activation(
            r128[:, :], bass.AP(xt, c * 4096, [[NKC * 4096, 128], [1, 4096]]),
            AF.Square)))(), ["x1_%d" % c, "m%d_2_fi_7" % (NC - 1)], "sq2_%d" % c)
        emit("dve", (lambda: (lambda eng: eng.tensor_reduce(
            m128[:, 32:64], bass.AP(r128, 0, [[4096, 128], [1, 32], [32, 128]]),
            AX.X, ALU.add)))(), ["sq2_%d" % c], None, serial=True)
        emit("dve", (lambda: (lambda eng: eng.tensor_add(
            st12[:, 32:64], st12[:, 32:64], m128[:, 32:64])))(), [], "stp_%d" % c,
            serial=True)
    if "x1c0" in dbg_t:
        dma("pool", dbg_t["x1c0"], bass.AP(xt, 0, [[NKC * 4096, 128], [1, 4096]]),
            ["stp_%d" % (NC - 1)], "d_x1", fam="dbgx1")
    emit("dve", lambda eng: eng.tensor_scalar_mul(st12[:, 64:96], st12[:, 0:32],
                                                  -1.0 / 1024.0),
         ["stp_%d" % (NC - 1)], None, serial=True)
    emit("dve", lambda eng: eng.tensor_scalar_mul(st12[:, 96:128], st12[:, 32:64],
                                                  1.0 / 1024.0), [], None, serial=True)
    emit("dve", lambda eng: eng.tensor_mul(st12[:, 0:32], st12[:, 64:96],
                                           st12[:, 64:96]), [], None, serial=True)
    emit("dve", lambda eng: eng.tensor_sub(st12[:, 96:128], st12[:, 96:128],
                                           st12[:, 0:32]), [], "var2", serial=True)
    emit("act", lambda eng: eng.activation(st12[:, 96:128], st12[:, 96:128],
                                           AF.Sqrt, bias=epsc[:, 0:1]), ["var2"], "sd2")
    emit("dve", lambda eng: eng.reciprocal(st12[:, 96:128], st12[:, 96:128]),
         ["sd2"], "mr2")

    for c in range(NC):
        for u32 in range(32):
            def mkts(c=c, u32=u32):
                def fn(eng):
                    src = bass.AP(xt, c * 4096 + u32, [[NKC * 4096, 128], [32, 128]])
                    dst = bass.AP(wsb, 4096 + u32, [[8192, 128], [32, 128]])
                    return eng.tensor_scalar(dst, src,
                                             st12[:, 64 + u32:65 + u32],
                                             st12[:, 96 + u32:97 + u32],
                                             ALU.add, ALU.mult)
                return fn
            w = []
            if u32 == 0:
                w = ["mr2"] if c == 0 else ["grd_%d" % (c - 1)]
                if c == 0:
                    w.append("lty_%d" % (NC - 1))
            emit("dve", mkts(), w, None)
        emit("dve", lambda eng: eng.tensor_reduce(
            m128[:, 0:256],
            bass.AP(wsb, 4096, [[8192, 128], [32, 128], [16, 2], [1, 16]]),
            AX.X, ALU.add), [], None, serial=True)
        emit("dve", lambda eng: eng.tensor_scalar_mul(
            m128[:, 0:256], m128[:, 0:256], 1.0 / 16.0), [], "grd_%d" % c,
            serial=True)
        ltw = ["ltx_%d" % (NC - 1), "x1_%d" % (NC - 1)] if c == 0 \
            else ["gt_%d_0" % (c - 1), "gt_%d_1" % (c - 1)]
        emit("dve", (lambda c=c: (lambda eng: eng.tensor_copy(
            xv[0][:, 0:256], m128[:, 0:256])))(), ltw, "gcp_%d" % c, serial=True)
        for gh in range(2):
            def mkt(c=c, gh=gh):
                def fn(eng):
                    return eng.transpose(pp[gh][:, 0:128], xv[0][:, gh:256:2],
                                         idents[:, :])
                return fn
            w = ["gcp_%d" % c, "ld_id"]
            if c >= 1:
                w.append("ge_%d_%d" % (c - 1, gh))
            else:
                w.append("proj_%d_1" % (NC - 1))
                w.append("projd_%d_1" % (NC - 1))
            emit("pe", mkt(), w, "gt_%d_%d" % (c, gh))
            def mkte(c=c, gh=gh):
                def fn(eng):
                    dst = bass.AP(w2s, c * 256 + gh * 64,
                                  [[16384, 128], [32, 2], [128, 2], [1, 32]])
                    src = bass.AP(pp[gh], 0, [[512, 128], [1, 128]])
                    return eng.activation(dst, src, AF.Identity)
                return fn
            emit("act", mkte(), ["gt_%d_%d" % (c, gh)], "ge_%d_%d" % (c, gh))
    EV["groups_done"] = EV["ge_%d_1" % (NC - 1)]
    if "groups" in dbg_t:
        dma("pool", dbg_t["groups"], bass.AP(w2s, 0, [[16384, 128], [1, 2048]]),
            ["groups_done"], "d_gr", fam="dbggr")

    wstream_prev = ["groups_done"]

    def stream_w_mms(tag, w_dram, psum_of, evac_name, rhs_fn, emit_evacs):
        for half in range(2):
            for kc in range(NKC):
                wbuf = kc % 2
                if kc >= 2:
                    wld = ["%s_mm_%d_%d" % (tag, half, kc - 2)]
                elif half > 0:
                    wld = ["%s_mm_%d_%d" % (tag, half - 1, NKC - 2 + kc)]
                else:
                    wld = list(wstream_prev)
                def mkd(kc=kc, half=half, wbuf=wbuf, w_dram=w_dram):
                    def fn(eng):
                        return eng.dma_start(
                            bass.AP(wps[wbuf], 0, [[2048, 128], [1, 512]]),
                            w_dram[kc * 128:(kc + 1) * 128, half * 512:(half + 1) * 512])
                    return fn
                emit("pool", mkd(), wld, "%s_ld_%d_%d" % (tag, half, kc), n=16,
                     fam="wps%d" % wbuf)
                for dc4 in range(4):
                    def mkm(kc=kc, half=half, dc4=dc4, wbuf=wbuf):
                        def fn(eng):
                            return eng.matmul(
                                psum_of(half, dc4),
                                bass.AP(wps[wbuf], dc4 * 128, [[2048, 128], [1, 128]]),
                                rhs_fn(kc),
                                start=(kc == 0), stop=(kc == NKC - 1))
                        return fn
                    w = []
                    if dc4 == 0:
                        w = ["%s_ld_%d_%d" % (tag, half, kc)]
                        if kc == 0 and half > 0:
                            w += [evac_name(half - 1, j) for j in range(4)]
                    emit("pe", mkm(), w,
                         "%s_mm_%d_%d" % (tag, half, kc) if dc4 == 3 else None)
            emit_evacs(half)
        wstream_prev[:] = [evac_name(1, j) for j in range(4)]

    PS4 = [pa[0], pa[1], px[0], px[1]]
    qkoff = {0: 2048, 1: 4096}
    for wi, wdr in ((0, wq_d), (1, wk_d)):
        def psof(half, dc4):
            return PS4[dc4][:, 0:256]
        evn = (lambda wi: (lambda half, dc4: "qk%d_ev_%d_%d" % (wi, half, dc4)))(wi)
        def emit_evacs(half, wi=wi):
            for dc4 in range(4):
                dc = half * 4 + dc4
                def mke(wi=wi, dc=dc, dc4=dc4):
                    def fn(eng):
                        sc = 0.125 if wi == 0 else 1.0
                        dst = bass.AP(w2s, qkoff[wi] + dc * 256, [[16384, 128], [1, 256]])
                        return eng.activation(
                            dst, PS4[dc4][:, 0:256],
                            AF.Identity, scale=sc)
                    return fn
                emit("act", mke(), ["qk%d_mm_%d_%d" % (wi, half, NKC - 1)],
                     "qk%d_ev_%d_%d" % (wi, half, dc4))
        stream_w_mms("qk%d" % wi, wdr, psof, evn,
                     lambda kc: bass.AP(w2s, kc * 256, [[16384, 128], [1, 256]]),
                     emit_evacs)
    for sb_ in range(2):
        for half in range(2):
            for kc in range(NKC):
                wbuf = kc % 2
                if kc >= 2:
                    wld = ["v_mm_%d_%d_%d" % (sb_, half, kc - 2)]
                elif (sb_, half) != (0, 0):
                    ph = (sb_, half - 1) if half > 0 else (sb_ - 1, 1)
                    wld = ["v_mm_%d_%d_%d" % (ph[0], ph[1], NKC - 2 + kc)]
                else:
                    wld = ["qk1_mm_1_%d" % (NKC - 2 + kc)]
                def mkd(kc=kc, half=half, wbuf=wbuf):
                    def fn(eng):
                        return eng.dma_start(
                            bass.AP(wps[wbuf], 0, [[2048, 128], [1, 512]]),
                            wv_d[kc * 128:(kc + 1) * 128, half * 512:(half + 1) * 512])
                    return fn
                emit("pool", mkd(), wld, "v_ld_%d_%d_%d" % (sb_, half, kc), n=16,
                     fam="wps%d" % wbuf)
                def mkm(kc=kc, sb_=sb_, half=half, wbuf=wbuf):
                    def fn(eng):
                        return eng.matmul(
                            px[half][:, 0:512],
                            bass.AP(w2s, kc * 256 + sb_ * 128, [[16384, 128], [1, 128]]),
                            bass.AP(wps[wbuf], 0, [[2048, 128], [1, 512]]),
                            start=(kc == 0), stop=(kc == NKC - 1))
                    return fn
                vw = ["v_ld_%d_%d_%d" % (sb_, half, kc)]
                if kc == 0 and sb_ > 0:
                    vw.append("v_ev_%d_%d" % (sb_ - 1, half))
                if kc == 0 and (sb_, half) == (0, 0):
                    vw += ["qk1_ev_1_%d" % j for j in range(4)]
                emit("pe", mkm(), vw, "v_mm_%d_%d_%d" % (sb_, half, kc))
            def mkev(sb_=sb_, half=half):
                def fn(eng):
                    dst = bass.AP(w2s, 6144 + sb_ * 1024 + half * 512,
                                  [[16384, 128], [1, 512]])
                    return eng.activation(dst, px[half][:, 0:512], AF.Identity)
                return fn
            emit("act", mkev(), ["v_mm_%d_%d_%d" % (sb_, half, NKC - 1)],
                 "v_ev_%d_%d" % (sb_, half))
    for kc in range(NKC):
        wbuf = kc % 2
        wld = ["v_mm_1_1_%d" % (NKC - 2 + kc)] if kc < 2 else ["h_mm_%d" % (kc - 2)]
        def mkd(kc=kc, wbuf=wbuf):
            def fn(eng):
                return eng.dma_start(bass.AP(wps[wbuf], 0, [[2048, 128], [1, 256]]),
                                     ws1_d[kc * 128:(kc + 1) * 128, :])
            return fn
        emit("pool", mkd(), wld, "h_ld_%d" % kc, n=16, fam="wps%d" % wbuf)
        for hf in range(2):
            def mkm(kc=kc, hf=hf, wbuf=wbuf):
                def fn(eng):
                    return eng.matmul(
                        pyp[hf][:, 0:256],
                        bass.AP(wps[wbuf], hf * 128, [[2048, 128], [1, 128]]),
                        bass.AP(w2s, kc * 256, [[16384, 128], [1, 256]]),
                        start=(kc == 0), stop=False)
                return fn
            w = ["h_ld_%d" % kc]
            if kc == 0 and hf == 0:
                w += ["m%d_2_ye_15" % (NC - 1)]
            emit("pe", mkm(), w if hf == 0 else [],
                 "h_mm_%d" % kc if hf == 1 else None)
    dma("pool", rowf[0:1, 0:256], bs1_d, ["fpe_63"], "ld_bs1", fam="bs1")
    dma("pool", rowf[0:1, 256:257], bs2_d, [], "ld_bs2", fam="bs2")
    for hf in range(2):
        def mkhb(hf=hf):
            def fn(eng):
                return eng.matmul(pyp[hf][:, 0:256],
                                  rowf[0:1, hf * 128:hf * 128 + 128],
                                  onesb[0:1, 0:256], start=False, stop=True)
            return fn
        emit("pe", mkhb(), ["ld_bs1", "h_mm_%d" % (NKC - 1)] if hf == 0 else [],
             "hb_%d" % hf)
        def mkhe(hf=hf):
            def fn(eng):
                return eng.activation(bass.AP(wi2s, hf * 256, [[4096, 128], [1, 256]]),
                                      pyp[hf][:, 0:256], AF.Relu)
            return fn
        emit("act", mkhe(), ["hb_%d" % hf], "hev_%d" % hf)
    for kc in range(2):
        def mks(kc=kc):
            def fn(eng):
                return eng.matmul(pp[0][0:1, 0:256], ws2s[:, kc:kc + 1],
                                  bass.AP(wi2s, kc * 256, [[4096, 128], [1, 256]]),
                                  start=(kc == 0), stop=(kc == 1))
            return fn
        emit("pe", mks(), ["hev_0", "hev_1", "ld_ws2"] if kc == 0 else [],
             "smm" if kc == 1 else None)
    scrow = lambda a, n: xv[0][0:1, 1024 + a:1024 + a + n]
    emit("act", lambda eng: eng.activation(scrow(0, 256), pp[0][0:1, 0:256],
                                           AF.Identity, bias=rowf[0:1, 256:257]),
         ["smm", "ld_bs2", "gcp_%d" % (NC - 1)], "sc_done")
    if "scores" in dbg_t:
        dma("pool", dbg_t["scores"], scrow(0, 256), ["sc_done"], "d_sc", fam="dbgsc")

    XA = lambda a, n=1: xv[0][:, 2048 + a:2048 + a + n]
    for b in range(2):
        def mksc(b=b):
            def fn(eng):
                return eng.matmul(pp[1][:, 0:1], scrow(b * 128, 128),
                                  ones32[0:1, 0:1], start=True, stop=True)
            return fn
        emit("pe", mksc(), ["sc_done"], "scol_%d" % b)
        emit("act", (lambda b=b: (lambda eng: eng.activation(
            XA(8 + b), pp[1][:, 0:1], AF.Identity)))(), ["scol_%d" % b], "scv_%d" % b)
        def mksr(b=b):
            def fn(eng):
                return eng.matmul(pp[1][:, 128:256], ones32[0:1, 0:128],
                                  scrow(b * 128, 128), start=True, stop=True)
            return fn
        emit("pe", mksr(), ["scv_%d" % b], "srow_%d" % b)
        emit("act", (lambda b=b: (lambda eng: eng.activation(
            XA(256, 128), pp[1][:, 128:256], AF.Identity)))(),
            ["srow_%d" % b], "srv_%d" % b)
        emit("dve", (lambda b=b: (lambda eng: eng.tensor_tensor(
            XA(384, 128), XA(256, 128),
            XA(8 + b, 1).broadcast_to([128, 128]), ALU.is_gt)))(),
            ["srv_%d" % b], None)
        emit("dve", (lambda b=b: (lambda eng: eng.tensor_reduce(
            XA(10 + b), XA(384, 128), AX.X, ALU.add)))(), [], None, serial=True)
        emit("dve", (lambda b=b: (lambda eng: eng.tensor_single_scalar(
            XA(0 + b), XA(10 + b), 63.5, ALU.is_lt)))(), [], "msk_%d" % b, serial=True)
        emit("dve", (lambda b=b: (lambda eng: eng.tensor_scalar(
            XA(2 + b), XA(0 + b), -1.0, 1e9, ALU.add, ALU.mult)))(), [], "pen_%d" % b,
            serial=True)
        def mkpt(b=b):
            def fn(eng):
                return eng.transpose(pp[1][0:1, 256:384], XA(2 + b), idents[:, :])
            return fn
        emit("pe", mkpt(), ["pen_%d" % b], "pt_%d" % b)
        emit("act", (lambda b=b: (lambda eng: eng.activation(
            scrow(256 + b * 128, 128), pp[1][0:1, 256:384], AF.Identity)))(),
            ["pt_%d" % b], "penr_%d" % b)
    if "mask" in dbg_t:
        dma("pool", dbg_t["mask"], xv[0][:, 2048:2050], ["msk_0", "msk_1"], "d_mk", fam="dbgmk")

    for b in range(2):
        for h in range(NH):
            hc, hb = h // 2, (h % 2) * 64
            nm = "at_%d_%d" % (b, h)
            qcol = 2048 + hc * 256 + b * 128
            kcol = 4096 + hc * 256 + b * 128
            if b == 0 and h == 0:
                w = ["v_ev_1_1", "penr_0", "penr_1", "qk0_ev_1_3", "qk1_ev_1_3"]
            else:
                pb, ph_ = (b, h - 1) if h > 0 else (b - 1, NH - 1)
                w = ["at_%d_%d_ctx" % (pb, ph_)]
            def mks(hb=hb, qcol=qcol, kcol=kcol):
                def fn(eng):
                    return eng.matmul(
                        px[0][:, 0:128],
                        bass.AP(w2s, hb * 16384 + qcol, [[16384, 64], [1, 128]]),
                        bass.AP(w2s, hb * 16384 + kcol, [[16384, 64], [1, 128]]),
                        start=True, stop=False, tile_position=(hb, 0))
                return fn
            emit("pe", mks(), w, None)
            def mkp(b=b):
                def fn(eng):
                    return eng.matmul(px[0][:, 0:128], ones32[0:1, 0:128],
                                      scrow(256 + b * 128, 128), start=False, stop=True)
                return fn
            emit("pe", mkp(), [], "%s_s" % nm)
            emit("dve", (lambda nm=nm: (lambda eng: eng.tensor_reduce(
                XA(4), px[0][:, 0:128], AX.X, ALU.max, negate=True)))(),
                ["%s_s" % nm], "%s_nm" % nm)
            def mknt(nm=nm):
                def fn(eng):
                    return eng.transpose(pp[1][0:1, 384:512], XA(4), idents[:, :])
                return fn
            emit("pe", mknt(), ["%s_nm" % nm], "%s_nt" % nm)
            emit("act", (lambda nm=nm: (lambda eng: eng.activation(
                scrow(640, 128), pp[1][0:1, 384:512], AF.Identity)))(),
                ["%s_nt" % nm], "%s_nr" % nm)
            def mkst(hb=hb, qcol=qcol, kcol=kcol):
                def fn(eng):
                    return eng.matmul(
                        px[1][:, 0:128],
                        bass.AP(w2s, hb * 16384 + kcol, [[16384, 64], [1, 128]]),
                        bass.AP(w2s, hb * 16384 + qcol, [[16384, 64], [1, 128]]),
                        start=True, stop=False, tile_position=(hb, 0))
                return fn
            emit("pe", mkst(), ["%s_nr" % nm], None)
            def mkst2(nm=nm):
                def fn(eng):
                    return eng.matmul(px[1][:, 0:128], ones32[0:1, 0:128],
                                      scrow(640, 128), start=False, stop=True)
                return fn
            emit("pe", mkst2(), [], "%s_st" % nm)
            def mkexp(nm=nm, b=b):
                def fn(eng):
                    return eng.activation(
                        bass.AP(wi2s, 512, [[4096, 128], [1, 128]]),
                        px[1][:, 0:128], AF.Exp, bias=XA(2 + b))
                return fn
            emit("act", mkexp(), ["%s_st" % nm], "%s_e" % nm)
            def mksum(nm=nm):
                def fn(eng):
                    return eng.matmul(pp[1][:, 0:1],
                                      bass.AP(wi2s, 512, [[4096, 128], [1, 128]]),
                                      onesb[:, 0:1], start=True, stop=True)
                return fn
            emit("pe", mksum(), ["%s_e" % nm], "%s_sum" % nm)
            emit("dve", (lambda: (lambda eng: eng.reciprocal(
                XA(5), pp[1][:, 0:1])))(), ["%s_sum" % nm], None)
            emit("dve", (lambda b=b: (lambda eng: eng.tensor_mul(
                XA(6), XA(5), XA(0 + b))))(), [], "%s_rs" % nm, serial=True)
            def mkc(b=b, h=h):
                def fn(eng):
                    return eng.matmul(
                        pp[0][:, 256:320],
                        bass.AP(wi2s, 512, [[4096, 128], [1, 128]]),
                        bass.AP(w2s, 6144 + b * 1024 + h * 64, [[16384, 128], [1, 64]]),
                        start=True, stop=True)
                return fn
            emit("pe", mkc(), ["%s_rs" % nm], "%s_cm" % nm)
            def mkce(b=b, h=h, nm=nm):
                def fn(eng):
                    return eng.activation(
                        xv[1][:, b * 1024 + h * 64:b * 1024 + h * 64 + 64],
                        pp[0][:, 256:320], AF.Identity, scale=XA(6))
                return fn
            cw = ["%s_cm" % nm]
            if b == 0 and h == 0:
                cw.append("grd_%d" % (NC - 1))
            emit("act", mkce(), cw, "%s_ctx" % nm)
    for b in range(2):
        for dc in range(8):
            def mkt(b=b, dc=dc):
                def fn(eng):
                    return eng.transpose(
                        pp[dc % 2][:, 128:256],
                        xv[1][:, b * 1024 + dc * 128:b * 1024 + dc * 128 + 128],
                        idents[:, :])
                return fn
            w = ["at_%d_%d_ctx" % (b, NH - 1)] if dc == 0 else []
            if dc >= 2:
                w.append("ctf_%d_%d" % (b, dc - 2))
            elif b == 1:
                w.append("ctf_0_%d" % (6 + dc))
            emit("pe", mkt(), w, "ctt_%d_%d" % (b, dc))
            def mkte(b=b, dc=dc):
                def fn(eng):
                    dst = bass.AP(w2s, 8192 + dc * 256 + b * 128, [[16384, 128], [1, 128]])
                    return eng.activation(dst, pp[dc % 2][:, 128:256], AF.Identity)
                return fn
            emit("act", mkte(), ["ctt_%d_%d" % (b, dc)], "ctf_%d_%d" % (b, dc))
    EV["ctxfm_done"] = ev_now("act")
    def wo_psum(half, dc4):
        return PS4[dc4][:, 0:256]
    def wo_emit_evacs(half):
        for dc4 in range(4):
            dc = half * 4 + dc4
            def fn_(eng, dc=dc, dc4=dc4):
                dst = bass.AP(w2s, 10240 + dc * 256, [[16384, 128], [1, 256]])
                return eng.activation(dst, wo_psum(0, dc4), AF.Identity)
            emit("act", fn_, ["wo_mm_%d_%d" % (half, NKC - 1)],
                 "wo_ev_%d_%d" % (half, dc4))
    wstream_prev[:] = ["ctxfm_done"]
    stream_w_mms("wo", wo_d, wo_psum, (lambda half, dc4: "wo_ev_%d_%d" % (half, dc4)),
                 lambda kc: bass.AP(w2s, 8192 + kc * 256, [[16384, 128], [1, 256]]),
                 wo_emit_evacs)
    if "outg" in dbg_t:
        dma("pool", dbg_t["outg"], bass.AP(w2s, 10240, [[16384, 128], [1, 2048]]),
            ["wo_ev_1_3"], "d_og", fam="dbgog")
    for c in range(NC):
        for gh in range(2):
            def mkcp(c=c, gh=gh):
                def fn(eng):
                    src = bass.AP(w2s, 10240 + c * 256 + gh * 64,
                                  [[16384, 128], [32, 2], [128, 2], [1, 32]])
                    return eng.tensor_copy(xv[1][:, 2048 + gh * 128:2048 + gh * 128 + 128],
                                           src)
                return fn
            w = ["wo_ev_1_3"] if (c == 0 and gh == 0) else []
            if c >= 1 and gh == 0:
                w = ["og_%d_1" % (c - 1)]
            emit("dve", mkcp(), w, "ocp_%d_%d" % (c, gh))
            def mkt(c=c, gh=gh):
                def fn(eng):
                    return eng.transpose(pp[gh][:, 128:256],
                                         xv[1][:, 2048 + gh * 128:2048 + gh * 128 + 128],
                                         idents[:, :])
                return fn
            w = ["ocp_%d_%d" % (c, gh)]
            if c >= 1:
                w.append("og_%d_%d" % (c - 1, gh))
            emit("pe", mkt(), w, "ogt_%d_%d" % (c, gh))
            def mkte(c=c, gh=gh):
                def fn(eng):
                    dst = bass.AP(w2s, 12288 + c * 256 + gh, [[16384, 128], [2, 128]])
                    return eng.activation(dst, pp[gh][:, 128:256], AF.Identity)
                return fn
            emit("act", mkte(), ["ogt_%d_%d" % (c, gh)], "og_%d_%d" % (c, gh))
    ovw = xv[0].rearrange("p (dd gh u) -> p dd gh u", gh=2, u=16)
    for c in range(NC):
        for u16 in range(16):
            def mko(c=c, u16=u16):
                def fn(eng):
                    x1v = bass.AP(xt, c * 4096 + u16, [[NKC * 4096, 128], [32, 128], [16, 2]])
                    ogv = bass.AP(w2s, 12288 + c * 256, [[16384, 128], [2, 128], [1, 2]])
                    dst = ovw[:, :, :, u16]
                    return eng.tensor_add(dst, x1v, ogv)
                return fn
            w = []
            if u16 == 0:
                w = ["og_%d_0" % c, "og_%d_1" % c]
                if c == 0:
                    w += ["msk_0", "msk_1", "at_1_%d_ctx" % (NH - 1), "sc_done",
                          "penr_0", "penr_1"]
                    if "scores" in dbg_t:
                        w.append("d_sc")
                    if "mask" in dbg_t:
                        w.append("d_mk")
                if c >= 1:
                    w.append("ost_%d" % (c - 1))
            emit("dve", mko(), w, "oadd_%d" % c if u16 == 15 else None)
        def mkst(c=c):
            def fn(eng):
                return eng.dma_start(out_t[c * 128:(c + 1) * 128, :], xv[0][:, :])
            return fn
        emit("sp", mkst(), ["oadd_%d" % c], "ost_%d" % c, n=16, fam="out")
    drain = ["ost_%d" % (NC - 1)]
    for s in list(cnt):
        if s.startswith("d_dbg"):
            drain.append((s, cnt[s]))
    emit("sp", lambda eng: eng.nop(), drain, None)
    for e in ("pe", "act", "dve", "pool"):
        emit(e, lambda eng: eng.nop(), drain, None)

    def check_sync():
        ptr = {e: 0 for e in streams}
        eclk = {e: {} for e in streams}
        prod_clk = {}
        made = {}
        dma_info = []
        wait_reg = []
        progress = True
        nleft = sum(len(s) for s in streams.values())
        while nleft and progress:
            progress = False
            for e in streams:
                while ptr[e] < len(streams[e]):
                    fn, w, rec, inc = streams[e][ptr[e]]
                    ok = True
                    for (s, v) in w:
                        vs = made.get(s, [])
                        if not vs or vs[-1] < v:
                            ok = False
                            break
                    if not ok:
                        break
                    clk = eclk[e]
                    for (s, v) in w:
                        import bisect
                        vs = made[s]
                        pv = vs[bisect.bisect_left(vs, v)]
                        for k2, v2 in prod_clk[(s, pv)].items():
                            if clk.get(k2, 0) < v2:
                                clk[k2] = v2
                        if clk.get(s, 0) < v:
                            clk[s] = v
                        wait_reg.append((s, v, e, ptr[e]))
                    if rec is not None:
                        s, n_ = rec
                        if s.startswith("d_") or s == "s_cc":
                            dma_info.append((s, EV[inc][1] if inc else None,
                                             clk.get(s, 0), inc, e))
                        val = EV[inc][1] if inc else None
                        if val is None:
                            val = made[s][-1] + n_ if s in made else n_
                        cc = dict(clk)
                        cc[s] = max(cc.get(s, 0), val)
                        prod_clk[(s, val)] = cc
                        made.setdefault(s, []).append(val)
                        clk[s] = max(clk.get(s, 0), val) if not s.startswith("d_") \
                            else clk.get(s, 0)
                    ptr[e] += 1
                    nleft -= 1
                    progress = True
        if nleft:
            stuck = {e: (ptr[e], streams[e][ptr[e]][1] if ptr[e] < len(streams[e])
                         else None) for e in streams if ptr[e] < len(streams[e])}
            raise RuntimeError("check_sync DEADLOCK: %r (made=%r)"
                               % (stuck, {k: v[-3:] for k, v in made.items()}))
        waited = {}
        for (s, v, we, wi) in wait_reg:
            if s.startswith("d_"):
                waited.setdefault(s, set()).add(v)
        bad = []
        for (s, val, wmark, inc, e) in dma_info:
            if s == "s_cc" or val is None:
                continue
            for v in waited.get(s, ()):
                if val > v and wmark < v:
                    bad.append((s, "waited", v, "later dma", inc, "at", val,
                                "issue watermark", wmark))
        if bad:
            raise RuntimeError("check_sync RACES:\n" +
                               "\n".join(str(b) for b in bad[:20]) +
                               "\n(%d total)" % len(bad))

    check_sync()

    semobjs = {}
    for s in cnt:
        semobjs[s] = nc.alloc_semaphore(name=s)

    import contextlib
    with nc.Block() as block, nc.allow_low_precision(reason="bf16 residual stream by design"):
        def mkbody(engname):
            def body(eng):
                for fn, waits, rec, _inc in streams[engname]:
                    for (s, v) in waits:
                        eng.wait_ge(semobjs[s], v)
                    ins = fn(eng)
                    if rec is not None and ins is not None:
                        ins.then_inc(semobjs[rec[0]], rec[1])
            return body
        block.tensor(mkbody("pe"))
        block.scalar(mkbody("act"))
        block.vector(mkbody("dve"))
        block.gpsimd(mkbody("pool"))
        block.sync(mkbody("sp"))

    for h in reversed(ex):
        h.__exit__(None, None, None)
    build_nc.last_cnt = dict(cnt)
    build_nc.last_EV = dict(EV)
    build_nc.last_waits = list(WAITLOG)
    return nc, dbg_t


def stage_inputs(inputs):
    X = np.asarray(inputs["L0_attn_X"])
    Wp = np.ascontiguousarray(inputs["L0_attn_Wp"], np.float32)
    bp = np.ascontiguousarray(inputs["L0_attn_bp"], np.float32).reshape(1, -1)
    Wf = np.ascontiguousarray(inputs["L0_attn_Wf"], np.float32)
    bf_ = np.ascontiguousarray(inputs["L0_attn_bf"], np.float32).reshape(1, -1)
    Wq = np.ascontiguousarray(inputs["L0_ffn_Wq"], np.float32)
    Wk = np.ascontiguousarray(inputs["L0_ffn_Wk"], np.float32)
    Wv = np.ascontiguousarray(inputs["L0_ffn_Wv"], np.float32)
    Wo = np.ascontiguousarray(inputs["L0_ffn_Wo"], np.float32)
    Ws1 = np.ascontiguousarray(inputs["L0_ffn_Ws1"], np.float32)
    bs1 = np.ascontiguousarray(inputs["L0_ffn_bs1"], np.float32).reshape(1, -1)
    Ws2 = np.ascontiguousarray(inputs["L0_ffn_Ws2"], np.float32)
    bs2 = np.ascontiguousarray(inputs["L0_ffn_bs2"], np.float32).reshape(1, 1)

    t = np.arange(T)
    v, u = t // 64, t % 64
    uh, u32 = u // 32, u % 32
    prow = uh * 64 + v
    w1 = build_w1()
    w2 = build_w2()
    wi1 = build_wi1()
    wi2 = build_wi2()
    pos_sc = build_pos_scrambled()
    ident = np.eye(128, dtype=np.float32)
    ws2t = np.ascontiguousarray(Ws2.reshape(2, 128).T)

    in_maps = []
    cols = u32[:, None] + np.arange(128)[None, :] * 32
    for core in range(NCORE):
        Xc = np.ascontiguousarray(X[2 * core:2 * core + 2], np.float32)
        x_nat = np.ascontiguousarray(Xc.transpose(2, 0, 1).reshape(D, BPC * T))
        x_til = np.zeros((NC * 128, 4096), np.float32)
        for b in range(BPC):
            rows = prow + b * 32
            for c in range(NC):
                x_til[c * 128 + rows[:, None], cols] = Xc[b, :, c * 128:(c + 1) * 128]
        ccols = (np.arange(3)[:, None] * 1024 + core * 128 + np.arange(128)[None, :]).ravel()
        wf_slc = np.ascontiguousarray(Wf[:, ccols])
        bf_slc = np.ascontiguousarray(bf_[:, ccols])
        in_maps.append(dict(
            x_nat=x_nat, x_til=x_til, wp=Wp, bp_d=bp, wf_sl=wf_slc, bf_sl=bf_slc,
            pos_d=pos_sc, wq_d=Wq, wk_d=Wk, wv_d=Wv, wo_d=Wo, ws1_d=Ws1,
            bs1_d=bs1, ws2t_d=ws2t, bs2_d=bs2, w1c=w1, w2c=w2, wi1c=wi1,
            wi2c=wi2, identc=ident))
    return in_maps


def unscramble_out(res_list):
    t = np.arange(T)
    v, u = t // 64, t % 64
    uh, u32 = u // 32, u % 32
    prow = uh * 64 + v
    out = np.empty((2 * NCORE, T, D), np.float32)
    cols = u32[:, None] + np.arange(128)[None, :] * 32
    for core in range(NCORE):
        o = res_list[core]
        for b in range(BPC):
            rows = prow + b * 32
            for c in range(NC):
                out[2 * core + b, :, c * 128:(c + 1) * 128] = o[c * 128 + rows[:, None], cols]
    return out


def kernel(**inputs):
    dbg = tuple(x for x in os.environ.get("MK_DBG", "").split(",") if x)
    nc, dbg_t = build_nc(dbg)
    in_maps = stage_inputs(inputs)
    res = run_bass_kernel_spmd(nc, in_maps, core_ids=list(range(NCORE)))
    outs = [np.asarray(res.results[i]["out_t"]) for i in range(NCORE)]
    if dbg:
        kernel.dbg_results = res.results
    return unscramble_out(outs)


# version 05
# speedup vs baseline: 1.1905x; 1.1905x over previous
import os
import numpy as np
import ml_dtypes
import concourse.bass as bass
import concourse.mybir as mybir
from concourse.bass_utils import run_bass_kernel_spmd

F32 = mybir.dt.float32
BF16 = mybir.dt.bfloat16
AF = mybir.ActivationFunctionType
ALU = mybir.AluOpType
AX = mybir.AxisListType

T = 2048
D = 1024
NCORE = 8
BPC = 2
NC = 8
NKC = 8
NH = 16

bf16 = ml_dtypes.bfloat16


def build_w1():
    v = np.arange(32)
    km = np.arange(64)
    th = 2 * np.pi * np.outer(v, km) / 64.0
    W = np.zeros((64, 128), np.float64)
    W[0:32, 0:64] = np.cos(th)
    W[32:64, 0:64] = np.sin(th)
    W[0:32, 64:128] = -np.sin(th)
    W[32:64, 64:128] = np.cos(th)
    return np.concatenate([W, W], axis=0).astype(bf16)


def build_w2():
    u = np.arange(64)
    kd = np.arange(64)
    out = np.zeros((64, 64, 2, 2, 64), np.float64)
    for km in range(64):
        phi = 2 * np.pi * np.outer(u, (km + 64 * kd)) / 4096.0
        out[:, km, 0, 0, :] = np.cos(phi)
        out[:, km, 0, 1, :] = -np.sin(phi)
        out[:, km, 1, 0, :] = np.sin(phi)
        out[:, km, 1, 1, :] = np.cos(phi)
    flat = out.reshape(64, 64 * 256)
    return np.concatenate([flat, flat], axis=0).astype(bf16)


def build_wi1():
    kd = np.arange(64)
    tu = np.arange(64)
    psi = 2 * np.pi * np.outer(kd, tu) / 64.0
    W = np.zeros((64, 192), np.float64)
    W[:, 0:64] = np.cos(psi)
    W[:, 64:128] = np.sin(psi)
    W[:, 128:192] = -np.sin(psi)
    return np.concatenate([W, W], axis=0).astype(bf16)


def build_wi2():
    km = np.arange(64)
    tv = np.arange(32)
    out = np.zeros((2, 64, 64, 2, 32), np.float64)
    for tu in range(64):
        chi = 2 * np.pi * np.outer(km, (64 * tv + tu)) / 4096.0
        out[0, :, tu, 0, :] = np.cos(chi)
        out[1, :, tu, 0, :] = -np.sin(chi)
        out[0, :, tu, 1, :] = np.sin(chi)
        out[1, :, tu, 1, :] = np.cos(chi)
    return out.reshape(128, 4096).astype(bf16)


def build_pos_scrambled():
    pos_t = np.arange(T, dtype=np.float64)[:, None]
    i = np.arange(D)[None, :]
    angle = pos_t / np.power(10000.0, (2 * (i // 2)).astype(np.float64) / D)
    pos = np.where(i % 2 == 0, np.sin(angle), np.cos(angle))
    t = np.arange(T)
    v, u = t // 64, t % 64
    uh, u32 = u // 32, u % 32
    f = u32 * 64 + uh * 32 + v
    pos_sc = np.zeros((D, T), np.float32)
    pos_sc[:, f] = pos.T.astype(np.float32)
    return pos_sc


EVD = set(os.environ.get("MK_EVD", "azi"))


def build_nc(dbg=()):
    dbg = set(dbg)
    nc = bass.Bass("TRN2", target_bir_lowering=False)

    def din(name, shape, dtype=F32):
        return nc.dram_tensor(name, shape, dtype, kind="ExternalInput").ap()

    x_nat = din("x_nat", [D, BPC * T])
    x_til = din("x_til", [NC * 128, 4096])
    wp = din("wp", [D, 4 * D])
    bp_d = din("bp_d", [1, 4 * D])
    wf_sl = din("wf_sl", [D, 384])
    bf_sl = din("bf_sl", [1, 384])
    pos_d = din("pos_d", [D, T], BF16)
    wq_d = din("wq_d", [D, D])
    wk_d = din("wk_d", [D, D])
    wv_d = din("wv_d", [D, D])
    wo_d = din("wo_d", [D, D])
    ws1_d = din("ws1_d", [D, 256])
    bs1_d = din("bs1_d", [1, 256])
    ws2t_d = din("ws2t_d", [128, 2])
    bs2_d = din("bs2_d", [1, 1])
    w1c = din("w1c", [128, 128], BF16)
    w2c = din("w2c", [128, 16384], BF16)
    wi1c = din("wi1c", [128, 192], BF16)
    wi2c = din("wi2c", [128, 4096], BF16)
    identc = din("identc", [128, 128], F32)

    out_t = nc.dram_tensor("out_t", [NC * 128, 4096], F32, kind="ExternalOutput").ap()
    f_slice = nc.dram_tensor("f_slice", [384, 8192], BF16).ap()
    f_full = [nc.dram_tensor("f_full%d" % s, [1024, 8192], BF16).ap()
              for s in range(3)]
    y3_dram = nc.dram_tensor("y3_dram", [NC * 128, 4096], BF16).ap()

    dbg_t = {}
    for name, shape in [("xtilde", [128, NKC * 4096]), ("tile0", [128, 4096]),
                        ("tile1", [128, 4096]), ("asb", [128, 8192]),
                        ("wspec", [128, 8192]), ("y1", [128, 4096]),
                        ("y3", [128, 4096]), ("groups", [128, 2048]),
                        ("scores", [1, 256]), ("mask", [128, 2]),
                        ("outg", [128, 2048]), ("x1c0", [128, 4096]),
                        ("fsl", [384, 8192])]:
        if name in dbg:
            dbg_t[name] = nc.dram_tensor("dbg_" + name, shape, F32,
                                         kind="ExternalOutput").ap()

    ex = []

    def sbuf(name, shape, dtype):
        h = nc.sbuf_tensor(name, shape, dtype)
        tt = h.__enter__()
        ex.append(h)
        return tt

    def psumt(name, shape):
        h = nc.psum_tensor(name, shape, F32)
        tt = h.__enter__()
        ex.append(h)
        return tt

    xt = sbuf("xt", [128, NKC * 4096], BF16)
    w1s = sbuf("w1s", [128, 128], BF16)
    w2s = sbuf("w2s", [128, 16384], BF16)
    wi1s = sbuf("wi1s", [128, 192], BF16)
    wi2s = sbuf("wi2s", [128, 4096], BF16)
    idents = sbuf("idents", [128, 128], F32)
    tile = [sbuf(f"tile{i}", [128, 4096], BF16) for i in range(4)]
    absb = sbuf("absb", [128, 8192], BF16)
    wsb = sbuf("wsb", [128, 8192], BF16)
    fsb = [sbuf(f"fsb{i}", [128, 1024], BF16) for i in range(2)]
    wps = [sbuf(f"wps{i}", [128, 2048], BF16) for i in range(2)]
    m128 = sbuf("m128", [128, 4096], BF16)
    r128 = sbuf("r128", [128, 4096], BF16)
    onesb = sbuf("onesb", [128, 256], BF16)
    ones32 = sbuf("ones32", [128, 128], F32)
    epsc = sbuf("epsc", [128, 1], F32)
    rowf = sbuf("rowf", [1, 512], BF16)
    ws2s = sbuf("ws2s", [128, 2], BF16)
    st12 = sbuf("st12", [128, 128], F32)

    pp = [psumt(f"pp{i}", [128, 512]) for i in range(2)]
    pa = [psumt(f"pa{i}", [128, 512]) for i in range(2)]
    px = [psumt(f"px{d}", [128, 512]) for d in range(2)]
    pyp = [psumt(f"py{d}", [128, 256]) for d in range(2)]

    xv = [absb.ap().bitcast(F32), wsb.ap().bitcast(F32)]

    EV = {}
    cnt = {}
    streams = {k: [] for k in ("pe", "act", "dve", "pool", "sp")}

    anon = [0]
    WAITLOG = []

    def emit(engine, fn, waits=(), inc=None, n=1, fam=None, serial=False):
        w = []
        for x in waits:
            if x is None:
                continue
            WAITLOG.append(x if isinstance(x, str) else None)
            w.append(EV[x] if isinstance(x, str) else x)
        rec = None
        if inc is not None and inc.startswith("cc_done"):
            s = "s_" + inc
            cnt[s] = 1
            EV[inc] = (s, 1)
            streams[engine].append((fn, w, (s, 1), inc))
            return
        if n == 16:
            if inc is None:
                anon[0] += 1
                inc = "_anon%d" % anon[0]
            s = "d_" + (fam if fam is not None else engine + "_anon")
            cnt[s] = cnt.get(s, 0) + 16
            EV[inc] = (s, cnt[s])
            streams[engine].append((fn, w, (s, 16), inc))
            return
        if engine in ("dve", "act"):
            s = "s_" + engine
            prev = cnt.get(s, 0)
            tgt = prev if serial else prev - 1
            if tgt >= 1:
                w.append((s, tgt))
            if inc is None:
                anon[0] += 1
                inc = "_anon%d" % anon[0]
        if inc is not None:
            s = "s_" + engine
            cnt[s] = cnt.get(s, 0) + n
            EV[inc] = (s, cnt[s])
            rec = (s, n)
        streams[engine].append((fn, w, rec, inc))

    def ev_now(engine):
        s = "s_" + engine
        return (s, cnt.get(s, 0))

    def dma(engine, dst, src, waits=(), inc=None, fam=None):
        emit(engine, (lambda d=dst, s=src: (lambda eng: eng.dma_start(d, s)))(),
             waits, inc, n=16, fam=fam)

    emit("dve", lambda eng: eng.memset(onesb[:, :], 1.0), [], None)
    emit("dve", lambda eng: eng.memset(ones32[:, :], 1.0), [], None)
    emit("dve", lambda eng: eng.memset(epsc[:, :], 1e-5), [], "ones_set")
    emit("dve", lambda eng: eng.memset(
        bass.AP(xt, 0, [[NKC * 4096, 1], [1, 8192]]), 0), [], "stz0")
    dma("sp", w1s[:, :], w1c, [], "ld_w1", fam="c0")
    dma("sp", wi1s[:, :], wi1c, [], "ld_wi1", fam="c0")
    dma("sp", wi2s[:, :], wi2c, [], "ld_wi2", fam="c0")
    dma("sp", idents[:, :], identc, [], "ld_id", fam="c0")
    for _nm in ("ld_w1", "ld_wi1", "ld_wi2", "ld_id"):
        EV[_nm] = EV["ld_id"]
    dma("pool", rowf[0:1, 0:384], bf_sl, [], "ld_bf", fam="bf")
    dma("pool", ws2s[:, :], ws2t_d, [], "ld_ws2", fam="ws2")
    for kc in range(NKC):
        dma("pool", bass.AP(w2s, kc * 2048, [[16384, 128], [1, 2048]]),
            pos_d[kc * 128:(kc + 1) * 128, :], [], "ld_pos%d" % kc, fam="pos")
        dma("pool", bass.AP(tile[3], kc * 384, [[4096, 128], [1, 384]]),
            wf_sl[kc * 128:(kc + 1) * 128, :], [], "ld_wf%d" % kc, fam="wf")

    rowx = lambda off, n: bass.AP(xt, off, [[NKC * 4096, 1], [1, n]])
    strot = [pa[0], pa[1], px[0], px[1]]
    for kc in range(NKC):
        bufi = kc % 2
        w = ["st_dv_%d" % (kc - 2)] if kc >= 2 else []
        dma("sp", xv[bufi][:, :], x_nat[kc * 128:(kc + 1) * 128, :], w, "ld_x%d" % kc,
            fam="xa" if bufi == 0 else "xw")
        emit("dve",
             (lambda b=bufi: (lambda eng: eng.tensor_copy(tile[0][:, :], xv[b][:, :])))(),
             ["ld_x%d" % kc, "ones_set", "stz0"]
             + (["st_pe_%d" % (kc - 1)] if kc >= 1 else []), "cast_%d" % kc)
        emit("act",
             (lambda b=bufi: (lambda eng: eng.activation(tile[1][:, :], xv[b][:, :],
                                                         AF.Square)))(),
             ["ld_x%d" % kc] + (["st_pe_%d" % (kc - 1)] if kc >= 1 else []),
             "sqr_%d" % kc)
        for nsl in range(8):
            for mom in range(2):
                ridx = (nsl * 2 + mom) % 4
                def mk(kc=kc, nsl=nsl, mom=mom, ridx=ridx):
                    def fn(eng):
                        return eng.matmul(strot[ridx][:, :], onesb[:, 0:128],
                                          tile[mom][:, nsl * 512:(nsl + 1) * 512],
                                          start=True, stop=True)
                    return fn
                w = ["cast_%d" % kc, "sqr_%d" % kc] if (nsl == 0 and mom == 0) else []
                if (kc, nsl, mom) != (0, 0, 0):
                    pr = (kc * 16 + nsl * 2 + mom) - 4
                    if pr >= 0:
                        w.append("st_add_%d" % pr)
                emit("pe", mk(), w, "st_mm_%d" % (kc * 16 + nsl * 2 + mom))
                def mka(kc=kc, nsl=nsl, mom=mom, ridx=ridx):
                    def fn(eng):
                        return eng.tensor_add(rowx(mom * 4096 + nsl * 512, 512),
                                              rowx(mom * 4096 + nsl * 512, 512),
                                              strot[ridx][0:1, :])
                    return fn
                emit("dve", mka(), ["st_mm_%d" % (kc * 16 + nsl * 2 + mom)],
                     "st_add_%d" % (kc * 16 + nsl * 2 + mom))
        EV["st_pe_%d" % kc] = ev_now("pe")
        EV["st_dv_%d" % kc] = ev_now("dve")
    EV["stats_done"] = ev_now("dve")
    trow = lambda off, n: bass.AP(tile[0], off, [[4096, 1], [1, n]])
    emit("dve", lambda eng: eng.tensor_scalar_mul(rowx(0, 4096), rowx(0, 4096),
                                                  -1.0 / 1024.0), ["stats_done"], None,
         serial=True)
    emit("dve", lambda eng: eng.tensor_scalar_mul(rowx(4096, 4096), rowx(4096, 4096),
                                                  1.0 / 1024.0), [], None, serial=True)
    emit("dve", lambda eng: eng.tensor_mul(trow(0, 4096), rowx(0, 4096), rowx(0, 4096)),
         [], None, serial=True)
    emit("dve", lambda eng: eng.tensor_sub(rowx(4096, 4096), rowx(4096, 4096),
                                           trow(0, 4096)), [], "var_done", serial=True)
    emit("act", lambda eng: eng.activation(rowx(4096, 4096), rowx(4096, 4096), AF.Sqrt,
                                           bias=epsc[0:1, 0:1]), ["var_done"], "sd_done")
    emit("dve", lambda eng: eng.reciprocal(rowx(4096, 4096), rowx(4096, 4096)),
         ["sd_done"], "r_done")
    for half, dst in ((0, m128), (1, r128)):
        for nsl in range(8):
            def mkb(half=half, nsl=nsl):
                def fn(eng):
                    return eng.matmul(pp[nsl % 2][:, :], onesb[0:1, 0:128],
                                      rowx(half * 4096 + nsl * 512, 512),
                                      start=True, stop=True)
                return fn
            w = ["r_done"] if (half == 0 and nsl == 0) else []
            if nsl >= 2:
                w.append("bc_%d_%d" % (half, nsl - 2))
            elif half == 1:
                w.append("bc_0_%d" % (6 + nsl))
            emit("pe", mkb(), w, "bcm_%d_%d" % (half, nsl))
            def mkbe(half=half, nsl=nsl, dst=dst):
                def fn(eng):
                    return eng.activation(dst[:, nsl * 512:(nsl + 1) * 512],
                                          pp[nsl % 2][:, :], AF.Identity)
                return fn
            emit("act", mkbe(), ["bcm_%d_%d" % (half, nsl)], "bc_%d_%d" % (half, nsl))
    emit("dve", lambda eng: eng.tensor_mul(m128[:, :], m128[:, :], r128[:, :]),
         ["bc_1_7"], "mr_mul")
    scr_ap = lambda base: bass.AP(xt, base,
                                  [[NKC * 4096, 128], [32, 2], [1, 32], [64, 2], [128, 32]])
    for kc in range(NKC):
        bufi = kc % 2
        w = ["st_pe_%d" % (NKC - 1)] if kc < 2 else ["xt_%d" % (kc - 2)]
        if kc == 0:
            w = w + ["bcm_1_7", "stats_done"]
        dma("sp", xv[bufi][:, :], x_nat[kc * 128:(kc + 1) * 128, :], w, "l2x_%d" % kc,
            fam="xa" if bufi == 0 else "xw")
        emit("dve", (lambda kc=kc, b=bufi: (lambda eng: eng.tensor_mul(
            scr_ap(kc * 4096), xv[b][:, :], r128.ap())))(),
            ["l2x_%d" % kc, "mr_mul"], None)
        emit("dve", (lambda kc=kc: (lambda eng: eng.tensor_add(
            scr_ap(kc * 4096), scr_ap(kc * 4096), m128.ap())))(), [], "xt_%d" % kc,
            serial=True)
    EV["xt_all"] = ev_now("dve")
    dma("pool", bass.AP(m128, 64 * 4096, [[4096, 1], [1, 4096]]), bp_d,
        ["xt_all"], "ld_bp", fam="bp")
    if "xtilde" in dbg_t:
        dma("pool", dbg_t["xtilde"], xt.ap(), ["xt_all"], "d_xt", fam="dbgxt")

    for s in range(3):
        emit("dve", (lambda s=s: (lambda eng: eng.memset(tile[s][:, :], 0)))(),
             ["st_pe_%d" % (NKC - 1), "stats_done"], "fz_%d" % s)
    for seti in range(64):
        u32, uh = seti // 2, seti % 2
        rot = seti % 2
        for kc in range(NKC):
            def mk(u32=u32, uh=uh, kc=kc, rot=rot):
                def fn(eng):
                    return eng.matmul(
                        pp[rot][uh * 64:uh * 64 + 32, 0:384],
                        bass.AP(w2s, kc * 2048 + u32 * 64 + uh * 32,
                                [[16384, 128], [1, 32]]),
                        bass.AP(tile[3], kc * 384, [[4096, 128], [1, 384]]),
                        start=(kc == 0), stop=False, tile_position=(0, uh * 64))
                return fn
            w = []
            if kc == 0:
                if seti == 0:
                    w = ["ld_pos%d" % (NKC - 1), "ld_wf%d" % (NKC - 1), "bc_1_7"]
                if seti >= 2:
                    w.append("fpe_%d" % (seti - 2))
            emit("pe", mk(), w, None)
        def mkb(uh=uh, rot=rot):
            def fn(eng):
                return eng.matmul(pp[rot][uh * 64:uh * 64 + 32, 0:384],
                                  onesb[0:1, 0:32], rowf[0:1, 0:384],
                                  start=False, stop=True, tile_position=(0, uh * 64))
            return fn
        emit("pe", mkb(), ["ld_bf"] if seti == 0 else [], "fpm_%d" % seti)
        for s in range(3):
            def mke(u32=u32, uh=uh, s=s, rot=rot):
                def fn(eng):
                    dst = bass.AP(tile[s], (uh * 64) * 4096 + u32, [[4096, 32], [32, 128]])
                    src = bass.AP(pp[rot], (uh * 64) * 512 + s * 128, [[512, 32], [1, 128]])
                    return eng.activation(dst, src, AF.Identity)
                return fn
            emit("act", mke(), ["fpm_%d" % seti, "fz_%d" % s],
                 "fpe_%d" % seti if s == 2 else None)
    EV["fproj_done"] = ev_now("act")
    dma("sp", w2s[:, :], w2c, ["fproj_done"], "ld_w2", fam="w2")

    def emit_fwd(name, ytile, l1_waits, fscale):
        for dd in range(128):
            rot = (dd // 8) % 2
            blk = (dd // 2) % 4
            for uh in range(2):
                base = (dd % 2) * 64 + uh * 32
                def mk(dd=dd, uh=uh, base=base, rot=rot, blk=blk, yt=ytile):
                    def fn(eng):
                        return eng.matmul(
                            pa[rot][base:base + 32, blk * 128:blk * 128 + 128],
                            yt[uh * 64:(uh + 1) * 64, dd * 32:(dd + 1) * 32],
                            w1s[uh * 64:(uh + 1) * 64, :],
                            start=True, stop=True, tile_position=(uh * 64, base))
                    return fn
                w = []
                if uh == 0 and dd % 8 == 0:
                    if dd == 0:
                        w = list(l1_waits) + ["ld_w1"]
                    if dd >= 16:
                        w.append("%s_ae_%d" % (name, dd // 8 - 2))
                emit("pe", mk(), w,
                     "%s_l1_%d" % (name, dd // 8) if (dd % 8 == 7 and uh == 1) else None)
            if dd % 8 == 7:
                g = dd // 8
                def mke(g=g, rot=(g % 2)):
                    def fn(eng):
                        dst = bass.AP(absb, g * 4, [[8192, 128], [1, 4], [64, 128]])
                        src = bass.AP(pa[rot], 0, [[512, 128], [128, 4], [1, 128]])
                        if g % 2 == 0 or "a" not in EVD:
                            return eng.activation(dst, src, AF.Identity)
                        return eng.tensor_copy(dst, src)
                    return fn
                emit("act" if (g % 2 == 0 or "a" not in EVD) else "dve", mke(),
                     ["%s_l1_%d" % (name, g)], "%s_ae_%d" % (name, g))
        for kmq in range(32):
            rot = kmq % 2
            for dpar in range(2):
                for kmi in range(2):
                    km = kmq * 2 + kmi
                    for riout in range(2):
                        for ri in range(2):
                            def mk(km=km, dpar=dpar, ri=ri, riout=riout,
                                   rot=rot, kmi=kmi):
                                def fn(eng):
                                    return eng.matmul(
                                        px[rot][dpar * 64:(dpar + 1) * 64,
                                                (kmi * 2 + riout) * 64:(kmi * 2 + riout) * 64 + 64],
                                        w2s[dpar * 64:(dpar + 1) * 64,
                                            km * 256 + ri * 128 + riout * 64:
                                            km * 256 + ri * 128 + riout * 64 + 64],
                                        absb[dpar * 64:(dpar + 1) * 64,
                                             (ri * 64 + km) * 64:(ri * 64 + km) * 64 + 64],
                                        start=(ri == 0), stop=(ri == 1),
                                        tile_position=(dpar * 64, dpar * 64))
                                return fn
                            w = []
                            if kmi == 0 and ri == 0 and riout == 0 and dpar == 0:
                                if kmq == 0:
                                    w = ["%s_ae_%d" % (name, g) for g in (14, 15)] + ["ld_w2"]
                                if kmq >= 2:
                                    w += ["%s_ze_%d" % (name, kmq - 2)]
                            emit("pe", mk(), w,
                                 "%s_l2_%d" % (name, kmq)
                                 if (kmi == 1 and riout == 1 and ri == 1 and dpar == 1) else None)
            def mkz(kmq=kmq, rot=rot, fs=fscale, kmi=None):
                def fn(eng):
                    km0 = kmq * 2
                    if kmi is None:
                        dst = bass.AP(wsb, km0,
                                      [[8192, 128], [1, 2], [64, 2], [128, 64]])
                        src = bass.AP(px[rot], 0,
                                      [[512, 128], [128, 2], [64, 2], [1, 64]])
                        if fs:
                            return eng.activation(dst, src, AF.Identity,
                                                  scale=1.0 / 4096.0)
                        return eng.activation(dst, src, AF.Identity)
                    dst = bass.AP(wsb, km0 + kmi, [[8192, 128], [64, 2], [128, 64]])
                    src = bass.AP(px[rot], kmi * 128, [[512, 128], [64, 2], [1, 64]])
                    if fs:
                        return eng.tensor_scalar_mul(dst, src, 1.0 / 4096.0)
                    return eng.tensor_copy(dst, src)
                return fn
            if kmq % 2 == 0 or "z" not in EVD:
                emit("act", mkz(), ["%s_l2_%d" % (name, kmq)], "%s_ze_%d" % (name, kmq))
            else:
                emit("dve", mkz(kmi=0), ["%s_l2_%d" % (name, kmq)], None)
                emit("dve", mkz(kmi=1), [], "%s_ze_%d" % (name, kmq))
        EV["%s_zdone" % name] = EV["%s_ze_30" % name]
        EV["%s_zdoned" % name] = EV["%s_ze_31" % name]

    def emit_filter(name, c, s, first_waits):
        def zre(e):
            return bass.AP(wsb, e * 1024, [[8192, 128], [128, 8], [1, 64]])
        def zim(e):
            return bass.AP(wsb, e * 1024 + 64, [[8192, 128], [128, 8], [1, 64]])
        def fre(fb):
            return bass.AP(fsb[fb], 0, [[1024, 128], [128, 8], [1, 64]])
        def fim(fb):
            return bass.AP(fsb[fb], 64, [[1024, 128], [128, 8], [1, 64]])
        t1 = bass.AP(r128, 0, [[4096, 128], [1, 512]])
        t2 = bass.AP(r128, 512, [[4096, 128], [1, 512]])
        for e in range(8):
            fb = e % 2
            w = []
            if e < 2:
                w = list(first_waits)
                if not (c == 0 and s == 0):
                    pcs = (c, s - 1) if s > 0 else (c - 1, 2)
                    w.append("m%d_%d_fi_%d" % (pcs[0], pcs[1], 6 + e))
            else:
                w.append("%s_fi_%d" % (name, e - 2))
            def mkd(c=c, s=s, e=e, fb=fb):
                def fn(eng):
                    rows = f_full[s][c * 128:(c + 1) * 128,
                                     e * 1024:(e + 1) * 1024]
                    return eng.dma_start(fsb[fb][:, :], rows)
                return fn
            emit("sp", mkd(), w, "%s_fl_%d" % (name, e), n=16, fam="fb%d" % fb)
            w = ["%s_zdone" % name, "%s_zdoned" % name, "%s_fl_%d" % (name, e)]
            emit("dve", (lambda fb=fb, e=e, t1=t1: (lambda eng: eng.tensor_mul(
                t1, zim(e), fim(fb))))(), w, None, serial=True)
            emit("dve", (lambda fb=fb, e=e, t2=t2: (lambda eng: eng.tensor_mul(
                t2, zim(e), fre(fb))))(), [], None)
            emit("dve", (lambda fb=fb, e=e: (lambda eng: eng.tensor_mul(
                zim(e), zre(e), fim(fb))))(), [], None, serial=True)
            emit("dve", (lambda e=e, t2=t2: (lambda eng: eng.tensor_add(
                zim(e), zim(e), t2)))(), [], None, serial=True)
            emit("dve", (lambda fb=fb, e=e: (lambda eng: eng.tensor_mul(
                zre(e), zre(e), fre(fb))))(), [], None, serial=True)
            emit("dve", (lambda e=e, t1=t1: (lambda eng: eng.tensor_sub(
                zre(e), zre(e), t1)))(), [], "%s_fi_%d" % (name, e), serial=True)
        EV["%s_filt" % name] = ev_now("dve")

    def emit_inv(name, ytile_next, gtile, gate_ready=None):
        part_of = {(0, 0): 0, (0, 1): 64, (1, 0): 128, (1, 1): 0}
        for dd in range(128):
            rot = (dd // 8) % 2
            blk = (dd // 2) % 4
            dpar = dd % 2
            tq = dd // 2
            for riout in range(2):
                for riin in range(2):
                    def mk(dd=dd, rot=rot, blk=blk, dpar=dpar, tq=tq,
                           riin=riin, riout=riout):
                        def fn(eng):
                            pt = pa[rot] if dpar == 0 else pyp[rot]
                            return eng.matmul(
                                pt[riout * 64:riout * 64 + 64,
                                   blk * 64:blk * 64 + 64],
                                bass.AP(wsb, tq * 128 + riin * 64 + dpar * 64 * 8192,
                                        [[8192, 64], [1, 64]]),
                                wi1s[dpar * 64:(dpar + 1) * 64,
                                     part_of[(riin, riout)]:part_of[(riin, riout)] + 64],
                                start=(riin == 0), stop=(riin == 1),
                                tile_position=(dpar * 64, riout * 64))
                        return fn
                    w = []
                    if dd % 8 == 0 and riout == 0 and riin == 0:
                        w = ["%s_fi_%d" % (name, dd // 16), "ld_wi1"]
                        if dd >= 16:
                            w.append("%s_be_%d" % (name, dd // 8 - 2))
                            w.append("%s_b0_%d" % (name, dd // 8 - 2))
                    emit("pe", mk(), w,
                         "%s_i1_%d" % (name, dd // 8)
                         if (dd % 8 == 7 and riout == 1 and riin == 1) else None)
            if dd % 8 == 7:
                g = dd // 8
                def mke0(g=g, rot=(g % 2)):
                    def fn(eng):
                        dst = bass.AP(absb, g * 8, [[8192, 128], [2, 4], [128, 64]])
                        src = bass.AP(pa[rot], 0, [[512, 128], [64, 4], [1, 64]])
                        return eng.activation(dst, src, AF.Identity)
                    return fn
                def mke1(g=g, rot=(g % 2)):
                    def fn(eng):
                        dst = bass.AP(absb, g * 8 + 1, [[8192, 128], [2, 4], [128, 64]])
                        src = bass.AP(pyp[rot], 0, [[256, 128], [64, 4], [1, 64]])
                        if "i" not in EVD:
                            return eng.activation(dst, src, AF.Identity)
                        return eng.tensor_copy(dst, src)
                    return fn
                emit("act", mke0(), ["%s_i1_%d" % (name, g)], "%s_b0_%d" % (name, g))
                emit("dve" if "i" in EVD else "act", mke1(),
                     ["%s_i1_%d" % (name, g)], "%s_be_%d" % (name, g))
        for tpq in range(16):
            rot = tpq % 2
            for tpi in range(2):
                tp = tpq * 2 + tpi
                for member in range(2):
                    tum = tp + 32 * member
                    def mk(tpi=tpi, member=member, tum=tum, rot=rot):
                        def fn(eng):
                            return eng.matmul(
                                pyp[rot][member * 64:member * 64 + 64,
                                         tpi * 128:tpi * 128 + 128],
                                wi2s[:, tum * 64:tum * 64 + 64],
                                bass.AP(absb, tum * 128, [[8192, 128], [1, 128]]),
                                start=True, stop=True,
                                tile_position=(0, member * 64))
                        return fn
                    w = []
                    if tpi == 0 and member == 0:
                        if tpq == 0:
                            w = (["%s_be_%d" % (name, g) for g in (14, 15)]
                                 + ["%s_b0_%d" % (name, g) for g in (14, 15)]
                                 + ["ld_wi2"])
                        if tpq >= 2:
                            w.append("%s_ye_%d" % (name, tpq - 2))
                    emit("pe", mk(), w,
                         "%s_i2_%d" % (name, tpq)
                         if (tpi == 1 and member == 1) else None)
            def mkg(tpq=tpq, rot=rot, yt=ytile_next, gt=gtile):
                def fn(eng):
                    tp0 = tpq * 2
                    src = bass.AP(pyp[rot], 0, [[256, 128], [128, 2], [1, 128]])
                    dstv = bass.AP(yt, tp0, [[4096, 128], [1, 2], [32, 128]])
                    gv = bass.AP(gt, tp0, [[4096, 128], [1, 2], [32, 128]])
                    return eng.tensor_mul(dstv, src, gv)
                return fn
            gw = ["%s_i2_%d" % (name, tpq)]
            if tpq == 0 and gate_ready:
                gw += list(gate_ready)
            emit("dve", mkg(), gw, "%s_ye_%d" % (name, tpq))
        EV["%s_done" % name] = ev_now("dve")

    proj_last_mm = {}
    proj_last_ev = {}

    def emit_proj(c, sp, dst0, dst1, mm_waits, evac_waits):
        wbuf = (c * 2 + sp) % 2
        for kc in range(NKC):
            def mkd(c=c, sp=sp, kc=kc, wbuf=wbuf):
                def fn(eng):
                    cols = bass.AP(wp.tensor, (kc * 128) * 4096 + (sp * 2) * 1024 + c * 128,
                                   [[4096, 128], [1024, 2], [1, 128]])
                    dst = bass.AP(wps[wbuf], kc * 256, [[2048, 128], [128, 2], [1, 128]])
                    return eng.dma_start(dst, cols)
                return fn
            w = []
            if kc == 0:
                prev = proj_last_mm.get((c * 2 + sp) - 2)
                if prev:
                    w.append(prev)
            emit("pool", mkd(), w, "wpl_%d_%d" % (c, sp) if kc == NKC - 1 else None,
                 n=16, fam="wps%d" % wbuf)
        for q in range(16):
            rot = q % 2
            for si in range(2):
                u32 = q * 2 + si
                for kc in range(NKC):
                    def mk(u32=u32, kc=kc, si=si, rot=rot, wbuf=wbuf):
                        def fn(eng):
                            return eng.matmul(
                                pp[rot][:, si * 256:si * 256 + 256],
                                bass.AP(xt, kc * 4096 + u32 * 128,
                                        [[NKC * 4096, 128], [1, 128]]),
                                bass.AP(wps[wbuf], kc * 256, [[2048, 128], [1, 256]]),
                                start=(kc == 0), stop=False)
                        return fn
                    w = []
                    if kc == 0 and si == 0:
                        if q == 0:
                            w = list(mm_waits) + ["wpl_%d_%d" % (c, sp), "xt_all", "ld_bp"]
                            prev = proj_last_ev.get((c * 2 + sp) - 1)
                            if prev:
                                w += list(prev)
                            else:
                                w.append("fpe_63")
                        if q >= 2:
                            w.append("pq_%d_%d_%d_0" % (c, sp, q - 2))
                            w.append("pq_%d_%d_%d_1" % (c, sp, q - 2))
                    emit("pe", mk(), w, None)
                def mkb(si=si, rot=rot, c=c, sp=sp):
                    def fn(eng):
                        bcols = bass.AP(m128, 64 * 4096 + (sp * 2) * 1024 + c * 128,
                                        [[4096, 1], [1024, 2], [1, 128]])
                        return eng.matmul(pp[rot][:, si * 256:si * 256 + 256],
                                          onesb[64:65, 0:128], bcols,
                                          start=False, stop=True)
                    return fn
                emit("pe", mkb(), [], "pj_%d_%d_%d" % (c, sp, q) if si == 1 else None)
            for s_i, dstt in ((0, dst0), (1, dst1)):
                def mke(q=q, s_i=s_i, dstt=dstt, rot=rot, h=None):
                    def fn(eng):
                        if h is None:
                            dst = bass.AP(dstt, q * 2, [[4096, 128], [1, 2], [32, 128]])
                            src = bass.AP(pp[rot], s_i * 128,
                                          [[512, 128], [256, 2], [1, 128]])
                            return eng.activation(dst, src, AF.Identity)
                        dst = bass.AP(dstt, q * 2 + h, [[4096, 128], [32, 128]])
                        src = bass.AP(pp[rot], s_i * 128 + h * 256,
                                      [[512, 128], [1, 128]])
                        return eng.tensor_copy(dst, src)
                    return fn
                w = ["pj_%d_%d_%d" % (c, sp, q)]
                if q == 0:
                    w += list(evac_waits)
                if s_i == 0 or "p" not in EVD:
                    emit("act", mke(), w, "pq_%d_%d_%d_%d" % (c, sp, q, s_i))
                else:
                    emit("dve", mke(h=0), w, None)
                    emit("dve", mke(h=1), [], "pq_%d_%d_%d_%d" % (c, sp, q, s_i))
        proj_last_mm[c * 2 + sp] = "pj_%d_%d_15" % (c, sp)
        proj_last_ev[c * 2 + sp] = ("pq_%d_%d_15_0" % (c, sp),
                                    "pq_%d_%d_15_1" % (c, sp))
        EV["proj_%d_%d" % (c, sp)] = ev_now("act")
        EV["projd_%d_%d" % (c, sp)] = ev_now("dve")

    for s in range(3):
        l1w = ["fpe_63"]
        if s > 0:
            l1w.append("fsl_%d" % (s - 1))
        emit_fwd("f%d" % s, tile[s], l1w, True)
        dma("sp", f_slice[s * 128:(s + 1) * 128, :], wsb.ap(),
            ["f%d_zdone" % s, "f%d_zdoned" % s], "fsl_%d" % s, fam="fsl")
        emit("pool", (lambda s=s: (lambda eng: eng.collective_compute(
            "AllGather", ALU.bypass, replica_groups=[list(range(NCORE))],
            ins=[f_slice[s * 128:(s + 1) * 128, :]], outs=[f_full[s]])))(),
            ["fsl_%d" % s], "cc_done%d" % s, n=16)
    if "fsl" in dbg_t:
        dma("pool", dbg_t["fsl"], f_slice, ["cc_done%d" % 2], "d_fsl", fam="dbgfsl")

    for c in range(NC):
        evw = ["f2_l1_15"] if c == 0 else ["y3s_%d" % (c - 1)]
        emit_proj(c, 0, tile[0], tile[1], [], evw)
        ytiles = [tile[0], tile[0], tile[1], tile[0]]
        gtiles = [None, tile[1], tile[2], tile[3]]
        for s in range(3):
            nm = "m%d_%d" % (c, s)
            if s == 0:
                l1w = ["proj_%d_0" % c, "projd_%d_0" % c]
                if c == 0:
                    l1w.append("f2_zdone")
                    l1w.append("f2_zdoned")
                else:
                    l1w.append("m%d_2_done" % (c - 1))
                    l1w.append("m%d_2_b0_15" % (c - 1))
            else:
                l1w = ["m%d_%d_done" % (c, s - 1), "m%d_%d_b0_15" % (c, s - 1)]
            emit_fwd(nm, ytiles[s], l1w, False)
            fw0 = []
            if c == 0:
                fw0.append("cc_done%d" % s)
                if s == 0:
                    fw0.append("xt_all")
            emit_filter(nm, c, s, fw0)
            emit_inv(nm, ytiles[s + 1], gtiles[s + 1],
                     gate_ready=[("proj_%d_0" % c, "projd_%d_0" % c),
                                 ("proj_%d_1" % c, "projd_%d_1" % c)][0 if s == 0 else 1])
            if s == 0:
                ew = ["bc_1_7"] if c == 0 else [
                    "m%d_1_ye_15" % (c - 1), "m%d_2_ye_15" % (c - 1)]
                emit_proj(c, 1, tile[2], tile[3], ["m%d_0_ae_15" % c], ew)
        def mky3(c=c):
            def fn(eng):
                return eng.dma_start(y3_dram[c * 128:(c + 1) * 128, :], tile[0].ap())
            return fn
        emit("sp", mky3(), ["m%d_2_done" % c], "y3s_%d" % c, n=16, fam="y3s")
    if "y3" in dbg_t:
        dma("pool", dbg_t["y3"], tile[0].ap(), ["y3s_%d" % (NC - 1)], "d_y3", fam="dbgy3")

    FDN = "m%d_2_done" % (NC - 1)
    emit("dve", lambda eng: eng.memset(st12[:, :], 0), [FDN], "stz")
    for c in range(NC):
        w = ["proj_%d_1" % (NC - 1), "projd_%d_1" % (NC - 1)]
        if c == 0:
            w.append("y3s_%d" % (NC - 1))
        w.append(FDN if c == 0 else "stp_%d" % (c - 1))
        dma("sp", xv[0][:, :], x_til[c * 128:(c + 1) * 128, :], w, "ltx_%d" % c,
            fam="xa")
        def mkyl(c=c):
            def fn(eng):
                return eng.dma_start(bass.AP(wsb, 0, [[8192, 128], [1, 4096]]),
                                     y3_dram[c * 128:(c + 1) * 128, :])
            return fn
        wy = ["m%d_2_i1_15" % (NC - 1), "m%d_2_fi_7" % (NC - 1),
              "y3s_%d" % (NC - 1)] if c == 0 else ["x1_%d" % (c - 1)]
        emit("sp", mkyl(), wy, "lty_%d" % c, n=16, fam="y3l")
        def mkx1(c=c):
            def fn(eng):
                dst = bass.AP(xt, c * 4096, [[NKC * 4096, 128], [1, 4096]])
                return eng.tensor_add(dst, xv[0][:, :],
                                      bass.AP(wsb, 0, [[8192, 128], [1, 4096]]))
            return fn
        wx1 = ["ltx_%d" % c, "lty_%d" % c, "stz"]
        if c == 0 and "xtilde" in dbg_t:
            wx1.append("d_xt")
        emit("dve", mkx1(), wx1, "x1_%d" % c)
        emit("dve", (lambda c=c: (lambda eng: eng.tensor_reduce(
            m128[:, 0:32], bass.AP(xt, c * 4096, [[NKC * 4096, 128], [1, 32], [32, 128]]),
            AX.X, ALU.add)))(), [], None, serial=True)
        emit("dve", lambda eng: eng.tensor_add(st12[:, 0:32], st12[:, 0:32],
                                               m128[:, 0:32]), [], None, serial=True)
        emit("act", (lambda c=c: (lambda eng: eng.activation(
            r128[:, :], bass.AP(xt, c * 4096, [[NKC * 4096, 128], [1, 4096]]),
            AF.Square)))(), ["x1_%d" % c, "m%d_2_fi_7" % (NC - 1)], "sq2_%d" % c)
        emit("dve", (lambda: (lambda eng: eng.tensor_reduce(
            m128[:, 32:64], bass.AP(r128, 0, [[4096, 128], [1, 32], [32, 128]]),
            AX.X, ALU.add)))(), ["sq2_%d" % c], None, serial=True)
        emit("dve", (lambda: (lambda eng: eng.tensor_add(
            st12[:, 32:64], st12[:, 32:64], m128[:, 32:64])))(), [], "stp_%d" % c,
            serial=True)
    if "x1c0" in dbg_t:
        dma("pool", dbg_t["x1c0"], bass.AP(xt, 0, [[NKC * 4096, 128], [1, 4096]]),
            ["stp_%d" % (NC - 1)], "d_x1", fam="dbgx1")
    emit("dve", lambda eng: eng.tensor_scalar_mul(st12[:, 64:96], st12[:, 0:32],
                                                  -1.0 / 1024.0),
         ["stp_%d" % (NC - 1)], None, serial=True)
    emit("dve", lambda eng: eng.tensor_scalar_mul(st12[:, 96:128], st12[:, 32:64],
                                                  1.0 / 1024.0), [], None, serial=True)
    emit("dve", lambda eng: eng.tensor_mul(st12[:, 0:32], st12[:, 64:96],
                                           st12[:, 64:96]), [], None, serial=True)
    emit("dve", lambda eng: eng.tensor_sub(st12[:, 96:128], st12[:, 96:128],
                                           st12[:, 0:32]), [], "var2", serial=True)
    emit("act", lambda eng: eng.activation(st12[:, 96:128], st12[:, 96:128],
                                           AF.Sqrt, bias=epsc[:, 0:1]), ["var2"], "sd2")
    emit("dve", lambda eng: eng.reciprocal(st12[:, 96:128], st12[:, 96:128]),
         ["sd2"], "mr2")

    for c in range(NC):
        for u32 in range(32):
            def mkts(c=c, u32=u32):
                def fn(eng):
                    src = bass.AP(xt, c * 4096 + u32, [[NKC * 4096, 128], [32, 128]])
                    dst = bass.AP(wsb, 4096 + u32, [[8192, 128], [32, 128]])
                    return eng.tensor_scalar(dst, src,
                                             st12[:, 64 + u32:65 + u32],
                                             st12[:, 96 + u32:97 + u32],
                                             ALU.add, ALU.mult)
                return fn
            w = []
            if u32 == 0:
                w = ["mr2"] if c == 0 else ["grd_%d" % (c - 1)]
                if c == 0:
                    w.append("lty_%d" % (NC - 1))
            emit("dve", mkts(), w, None)
        emit("dve", lambda eng: eng.tensor_reduce(
            m128[:, 0:256],
            bass.AP(wsb, 4096, [[8192, 128], [32, 128], [16, 2], [1, 16]]),
            AX.X, ALU.add), [], None, serial=True)
        emit("dve", lambda eng: eng.tensor_scalar_mul(
            m128[:, 0:256], m128[:, 0:256], 1.0 / 16.0), [], "grd_%d" % c,
            serial=True)
        ltw = ["ltx_%d" % (NC - 1), "x1_%d" % (NC - 1)] if c == 0 \
            else ["gt_%d_0" % (c - 1), "gt_%d_1" % (c - 1)]
        emit("dve", (lambda c=c: (lambda eng: eng.tensor_copy(
            xv[0][:, 0:256], m128[:, 0:256])))(), ltw, "gcp_%d" % c, serial=True)
        for gh in range(2):
            def mkt(c=c, gh=gh):
                def fn(eng):
                    return eng.transpose(pp[gh][:, 0:128], xv[0][:, gh:256:2],
                                         idents[:, :])
                return fn
            w = ["gcp_%d" % c, "ld_id"]
            if c >= 1:
                w.append("ge_%d_%d" % (c - 1, gh))
            else:
                w.append("proj_%d_1" % (NC - 1))
                w.append("projd_%d_1" % (NC - 1))
            emit("pe", mkt(), w, "gt_%d_%d" % (c, gh))
            def mkte(c=c, gh=gh):
                def fn(eng):
                    dst = bass.AP(w2s, c * 256 + gh * 64,
                                  [[16384, 128], [32, 2], [128, 2], [1, 32]])
                    src = bass.AP(pp[gh], 0, [[512, 128], [1, 128]])
                    return eng.activation(dst, src, AF.Identity)
                return fn
            emit("act", mkte(), ["gt_%d_%d" % (c, gh)], "ge_%d_%d" % (c, gh))
    EV["groups_done"] = EV["ge_%d_1" % (NC - 1)]
    if "groups" in dbg_t:
        dma("pool", dbg_t["groups"], bass.AP(w2s, 0, [[16384, 128], [1, 2048]]),
            ["groups_done"], "d_gr", fam="dbggr")

    wstream_prev = ["groups_done"]

    def stream_w_mms(tag, w_dram, psum_of, evac_name, rhs_fn, emit_evacs):
        for half in range(2):
            for kc in range(NKC):
                wbuf = kc % 2
                if kc >= 2:
                    wld = ["%s_mm_%d_%d" % (tag, half, kc - 2)]
                elif half > 0:
                    wld = ["%s_mm_%d_%d" % (tag, half - 1, NKC - 2 + kc)]
                else:
                    wld = list(wstream_prev)
                def mkd(kc=kc, half=half, wbuf=wbuf, w_dram=w_dram):
                    def fn(eng):
                        return eng.dma_start(
                            bass.AP(wps[wbuf], 0, [[2048, 128], [1, 512]]),
                            w_dram[kc * 128:(kc + 1) * 128, half * 512:(half + 1) * 512])
                    return fn
                emit("pool", mkd(), wld, "%s_ld_%d_%d" % (tag, half, kc), n=16,
                     fam="wps%d" % wbuf)
                for dc4 in range(4):
                    def mkm(kc=kc, half=half, dc4=dc4, wbuf=wbuf):
                        def fn(eng):
                            return eng.matmul(
                                psum_of(half, dc4),
                                bass.AP(wps[wbuf], dc4 * 128, [[2048, 128], [1, 128]]),
                                rhs_fn(kc),
                                start=(kc == 0), stop=(kc == NKC - 1))
                        return fn
                    w = []
                    if dc4 == 0:
                        w = ["%s_ld_%d_%d" % (tag, half, kc)]
                        if kc == 0 and half > 0:
                            w += [evac_name(half - 1, j) for j in range(4)]
                    emit("pe", mkm(), w,
                         "%s_mm_%d_%d" % (tag, half, kc) if dc4 == 3 else None)
            emit_evacs(half)
        wstream_prev[:] = [evac_name(1, j) for j in range(4)]

    PS4 = [pa[0], pa[1], px[0], px[1]]
    qkoff = {0: 2048, 1: 4096}
    for wi, wdr in ((0, wq_d), (1, wk_d)):
        def psof(half, dc4):
            return PS4[dc4][:, 0:256]
        evn = (lambda wi: (lambda half, dc4: "qk%d_ev_%d_%d" % (wi, half, dc4)))(wi)
        def emit_evacs(half, wi=wi):
            for dc4 in range(4):
                dc = half * 4 + dc4
                def mke(wi=wi, dc=dc, dc4=dc4):
                    def fn(eng):
                        sc = 0.125 if wi == 0 else 1.0
                        dst = bass.AP(w2s, qkoff[wi] + dc * 256, [[16384, 128], [1, 256]])
                        return eng.activation(
                            dst, PS4[dc4][:, 0:256],
                            AF.Identity, scale=sc)
                    return fn
                emit("act", mke(), ["qk%d_mm_%d_%d" % (wi, half, NKC - 1)],
                     "qk%d_ev_%d_%d" % (wi, half, dc4))
        stream_w_mms("qk%d" % wi, wdr, psof, evn,
                     lambda kc: bass.AP(w2s, kc * 256, [[16384, 128], [1, 256]]),
                     emit_evacs)
    for sb_ in range(2):
        for half in range(2):
            for kc in range(NKC):
                wbuf = kc % 2
                if kc >= 2:
                    wld = ["v_mm_%d_%d_%d" % (sb_, half, kc - 2)]
                elif (sb_, half) != (0, 0):
                    ph = (sb_, half - 1) if half > 0 else (sb_ - 1, 1)
                    wld = ["v_mm_%d_%d_%d" % (ph[0], ph[1], NKC - 2 + kc)]
                else:
                    wld = ["qk1_mm_1_%d" % (NKC - 2 + kc)]
                def mkd(kc=kc, half=half, wbuf=wbuf):
                    def fn(eng):
                        return eng.dma_start(
                            bass.AP(wps[wbuf], 0, [[2048, 128], [1, 512]]),
                            wv_d[kc * 128:(kc + 1) * 128, half * 512:(half + 1) * 512])
                    return fn
                emit("pool", mkd(), wld, "v_ld_%d_%d_%d" % (sb_, half, kc), n=16,
                     fam="wps%d" % wbuf)
                def mkm(kc=kc, sb_=sb_, half=half, wbuf=wbuf):
                    def fn(eng):
                        return eng.matmul(
                            px[half][:, 0:512],
                            bass.AP(w2s, kc * 256 + sb_ * 128, [[16384, 128], [1, 128]]),
                            bass.AP(wps[wbuf], 0, [[2048, 128], [1, 512]]),
                            start=(kc == 0), stop=(kc == NKC - 1))
                    return fn
                vw = ["v_ld_%d_%d_%d" % (sb_, half, kc)]
                if kc == 0 and sb_ > 0:
                    vw.append("v_ev_%d_%d" % (sb_ - 1, half))
                if kc == 0 and (sb_, half) == (0, 0):
                    vw += ["qk1_ev_1_%d" % j for j in range(4)]
                emit("pe", mkm(), vw, "v_mm_%d_%d_%d" % (sb_, half, kc))
            def mkev(sb_=sb_, half=half):
                def fn(eng):
                    dst = bass.AP(w2s, 6144 + sb_ * 1024 + half * 512,
                                  [[16384, 128], [1, 512]])
                    return eng.activation(dst, px[half][:, 0:512], AF.Identity)
                return fn
            emit("act", mkev(), ["v_mm_%d_%d_%d" % (sb_, half, NKC - 1)],
                 "v_ev_%d_%d" % (sb_, half))
    for kc in range(NKC):
        wbuf = kc % 2
        wld = ["v_mm_1_1_%d" % (NKC - 2 + kc)] if kc < 2 else ["h_mm_%d" % (kc - 2)]
        def mkd(kc=kc, wbuf=wbuf):
            def fn(eng):
                return eng.dma_start(bass.AP(wps[wbuf], 0, [[2048, 128], [1, 256]]),
                                     ws1_d[kc * 128:(kc + 1) * 128, :])
            return fn
        emit("pool", mkd(), wld, "h_ld_%d" % kc, n=16, fam="wps%d" % wbuf)
        for hf in range(2):
            def mkm(kc=kc, hf=hf, wbuf=wbuf):
                def fn(eng):
                    return eng.matmul(
                        pyp[hf][:, 0:256],
                        bass.AP(wps[wbuf], hf * 128, [[2048, 128], [1, 128]]),
                        bass.AP(w2s, kc * 256, [[16384, 128], [1, 256]]),
                        start=(kc == 0), stop=False)
                return fn
            w = ["h_ld_%d" % kc]
            if kc == 0 and hf == 0:
                w += ["m%d_2_ye_15" % (NC - 1)]
            emit("pe", mkm(), w if hf == 0 else [],
                 "h_mm_%d" % kc if hf == 1 else None)
    dma("pool", rowf[0:1, 0:256], bs1_d, ["fpe_63"], "ld_bs1", fam="bs1")
    dma("pool", rowf[0:1, 256:257], bs2_d, [], "ld_bs2", fam="bs2")
    for hf in range(2):
        def mkhb(hf=hf):
            def fn(eng):
                return eng.matmul(pyp[hf][:, 0:256],
                                  rowf[0:1, hf * 128:hf * 128 + 128],
                                  onesb[0:1, 0:256], start=False, stop=True)
            return fn
        emit("pe", mkhb(), ["ld_bs1", "h_mm_%d" % (NKC - 1)] if hf == 0 else [],
             "hb_%d" % hf)
        def mkhe(hf=hf):
            def fn(eng):
                return eng.activation(bass.AP(wi2s, hf * 256, [[4096, 128], [1, 256]]),
                                      pyp[hf][:, 0:256], AF.Relu)
            return fn
        emit("act", mkhe(), ["hb_%d" % hf], "hev_%d" % hf)
    for kc in range(2):
        def mks(kc=kc):
            def fn(eng):
                return eng.matmul(pp[0][0:1, 0:256], ws2s[:, kc:kc + 1],
                                  bass.AP(wi2s, kc * 256, [[4096, 128], [1, 256]]),
                                  start=(kc == 0), stop=(kc == 1))
            return fn
        emit("pe", mks(), ["hev_0", "hev_1", "ld_ws2"] if kc == 0 else [],
             "smm" if kc == 1 else None)
    scrow = lambda a, n: xv[0][0:1, 1024 + a:1024 + a + n]
    emit("act", lambda eng: eng.activation(scrow(0, 256), pp[0][0:1, 0:256],
                                           AF.Identity, bias=rowf[0:1, 256:257]),
         ["smm", "ld_bs2", "gcp_%d" % (NC - 1)], "sc_done")
    if "scores" in dbg_t:
        dma("pool", dbg_t["scores"], scrow(0, 256), ["sc_done"], "d_sc", fam="dbgsc")

    XA = lambda a, n=1: xv[0][:, 2048 + a:2048 + a + n]
    for b in range(2):
        def mksc(b=b):
            def fn(eng):
                return eng.matmul(pp[1][:, 0:1], scrow(b * 128, 128),
                                  ones32[0:1, 0:1], start=True, stop=True)
            return fn
        emit("pe", mksc(), ["sc_done"], "scol_%d" % b)
        emit("act", (lambda b=b: (lambda eng: eng.activation(
            XA(8 + b), pp[1][:, 0:1], AF.Identity)))(), ["scol_%d" % b], "scv_%d" % b)
        def mksr(b=b):
            def fn(eng):
                return eng.matmul(pp[1][:, 128:256], ones32[0:1, 0:128],
                                  scrow(b * 128, 128), start=True, stop=True)
            return fn
        emit("pe", mksr(), ["scv_%d" % b], "srow_%d" % b)
        emit("act", (lambda b=b: (lambda eng: eng.activation(
            XA(256, 128), pp[1][:, 128:256], AF.Identity)))(),
            ["srow_%d" % b], "srv_%d" % b)
        emit("dve", (lambda b=b: (lambda eng: eng.tensor_tensor(
            XA(384, 128), XA(256, 128),
            XA(8 + b, 1).broadcast_to([128, 128]), ALU.is_gt)))(),
            ["srv_%d" % b], None)
        emit("dve", (lambda b=b: (lambda eng: eng.tensor_reduce(
            XA(10 + b), XA(384, 128), AX.X, ALU.add)))(), [], None, serial=True)
        emit("dve", (lambda b=b: (lambda eng: eng.tensor_single_scalar(
            XA(0 + b), XA(10 + b), 63.5, ALU.is_lt)))(), [], "msk_%d" % b, serial=True)
        emit("dve", (lambda b=b: (lambda eng: eng.tensor_scalar(
            XA(2 + b), XA(0 + b), -1.0, 1e9, ALU.add, ALU.mult)))(), [], "pen_%d" % b,
            serial=True)
        def mkpt(b=b):
            def fn(eng):
                return eng.transpose(pp[1][0:1, 256:384], XA(2 + b), idents[:, :])
            return fn
        emit("pe", mkpt(), ["pen_%d" % b], "pt_%d" % b)
        emit("act", (lambda b=b: (lambda eng: eng.activation(
            scrow(256 + b * 128, 128), pp[1][0:1, 256:384], AF.Identity)))(),
            ["pt_%d" % b], "penr_%d" % b)
    if "mask" in dbg_t:
        dma("pool", dbg_t["mask"], xv[0][:, 2048:2050], ["msk_0", "msk_1"], "d_mk", fam="dbgmk")

    for b in range(2):
        for h in range(NH):
            hc, hb = h // 2, (h % 2) * 64
            nm = "at_%d_%d" % (b, h)
            qcol = 2048 + hc * 256 + b * 128
            kcol = 4096 + hc * 256 + b * 128
            if b == 0 and h == 0:
                w = ["v_ev_1_1", "penr_0", "penr_1", "qk0_ev_1_3", "qk1_ev_1_3"]
            else:
                pb, ph_ = (b, h - 1) if h > 0 else (b - 1, NH - 1)
                w = ["at_%d_%d_ctx" % (pb, ph_)]
            def mks(hb=hb, qcol=qcol, kcol=kcol):
                def fn(eng):
                    return eng.matmul(
                        px[0][:, 0:128],
                        bass.AP(w2s, hb * 16384 + qcol, [[16384, 64], [1, 128]]),
                        bass.AP(w2s, hb * 16384 + kcol, [[16384, 64], [1, 128]]),
                        start=True, stop=False, tile_position=(hb, 0))
                return fn
            emit("pe", mks(), w, None)
            def mkp(b=b):
                def fn(eng):
                    return eng.matmul(px[0][:, 0:128], ones32[0:1, 0:128],
                                      scrow(256 + b * 128, 128), start=False, stop=True)
                return fn
            emit("pe", mkp(), [], "%s_s" % nm)
            emit("dve", (lambda nm=nm: (lambda eng: eng.tensor_reduce(
                XA(4), px[0][:, 0:128], AX.X, ALU.max, negate=True)))(),
                ["%s_s" % nm], "%s_nm" % nm)
            def mknt(nm=nm):
                def fn(eng):
                    return eng.transpose(pp[1][0:1, 384:512], XA(4), idents[:, :])
                return fn
            emit("pe", mknt(), ["%s_nm" % nm], "%s_nt" % nm)
            emit("act", (lambda nm=nm: (lambda eng: eng.activation(
                scrow(640, 128), pp[1][0:1, 384:512], AF.Identity)))(),
                ["%s_nt" % nm], "%s_nr" % nm)
            def mkst(hb=hb, qcol=qcol, kcol=kcol):
                def fn(eng):
                    return eng.matmul(
                        px[1][:, 0:128],
                        bass.AP(w2s, hb * 16384 + kcol, [[16384, 64], [1, 128]]),
                        bass.AP(w2s, hb * 16384 + qcol, [[16384, 64], [1, 128]]),
                        start=True, stop=False, tile_position=(hb, 0))
                return fn
            emit("pe", mkst(), ["%s_nr" % nm], None)
            def mkst2(nm=nm):
                def fn(eng):
                    return eng.matmul(px[1][:, 0:128], ones32[0:1, 0:128],
                                      scrow(640, 128), start=False, stop=True)
                return fn
            emit("pe", mkst2(), [], "%s_st" % nm)
            def mkexp(nm=nm, b=b):
                def fn(eng):
                    return eng.activation(
                        bass.AP(wi2s, 512, [[4096, 128], [1, 128]]),
                        px[1][:, 0:128], AF.Exp, bias=XA(2 + b))
                return fn
            emit("act", mkexp(), ["%s_st" % nm], "%s_e" % nm)
            def mksum(nm=nm):
                def fn(eng):
                    return eng.matmul(pp[1][:, 0:1],
                                      bass.AP(wi2s, 512, [[4096, 128], [1, 128]]),
                                      onesb[:, 0:1], start=True, stop=True)
                return fn
            emit("pe", mksum(), ["%s_e" % nm], "%s_sum" % nm)
            emit("dve", (lambda: (lambda eng: eng.reciprocal(
                XA(5), pp[1][:, 0:1])))(), ["%s_sum" % nm], None)
            emit("dve", (lambda b=b: (lambda eng: eng.tensor_mul(
                XA(6), XA(5), XA(0 + b))))(), [], "%s_rs" % nm, serial=True)
            def mkc(b=b, h=h):
                def fn(eng):
                    return eng.matmul(
                        pp[0][:, 256:320],
                        bass.AP(wi2s, 512, [[4096, 128], [1, 128]]),
                        bass.AP(w2s, 6144 + b * 1024 + h * 64, [[16384, 128], [1, 64]]),
                        start=True, stop=True)
                return fn
            emit("pe", mkc(), ["%s_rs" % nm], "%s_cm" % nm)
            def mkce(b=b, h=h, nm=nm):
                def fn(eng):
                    return eng.activation(
                        xv[1][:, b * 1024 + h * 64:b * 1024 + h * 64 + 64],
                        pp[0][:, 256:320], AF.Identity, scale=XA(6))
                return fn
            cw = ["%s_cm" % nm]
            if b == 0 and h == 0:
                cw.append("grd_%d" % (NC - 1))
            emit("act", mkce(), cw, "%s_ctx" % nm)
    for b in range(2):
        for dc in range(8):
            def mkt(b=b, dc=dc):
                def fn(eng):
                    return eng.transpose(
                        pp[dc % 2][:, 128:256],
                        xv[1][:, b * 1024 + dc * 128:b * 1024 + dc * 128 + 128],
                        idents[:, :])
                return fn
            w = ["at_%d_%d_ctx" % (b, NH - 1)] if dc == 0 else []
            if dc >= 2:
                w.append("ctf_%d_%d" % (b, dc - 2))
            elif b == 1:
                w.append("ctf_0_%d" % (6 + dc))
            emit("pe", mkt(), w, "ctt_%d_%d" % (b, dc))
            def mkte(b=b, dc=dc):
                def fn(eng):
                    dst = bass.AP(w2s, 8192 + dc * 256 + b * 128, [[16384, 128], [1, 128]])
                    return eng.activation(dst, pp[dc % 2][:, 128:256], AF.Identity)
                return fn
            emit("act", mkte(), ["ctt_%d_%d" % (b, dc)], "ctf_%d_%d" % (b, dc))
    EV["ctxfm_done"] = ev_now("act")
    def wo_psum(half, dc4):
        return PS4[dc4][:, 0:256]
    def wo_emit_evacs(half):
        for dc4 in range(4):
            dc = half * 4 + dc4
            def fn_(eng, dc=dc, dc4=dc4):
                dst = bass.AP(w2s, 10240 + dc * 256, [[16384, 128], [1, 256]])
                return eng.activation(dst, wo_psum(0, dc4), AF.Identity)
            emit("act", fn_, ["wo_mm_%d_%d" % (half, NKC - 1)],
                 "wo_ev_%d_%d" % (half, dc4))
    wstream_prev[:] = ["ctxfm_done"]
    stream_w_mms("wo", wo_d, wo_psum, (lambda half, dc4: "wo_ev_%d_%d" % (half, dc4)),
                 lambda kc: bass.AP(w2s, 8192 + kc * 256, [[16384, 128], [1, 256]]),
                 wo_emit_evacs)
    if "outg" in dbg_t:
        dma("pool", dbg_t["outg"], bass.AP(w2s, 10240, [[16384, 128], [1, 2048]]),
            ["wo_ev_1_3"], "d_og", fam="dbgog")
    for c in range(NC):
        for gh in range(2):
            def mkcp(c=c, gh=gh):
                def fn(eng):
                    src = bass.AP(w2s, 10240 + c * 256 + gh * 64,
                                  [[16384, 128], [32, 2], [128, 2], [1, 32]])
                    return eng.tensor_copy(xv[1][:, 2048 + gh * 128:2048 + gh * 128 + 128],
                                           src)
                return fn
            w = ["wo_ev_1_3"] if (c == 0 and gh == 0) else []
            if c >= 1 and gh == 0:
                w = ["og_%d_1" % (c - 1)]
            emit("dve", mkcp(), w, "ocp_%d_%d" % (c, gh))
            def mkt(c=c, gh=gh):
                def fn(eng):
                    return eng.transpose(pp[gh][:, 128:256],
                                         xv[1][:, 2048 + gh * 128:2048 + gh * 128 + 128],
                                         idents[:, :])
                return fn
            w = ["ocp_%d_%d" % (c, gh)]
            if c >= 1:
                w.append("og_%d_%d" % (c - 1, gh))
            emit("pe", mkt(), w, "ogt_%d_%d" % (c, gh))
            def mkte(c=c, gh=gh):
                def fn(eng):
                    dst = bass.AP(w2s, 12288 + c * 256 + gh, [[16384, 128], [2, 128]])
                    return eng.activation(dst, pp[gh][:, 128:256], AF.Identity)
                return fn
            emit("act", mkte(), ["ogt_%d_%d" % (c, gh)], "og_%d_%d" % (c, gh))
    ovw = xv[0].rearrange("p (dd gh u) -> p dd gh u", gh=2, u=16)
    for c in range(NC):
        for u16 in range(16):
            def mko(c=c, u16=u16):
                def fn(eng):
                    x1v = bass.AP(xt, c * 4096 + u16, [[NKC * 4096, 128], [32, 128], [16, 2]])
                    ogv = bass.AP(w2s, 12288 + c * 256, [[16384, 128], [2, 128], [1, 2]])
                    dst = ovw[:, :, :, u16]
                    return eng.tensor_add(dst, x1v, ogv)
                return fn
            w = []
            if u16 == 0:
                w = ["og_%d_0" % c, "og_%d_1" % c]
                if c == 0:
                    w += ["msk_0", "msk_1", "at_1_%d_ctx" % (NH - 1), "sc_done",
                          "penr_0", "penr_1"]
                    if "scores" in dbg_t:
                        w.append("d_sc")
                    if "mask" in dbg_t:
                        w.append("d_mk")
                if c >= 1:
                    w.append("ost_%d" % (c - 1))
            emit("dve", mko(), w, "oadd_%d" % c if u16 == 15 else None)
        def mkst(c=c):
            def fn(eng):
                return eng.dma_start(out_t[c * 128:(c + 1) * 128, :], xv[0][:, :])
            return fn
        emit("sp", mkst(), ["oadd_%d" % c], "ost_%d" % c, n=16, fam="out")
    drain = ["ost_%d" % (NC - 1)]
    for s in list(cnt):
        if s.startswith("d_dbg"):
            drain.append((s, cnt[s]))
    emit("sp", lambda eng: eng.nop(), drain, None)
    for e in ("pe", "act", "dve", "pool"):
        emit(e, lambda eng: eng.nop(), drain, None)

    def check_sync():
        ptr = {e: 0 for e in streams}
        eclk = {e: {} for e in streams}
        prod_clk = {}
        made = {}
        dma_info = []
        wait_reg = []
        progress = True
        nleft = sum(len(s) for s in streams.values())
        while nleft and progress:
            progress = False
            for e in streams:
                while ptr[e] < len(streams[e]):
                    fn, w, rec, inc = streams[e][ptr[e]]
                    ok = True
                    for (s, v) in w:
                        vs = made.get(s, [])
                        if not vs or vs[-1] < v:
                            ok = False
                            break
                    if not ok:
                        break
                    clk = eclk[e]
                    for (s, v) in w:
                        import bisect
                        vs = made[s]
                        pv = vs[bisect.bisect_left(vs, v)]
                        for k2, v2 in prod_clk[(s, pv)].items():
                            if clk.get(k2, 0) < v2:
                                clk[k2] = v2
                        if clk.get(s, 0) < v:
                            clk[s] = v
                        wait_reg.append((s, v, e, ptr[e]))
                    if rec is not None:
                        s, n_ = rec
                        if s.startswith("d_") or s.startswith("s_cc"):
                            dma_info.append((s, EV[inc][1] if inc else None,
                                             clk.get(s, 0), inc, e))
                        val = EV[inc][1] if inc else None
                        if val is None:
                            val = made[s][-1] + n_ if s in made else n_
                        cc = dict(clk)
                        cc[s] = max(cc.get(s, 0), val)
                        prod_clk[(s, val)] = cc
                        made.setdefault(s, []).append(val)
                        clk[s] = max(clk.get(s, 0), val) if not s.startswith("d_") \
                            else clk.get(s, 0)
                    ptr[e] += 1
                    nleft -= 1
                    progress = True
        if nleft:
            stuck = {e: (ptr[e], streams[e][ptr[e]][1] if ptr[e] < len(streams[e])
                         else None) for e in streams if ptr[e] < len(streams[e])}
            raise RuntimeError("check_sync DEADLOCK: %r (made=%r)"
                               % (stuck, {k: v[-3:] for k, v in made.items()}))
        waited = {}
        for (s, v, we, wi) in wait_reg:
            if s.startswith("d_"):
                waited.setdefault(s, set()).add(v)
        bad = []
        for (s, val, wmark, inc, e) in dma_info:
            if s.startswith("s_cc") or val is None:
                continue
            for v in waited.get(s, ()):
                if val > v and wmark < v:
                    bad.append((s, "waited", v, "later dma", inc, "at", val,
                                "issue watermark", wmark))
        if bad:
            raise RuntimeError("check_sync RACES:\n" +
                               "\n".join(str(b) for b in bad[:20]) +
                               "\n(%d total)" % len(bad))

    check_sync()

    semobjs = {}
    for s in cnt:
        semobjs[s] = nc.alloc_semaphore(name=s)

    import contextlib
    with nc.Block() as block, nc.allow_low_precision(reason="bf16 residual stream by design"):
        def mkbody(engname):
            def body(eng):
                for fn, waits, rec, _inc in streams[engname]:
                    for (s, v) in waits:
                        eng.wait_ge(semobjs[s], v)
                    ins = fn(eng)
                    if rec is not None and ins is not None:
                        ins.then_inc(semobjs[rec[0]], rec[1])
            return body
        block.tensor(mkbody("pe"))
        block.scalar(mkbody("act"))
        block.vector(mkbody("dve"))
        block.gpsimd(mkbody("pool"))
        block.sync(mkbody("sp"))

    for h in reversed(ex):
        h.__exit__(None, None, None)
    build_nc.last_cnt = dict(cnt)
    build_nc.last_EV = dict(EV)
    build_nc.last_waits = list(WAITLOG)
    return nc, dbg_t


def stage_inputs(inputs):
    X = np.asarray(inputs["L0_attn_X"])
    Wp = np.ascontiguousarray(inputs["L0_attn_Wp"], np.float32)
    bp = np.ascontiguousarray(inputs["L0_attn_bp"], np.float32).reshape(1, -1)
    Wf = np.ascontiguousarray(inputs["L0_attn_Wf"], np.float32)
    bf_ = np.ascontiguousarray(inputs["L0_attn_bf"], np.float32).reshape(1, -1)
    Wq = np.ascontiguousarray(inputs["L0_ffn_Wq"], np.float32)
    Wk = np.ascontiguousarray(inputs["L0_ffn_Wk"], np.float32)
    Wv = np.ascontiguousarray(inputs["L0_ffn_Wv"], np.float32)
    Wo = np.ascontiguousarray(inputs["L0_ffn_Wo"], np.float32)
    Ws1 = np.ascontiguousarray(inputs["L0_ffn_Ws1"], np.float32)
    bs1 = np.ascontiguousarray(inputs["L0_ffn_bs1"], np.float32).reshape(1, -1)
    Ws2 = np.ascontiguousarray(inputs["L0_ffn_Ws2"], np.float32)
    bs2 = np.ascontiguousarray(inputs["L0_ffn_bs2"], np.float32).reshape(1, 1)

    t = np.arange(T)
    v, u = t // 64, t % 64
    uh, u32 = u // 32, u % 32
    prow = uh * 64 + v
    w1 = build_w1()
    w2 = build_w2()
    wi1 = build_wi1()
    wi2 = build_wi2()
    pos_sc = build_pos_scrambled().astype(bf16)
    ident = np.eye(128, dtype=np.float32)
    ws2t = np.ascontiguousarray(Ws2.reshape(2, 128).T)

    in_maps = []
    cols = u32[:, None] + np.arange(128)[None, :] * 32
    for core in range(NCORE):
        Xc = np.ascontiguousarray(X[2 * core:2 * core + 2], np.float32)
        x_nat = np.ascontiguousarray(Xc.transpose(2, 0, 1).reshape(D, BPC * T))
        x_til = np.zeros((NC * 128, 4096), np.float32)
        for b in range(BPC):
            rows = prow + b * 32
            for c in range(NC):
                x_til[c * 128 + rows[:, None], cols] = Xc[b, :, c * 128:(c + 1) * 128]
        ccols = (np.arange(3)[:, None] * 1024 + core * 128 + np.arange(128)[None, :]).ravel()
        wf_slc = np.ascontiguousarray(Wf[:, ccols])
        bf_slc = np.ascontiguousarray(bf_[:, ccols])
        in_maps.append(dict(
            x_nat=x_nat, x_til=x_til, wp=Wp, bp_d=bp, wf_sl=wf_slc, bf_sl=bf_slc,
            pos_d=pos_sc, wq_d=Wq, wk_d=Wk, wv_d=Wv, wo_d=Wo, ws1_d=Ws1,
            bs1_d=bs1, ws2t_d=ws2t, bs2_d=bs2, w1c=w1, w2c=w2, wi1c=wi1,
            wi2c=wi2, identc=ident))
    return in_maps


def unscramble_out(res_list):
    t = np.arange(T)
    v, u = t // 64, t % 64
    uh, u32 = u // 32, u % 32
    prow = uh * 64 + v
    out = np.empty((2 * NCORE, T, D), np.float32)
    cols = u32[:, None] + np.arange(128)[None, :] * 32
    for core in range(NCORE):
        o = res_list[core]
        for b in range(BPC):
            rows = prow + b * 32
            for c in range(NC):
                out[2 * core + b, :, c * 128:(c + 1) * 128] = o[c * 128 + rows[:, None], cols]
    return out


def kernel(**inputs):
    dbg = tuple(x for x in os.environ.get("MK_DBG", "").split(",") if x)
    nc, dbg_t = build_nc(dbg)
    in_maps = stage_inputs(inputs)
    res = run_bass_kernel_spmd(nc, in_maps, core_ids=list(range(NCORE)))
    outs = [np.asarray(res.results[i]["out_t"]) for i in range(NCORE)]
    if dbg:
        kernel.dbg_results = res.results
    return unscramble_out(outs)


# version 06 (final)
# speedup vs baseline: 1.1934x; 1.0024x over previous
import os
import numpy as np
import ml_dtypes
import concourse.bass as bass
import concourse.mybir as mybir
from concourse.bass_utils import run_bass_kernel_spmd

F32 = mybir.dt.float32
BF16 = mybir.dt.bfloat16
AF = mybir.ActivationFunctionType
ALU = mybir.AluOpType
AX = mybir.AxisListType

T = 2048
D = 1024
NCORE = 8
BPC = 2
NC = 8
NKC = 8
NH = 16

bf16 = ml_dtypes.bfloat16


def build_w1():
    v = np.arange(32)
    km = np.arange(64)
    th = 2 * np.pi * np.outer(v, km) / 64.0
    W = np.zeros((64, 128), np.float64)
    W[0:32, 0:64] = np.cos(th)
    W[32:64, 0:64] = np.sin(th)
    W[0:32, 64:128] = -np.sin(th)
    W[32:64, 64:128] = np.cos(th)
    return np.concatenate([W, W], axis=0).astype(bf16)


def build_w2():
    u = np.arange(64)
    kd = np.arange(64)
    out = np.zeros((64, 64, 2, 2, 64), np.float64)
    for km in range(64):
        phi = 2 * np.pi * np.outer(u, (km + 64 * kd)) / 4096.0
        out[:, km, 0, 0, :] = np.cos(phi)
        out[:, km, 0, 1, :] = -np.sin(phi)
        out[:, km, 1, 0, :] = np.sin(phi)
        out[:, km, 1, 1, :] = np.cos(phi)
    flat = out.reshape(64, 64 * 256)
    return np.concatenate([flat, flat], axis=0).astype(bf16)


def build_wi1():
    kd = np.arange(64)
    tu = np.arange(64)
    psi = 2 * np.pi * np.outer(kd, tu) / 64.0
    W = np.zeros((64, 192), np.float64)
    W[:, 0:64] = np.cos(psi)
    W[:, 64:128] = np.sin(psi)
    W[:, 128:192] = -np.sin(psi)
    return np.concatenate([W, W], axis=0).astype(bf16)


def build_wi2():
    km = np.arange(64)
    tv = np.arange(32)
    out = np.zeros((2, 64, 64, 2, 32), np.float64)
    for tu in range(64):
        chi = 2 * np.pi * np.outer(km, (64 * tv + tu)) / 4096.0
        out[0, :, tu, 0, :] = np.cos(chi)
        out[1, :, tu, 0, :] = -np.sin(chi)
        out[0, :, tu, 1, :] = np.sin(chi)
        out[1, :, tu, 1, :] = np.cos(chi)
    return out.reshape(128, 4096).astype(bf16)


def build_pos_scrambled():
    pos_t = np.arange(T, dtype=np.float64)[:, None]
    i = np.arange(D)[None, :]
    angle = pos_t / np.power(10000.0, (2 * (i // 2)).astype(np.float64) / D)
    pos = np.where(i % 2 == 0, np.sin(angle), np.cos(angle))
    t = np.arange(T)
    v, u = t // 64, t % 64
    uh, u32 = u // 32, u % 32
    f = u32 * 64 + uh * 32 + v
    pos_sc = np.zeros((D, T), np.float32)
    pos_sc[:, f] = pos.T.astype(np.float32)
    return pos_sc


EVD = set(os.environ.get("MK_EVD", "azi"))


def build_nc(dbg=()):
    dbg = set(dbg)
    nc = bass.Bass("TRN2", target_bir_lowering=False)

    def din(name, shape, dtype=F32):
        return nc.dram_tensor(name, shape, dtype, kind="ExternalInput").ap()

    x_nat = din("x_nat", [D, BPC * T])
    x_til = din("x_til", [NC * 128, 4096])
    wp = din("wp", [D, 4 * D])
    bp_d = din("bp_d", [1, 4 * D])
    wf_sl = din("wf_sl", [D, 384])
    bf_sl = din("bf_sl", [1, 384])
    pos_d = din("pos_d", [D, T], BF16)
    wq_d = din("wq_d", [D, D])
    wk_d = din("wk_d", [D, D])
    wv_d = din("wv_d", [D, D])
    wo_d = din("wo_d", [D, D])
    ws1_d = din("ws1_d", [D, 256])
    bs1_d = din("bs1_d", [1, 256])
    ws2t_d = din("ws2t_d", [128, 2])
    bs2_d = din("bs2_d", [1, 1])
    w1c = din("w1c", [128, 128], BF16)
    w2c = din("w2c", [128, 16384], BF16)
    wi1c = din("wi1c", [128, 192], BF16)
    wi2c = din("wi2c", [128, 4096], BF16)
    identc = din("identc", [128, 128], F32)

    out_t = nc.dram_tensor("out_t", [NC * 128, 4096], F32, kind="ExternalOutput").ap()
    f_slice = nc.dram_tensor("f_slice", [384, 8192], BF16).ap()
    f_full = [nc.dram_tensor("f_full%d" % s, [1024, 8192], BF16).ap()
              for s in range(3)]
    y3_dram = nc.dram_tensor("y3_dram", [NC * 128, 4096], BF16).ap()

    dbg_t = {}
    for name, shape in [("xtilde", [128, NKC * 4096]), ("tile0", [128, 4096]),
                        ("tile1", [128, 4096]), ("asb", [128, 8192]),
                        ("wspec", [128, 8192]), ("y1", [128, 4096]),
                        ("y3", [128, 4096]), ("groups", [128, 2048]),
                        ("scores", [1, 256]), ("mask", [128, 2]),
                        ("outg", [128, 2048]), ("x1c0", [128, 4096]),
                        ("fsl", [384, 8192])]:
        if name in dbg:
            dbg_t[name] = nc.dram_tensor("dbg_" + name, shape, F32,
                                         kind="ExternalOutput").ap()

    ex = []

    def sbuf(name, shape, dtype):
        h = nc.sbuf_tensor(name, shape, dtype)
        tt = h.__enter__()
        ex.append(h)
        return tt

    def psumt(name, shape):
        h = nc.psum_tensor(name, shape, F32)
        tt = h.__enter__()
        ex.append(h)
        return tt

    xt = sbuf("xt", [128, NKC * 4096], BF16)
    w1s = sbuf("w1s", [128, 128], BF16)
    w2s = sbuf("w2s", [128, 16384], BF16)
    wi1s = sbuf("wi1s", [128, 192], BF16)
    wi2s = sbuf("wi2s", [128, 4096], BF16)
    idents = sbuf("idents", [128, 128], F32)
    tile = [sbuf(f"tile{i}", [128, 4096], BF16) for i in range(4)]
    absb = sbuf("absb", [128, 8192], BF16)
    wsb = sbuf("wsb", [128, 8192], BF16)
    fsb = [sbuf(f"fsb{i}", [128, 1024], BF16) for i in range(2)]
    wps = [sbuf(f"wps{i}", [128, 2048], BF16) for i in range(2)]
    m128 = sbuf("m128", [128, 4096], BF16)
    r128 = sbuf("r128", [128, 4096], BF16)
    onesb = sbuf("onesb", [128, 256], BF16)
    ones32 = sbuf("ones32", [128, 128], F32)
    epsc = sbuf("epsc", [128, 1], F32)
    rowf = sbuf("rowf", [1, 512], BF16)
    ws2s = sbuf("ws2s", [128, 2], BF16)
    st12 = sbuf("st12", [128, 128], F32)

    pp = [psumt(f"pp{i}", [128, 512]) for i in range(2)]
    pa = [psumt(f"pa{i}", [128, 512]) for i in range(2)]
    px = [psumt(f"px{d}", [128, 512]) for d in range(2)]
    pyp = [psumt(f"py{d}", [128, 256]) for d in range(2)]

    xv = [absb.ap().bitcast(F32), wsb.ap().bitcast(F32)]

    EV = {}
    cnt = {}
    streams = {k: [] for k in ("pe", "act", "dve", "pool", "sp")}

    anon = [0]
    WAITLOG = []

    def emit(engine, fn, waits=(), inc=None, n=1, fam=None, serial=False):
        w = []
        for x in waits:
            if x is None:
                continue
            WAITLOG.append(x if isinstance(x, str) else None)
            w.append(EV[x] if isinstance(x, str) else x)
        rec = None
        if inc is not None and inc.startswith("cc_done"):
            s = "s_" + inc
            cnt[s] = 1
            EV[inc] = (s, 1)
            streams[engine].append((fn, w, (s, 1), inc))
            return
        if n == 16:
            if inc is None:
                anon[0] += 1
                inc = "_anon%d" % anon[0]
            s = "d_" + (fam if fam is not None else engine + "_anon")
            cnt[s] = cnt.get(s, 0) + 16
            EV[inc] = (s, cnt[s])
            streams[engine].append((fn, w, (s, 16), inc))
            return
        if engine in ("dve", "act"):
            s = "s_" + engine
            prev = cnt.get(s, 0)
            tgt = prev if serial else prev - 1
            if tgt >= 1:
                w.append((s, tgt))
            if inc is None:
                anon[0] += 1
                inc = "_anon%d" % anon[0]
        if inc is not None:
            s = "s_" + engine
            cnt[s] = cnt.get(s, 0) + n
            EV[inc] = (s, cnt[s])
            rec = (s, n)
        streams[engine].append((fn, w, rec, inc))

    def ev_now(engine):
        s = "s_" + engine
        return (s, cnt.get(s, 0))

    def dma(engine, dst, src, waits=(), inc=None, fam=None):
        emit(engine, (lambda d=dst, s=src: (lambda eng: eng.dma_start(d, s)))(),
             waits, inc, n=16, fam=fam)

    emit("dve", lambda eng: eng.memset(onesb[:, :], 1.0), [], None)
    emit("dve", lambda eng: eng.memset(ones32[:, :], 1.0), [], None)
    emit("dve", lambda eng: eng.memset(epsc[:, :], 1e-5), [], "ones_set")
    emit("dve", lambda eng: eng.memset(
        bass.AP(xt, 0, [[NKC * 4096, 1], [1, 8192]]), 0), [], "stz0")
    dma("sp", w1s[:, :], w1c, [], "ld_w1", fam="c0")
    dma("sp", wi1s[:, :], wi1c, [], "ld_wi1", fam="c0")
    dma("sp", wi2s[:, :], wi2c, [], "ld_wi2", fam="c0")
    dma("sp", idents[:, :], identc, [], "ld_id", fam="c0")
    for _nm in ("ld_w1", "ld_wi1", "ld_wi2", "ld_id"):
        EV[_nm] = EV["ld_id"]
    dma("pool", rowf[0:1, 0:384], bf_sl, [], "ld_bf", fam="bf")
    dma("pool", ws2s[:, :], ws2t_d, [], "ld_ws2", fam="ws2")
    for kc in range(NKC):
        dma("pool", bass.AP(w2s, kc * 2048, [[16384, 128], [1, 2048]]),
            pos_d[kc * 128:(kc + 1) * 128, :], [], "ld_pos%d" % kc, fam="pos")
        dma("pool", bass.AP(tile[3], kc * 384, [[4096, 128], [1, 384]]),
            wf_sl[kc * 128:(kc + 1) * 128, :], [], "ld_wf%d" % kc, fam="wf")

    rowx = lambda off, n: bass.AP(xt, off, [[NKC * 4096, 1], [1, n]])
    strot = [pa[0], pa[1], px[0], px[1]]
    for kc in range(NKC):
        bufi = kc % 2
        w = ["st_dv_%d" % (kc - 2)] if kc >= 2 else []
        xq = "sp" if (kc // 2) % 2 == 0 else "pool"
        dma(xq, xv[bufi][:, :], x_nat[kc * 128:(kc + 1) * 128, :], w, "ld_x%d" % kc,
            fam="x%d%d" % (bufi, (kc // 2) % 2))
        emit("dve",
             (lambda b=bufi: (lambda eng: eng.tensor_copy(tile[0][:, :], xv[b][:, :])))(),
             ["ld_x%d" % kc, "ones_set", "stz0"]
             + (["st_pe_%d" % (kc - 1)] if kc >= 1 else []), "cast_%d" % kc)
        emit("act",
             (lambda b=bufi: (lambda eng: eng.activation(tile[1][:, :], xv[b][:, :],
                                                         AF.Square)))(),
             ["ld_x%d" % kc] + (["st_pe_%d" % (kc - 1)] if kc >= 1 else []),
             "sqr_%d" % kc)
        for nsl in range(8):
            for mom in range(2):
                ridx = (nsl * 2 + mom) % 4
                def mk(kc=kc, nsl=nsl, mom=mom, ridx=ridx):
                    def fn(eng):
                        return eng.matmul(strot[ridx][:, :], onesb[:, 0:128],
                                          tile[mom][:, nsl * 512:(nsl + 1) * 512],
                                          start=True, stop=True)
                    return fn
                w = ["cast_%d" % kc, "sqr_%d" % kc] if (nsl == 0 and mom == 0) else []
                if (kc, nsl, mom) != (0, 0, 0):
                    pr = (kc * 16 + nsl * 2 + mom) - 4
                    if pr >= 0:
                        w.append("st_add_%d" % pr)
                emit("pe", mk(), w, "st_mm_%d" % (kc * 16 + nsl * 2 + mom))
                def mka(kc=kc, nsl=nsl, mom=mom, ridx=ridx):
                    def fn(eng):
                        return eng.tensor_add(rowx(mom * 4096 + nsl * 512, 512),
                                              rowx(mom * 4096 + nsl * 512, 512),
                                              strot[ridx][0:1, :])
                    return fn
                emit("dve", mka(), ["st_mm_%d" % (kc * 16 + nsl * 2 + mom)],
                     "st_add_%d" % (kc * 16 + nsl * 2 + mom))
        EV["st_pe_%d" % kc] = ev_now("pe")
        EV["st_dv_%d" % kc] = ev_now("dve")
    EV["stats_done"] = ev_now("dve")
    trow = lambda off, n: bass.AP(tile[0], off, [[4096, 1], [1, n]])
    emit("dve", lambda eng: eng.tensor_scalar_mul(rowx(0, 4096), rowx(0, 4096),
                                                  -1.0 / 1024.0), ["stats_done"], None,
         serial=True)
    emit("dve", lambda eng: eng.tensor_scalar_mul(rowx(4096, 4096), rowx(4096, 4096),
                                                  1.0 / 1024.0), [], None, serial=True)
    emit("dve", lambda eng: eng.tensor_mul(trow(0, 4096), rowx(0, 4096), rowx(0, 4096)),
         [], None, serial=True)
    emit("dve", lambda eng: eng.tensor_sub(rowx(4096, 4096), rowx(4096, 4096),
                                           trow(0, 4096)), [], "var_done", serial=True)
    emit("act", lambda eng: eng.activation(rowx(4096, 4096), rowx(4096, 4096), AF.Sqrt,
                                           bias=epsc[0:1, 0:1]), ["var_done"], "sd_done")
    emit("dve", lambda eng: eng.reciprocal(rowx(4096, 4096), rowx(4096, 4096)),
         ["sd_done"], "r_done")
    for half, dst in ((0, m128), (1, r128)):
        for nsl in range(8):
            def mkb(half=half, nsl=nsl):
                def fn(eng):
                    return eng.matmul(pp[nsl % 2][:, :], onesb[0:1, 0:128],
                                      rowx(half * 4096 + nsl * 512, 512),
                                      start=True, stop=True)
                return fn
            w = ["r_done"] if (half == 0 and nsl == 0) else []
            if nsl >= 2:
                w.append("bc_%d_%d" % (half, nsl - 2))
            elif half == 1:
                w.append("bc_0_%d" % (6 + nsl))
            emit("pe", mkb(), w, "bcm_%d_%d" % (half, nsl))
            def mkbe(half=half, nsl=nsl, dst=dst):
                def fn(eng):
                    return eng.activation(dst[:, nsl * 512:(nsl + 1) * 512],
                                          pp[nsl % 2][:, :], AF.Identity)
                return fn
            emit("act", mkbe(), ["bcm_%d_%d" % (half, nsl)], "bc_%d_%d" % (half, nsl))
    emit("dve", lambda eng: eng.tensor_mul(m128[:, :], m128[:, :], r128[:, :]),
         ["bc_1_7"], "mr_mul")
    scr_ap = lambda base: bass.AP(xt, base,
                                  [[NKC * 4096, 128], [32, 2], [1, 32], [64, 2], [128, 32]])
    for kc in range(NKC):
        bufi = kc % 2
        w = ["st_pe_%d" % (NKC - 1)] if kc < 2 else ["xt_%d" % (kc - 2)]
        if kc == 0:
            w = w + ["bcm_1_7", "stats_done"]
        xq = "sp" if (kc // 2) % 2 == 0 else "pool"
        dma(xq, xv[bufi][:, :], x_nat[kc * 128:(kc + 1) * 128, :], w, "l2x_%d" % kc,
            fam="x%d%d" % (bufi, (kc // 2) % 2))
        emit("dve", (lambda kc=kc, b=bufi: (lambda eng: eng.tensor_mul(
            scr_ap(kc * 4096), xv[b][:, :], r128.ap())))(),
            ["l2x_%d" % kc, "mr_mul"], None)
        emit("dve", (lambda kc=kc: (lambda eng: eng.tensor_add(
            scr_ap(kc * 4096), scr_ap(kc * 4096), m128.ap())))(), [], "xt_%d" % kc,
            serial=True)
    EV["xt_all"] = ev_now("dve")
    dma("pool", bass.AP(m128, 64 * 4096, [[4096, 1], [1, 4096]]), bp_d,
        ["xt_all"], "ld_bp", fam="bp")
    if "xtilde" in dbg_t:
        dma("pool", dbg_t["xtilde"], xt.ap(), ["xt_all"], "d_xt", fam="dbgxt")

    for s in range(3):
        emit("dve", (lambda s=s: (lambda eng: eng.memset(tile[s][:, :], 0)))(),
             ["st_pe_%d" % (NKC - 1), "stats_done"], "fz_%d" % s)
    for seti in range(64):
        u32, uh = seti // 2, seti % 2
        rot = seti % 2
        for kc in range(NKC):
            def mk(u32=u32, uh=uh, kc=kc, rot=rot):
                def fn(eng):
                    return eng.matmul(
                        pp[rot][uh * 64:uh * 64 + 32, 0:384],
                        bass.AP(w2s, kc * 2048 + u32 * 64 + uh * 32,
                                [[16384, 128], [1, 32]]),
                        bass.AP(tile[3], kc * 384, [[4096, 128], [1, 384]]),
                        start=(kc == 0), stop=False, tile_position=(0, uh * 64))
                return fn
            w = []
            if kc == 0:
                if seti == 0:
                    w = ["ld_pos%d" % (NKC - 1), "ld_wf%d" % (NKC - 1), "bc_1_7"]
                if seti >= 2:
                    w.append("fpe_%d" % (seti - 2))
            emit("pe", mk(), w, None)
        def mkb(uh=uh, rot=rot):
            def fn(eng):
                return eng.matmul(pp[rot][uh * 64:uh * 64 + 32, 0:384],
                                  onesb[0:1, 0:32], rowf[0:1, 0:384],
                                  start=False, stop=True, tile_position=(0, uh * 64))
            return fn
        emit("pe", mkb(), ["ld_bf"] if seti == 0 else [], "fpm_%d" % seti)
        for s in range(3):
            def mke(u32=u32, uh=uh, s=s, rot=rot):
                def fn(eng):
                    dst = bass.AP(tile[s], (uh * 64) * 4096 + u32, [[4096, 32], [32, 128]])
                    src = bass.AP(pp[rot], (uh * 64) * 512 + s * 128, [[512, 32], [1, 128]])
                    return eng.activation(dst, src, AF.Identity)
                return fn
            emit("act", mke(), ["fpm_%d" % seti, "fz_%d" % s],
                 "fpe_%d" % seti if s == 2 else None)
    EV["fproj_done"] = ev_now("act")
    dma("sp", w2s[:, :], w2c, ["fproj_done"], "ld_w2", fam="w2")

    def emit_fwd(name, ytile, l1_waits, fscale):
        for dd in range(128):
            rot = (dd // 8) % 2
            blk = (dd // 2) % 4
            for uh in range(2):
                base = (dd % 2) * 64 + uh * 32
                def mk(dd=dd, uh=uh, base=base, rot=rot, blk=blk, yt=ytile):
                    def fn(eng):
                        return eng.matmul(
                            pa[rot][base:base + 32, blk * 128:blk * 128 + 128],
                            yt[uh * 64:(uh + 1) * 64, dd * 32:(dd + 1) * 32],
                            w1s[uh * 64:(uh + 1) * 64, :],
                            start=True, stop=True, tile_position=(uh * 64, base))
                    return fn
                w = []
                if uh == 0 and dd % 8 == 0:
                    if dd == 0:
                        w = list(l1_waits) + ["ld_w1"]
                    if dd >= 16:
                        w.append("%s_ae_%d" % (name, dd // 8 - 2))
                emit("pe", mk(), w,
                     "%s_l1_%d" % (name, dd // 8) if (dd % 8 == 7 and uh == 1) else None)
            if dd % 8 == 7:
                g = dd // 8
                def mke(g=g, rot=(g % 2)):
                    def fn(eng):
                        dst = bass.AP(absb, g * 4, [[8192, 128], [1, 4], [64, 128]])
                        src = bass.AP(pa[rot], 0, [[512, 128], [128, 4], [1, 128]])
                        if g % 2 == 0 or "a" not in EVD:
                            return eng.activation(dst, src, AF.Identity)
                        return eng.tensor_copy(dst, src)
                    return fn
                emit("act" if (g % 2 == 0 or "a" not in EVD) else "dve", mke(),
                     ["%s_l1_%d" % (name, g)], "%s_ae_%d" % (name, g))
        for kmq in range(32):
            rot = kmq % 2
            for dpar in range(2):
                for kmi in range(2):
                    km = kmq * 2 + kmi
                    for riout in range(2):
                        for ri in range(2):
                            def mk(km=km, dpar=dpar, ri=ri, riout=riout,
                                   rot=rot, kmi=kmi):
                                def fn(eng):
                                    return eng.matmul(
                                        px[rot][dpar * 64:(dpar + 1) * 64,
                                                (kmi * 2 + riout) * 64:(kmi * 2 + riout) * 64 + 64],
                                        w2s[dpar * 64:(dpar + 1) * 64,
                                            km * 256 + ri * 128 + riout * 64:
                                            km * 256 + ri * 128 + riout * 64 + 64],
                                        absb[dpar * 64:(dpar + 1) * 64,
                                             (ri * 64 + km) * 64:(ri * 64 + km) * 64 + 64],
                                        start=(ri == 0), stop=(ri == 1),
                                        tile_position=(dpar * 64, dpar * 64))
                                return fn
                            w = []
                            if kmi == 0 and ri == 0 and riout == 0 and dpar == 0:
                                if kmq == 0:
                                    w = ["%s_ae_%d" % (name, g) for g in (14, 15)] + ["ld_w2"]
                                if kmq >= 2:
                                    w += ["%s_ze_%d" % (name, kmq - 2)]
                            emit("pe", mk(), w,
                                 "%s_l2_%d" % (name, kmq)
                                 if (kmi == 1 and riout == 1 and ri == 1 and dpar == 1) else None)
            def mkz(kmq=kmq, rot=rot, fs=fscale, kmi=None):
                def fn(eng):
                    km0 = kmq * 2
                    if kmi is None:
                        dst = bass.AP(wsb, km0,
                                      [[8192, 128], [1, 2], [64, 2], [128, 64]])
                        src = bass.AP(px[rot], 0,
                                      [[512, 128], [128, 2], [64, 2], [1, 64]])
                        if fs:
                            return eng.activation(dst, src, AF.Identity,
                                                  scale=1.0 / 4096.0)
                        return eng.activation(dst, src, AF.Identity)
                    dst = bass.AP(wsb, km0 + kmi, [[8192, 128], [64, 2], [128, 64]])
                    src = bass.AP(px[rot], kmi * 128, [[512, 128], [64, 2], [1, 64]])
                    if fs:
                        return eng.tensor_scalar_mul(dst, src, 1.0 / 4096.0)
                    return eng.tensor_copy(dst, src)
                return fn
            if kmq % 2 == 0 or "z" not in EVD:
                emit("act", mkz(), ["%s_l2_%d" % (name, kmq)], "%s_ze_%d" % (name, kmq))
            else:
                emit("dve", mkz(kmi=0), ["%s_l2_%d" % (name, kmq)], None)
                emit("dve", mkz(kmi=1), [], "%s_ze_%d" % (name, kmq))
        EV["%s_zdone" % name] = EV["%s_ze_30" % name]
        EV["%s_zdoned" % name] = EV["%s_ze_31" % name]

    def emit_filter(name, c, s, first_waits):
        def zre(e):
            return bass.AP(wsb, e * 1024, [[8192, 128], [128, 8], [1, 64]])
        def zim(e):
            return bass.AP(wsb, e * 1024 + 64, [[8192, 128], [128, 8], [1, 64]])
        def fre(fb):
            return bass.AP(fsb[fb], 0, [[1024, 128], [128, 8], [1, 64]])
        def fim(fb):
            return bass.AP(fsb[fb], 64, [[1024, 128], [128, 8], [1, 64]])
        t1 = bass.AP(r128, 0, [[4096, 128], [1, 512]])
        t2 = bass.AP(r128, 512, [[4096, 128], [1, 512]])
        for e in range(8):
            fb = e % 2
            w = []
            if e < 2:
                w = list(first_waits)
                if not (c == 0 and s == 0):
                    pcs = (c, s - 1) if s > 0 else (c - 1, 2)
                    w.append("m%d_%d_fi_%d" % (pcs[0], pcs[1], 6 + e))
            else:
                w.append("%s_fi_%d" % (name, e - 2))
            def mkd(c=c, s=s, e=e, fb=fb):
                def fn(eng):
                    rows = f_full[s][c * 128:(c + 1) * 128,
                                     e * 1024:(e + 1) * 1024]
                    return eng.dma_start(fsb[fb][:, :], rows)
                return fn
            emit("sp", mkd(), w, "%s_fl_%d" % (name, e), n=16, fam="fb%d" % fb)
            w = ["%s_zdone" % name, "%s_zdoned" % name, "%s_fl_%d" % (name, e)]
            emit("dve", (lambda fb=fb, e=e, t1=t1: (lambda eng: eng.tensor_mul(
                t1, zim(e), fim(fb))))(), w, None, serial=True)
            emit("dve", (lambda fb=fb, e=e, t2=t2: (lambda eng: eng.tensor_mul(
                t2, zim(e), fre(fb))))(), [], None)
            emit("dve", (lambda fb=fb, e=e: (lambda eng: eng.tensor_mul(
                zim(e), zre(e), fim(fb))))(), [], None, serial=True)
            emit("dve", (lambda e=e, t2=t2: (lambda eng: eng.tensor_add(
                zim(e), zim(e), t2)))(), [], None, serial=True)
            emit("dve", (lambda fb=fb, e=e: (lambda eng: eng.tensor_mul(
                zre(e), zre(e), fre(fb))))(), [], None, serial=True)
            emit("dve", (lambda e=e, t1=t1: (lambda eng: eng.tensor_sub(
                zre(e), zre(e), t1)))(), [], "%s_fi_%d" % (name, e), serial=True)
        EV["%s_filt" % name] = ev_now("dve")

    def emit_inv(name, ytile_next, gtile, gate_ready=None):
        part_of = {(0, 0): 0, (0, 1): 64, (1, 0): 128, (1, 1): 0}
        for dd in range(128):
            rot = (dd // 8) % 2
            blk = (dd // 2) % 4
            dpar = dd % 2
            tq = dd // 2
            for riout in range(2):
                for riin in range(2):
                    def mk(dd=dd, rot=rot, blk=blk, dpar=dpar, tq=tq,
                           riin=riin, riout=riout):
                        def fn(eng):
                            pt = pa[rot] if dpar == 0 else pyp[rot]
                            return eng.matmul(
                                pt[riout * 64:riout * 64 + 64,
                                   blk * 64:blk * 64 + 64],
                                bass.AP(wsb, tq * 128 + riin * 64 + dpar * 64 * 8192,
                                        [[8192, 64], [1, 64]]),
                                wi1s[dpar * 64:(dpar + 1) * 64,
                                     part_of[(riin, riout)]:part_of[(riin, riout)] + 64],
                                start=(riin == 0), stop=(riin == 1),
                                tile_position=(dpar * 64, riout * 64))
                        return fn
                    w = []
                    if dd % 8 == 0 and riout == 0 and riin == 0:
                        w = ["%s_fi_%d" % (name, dd // 16), "ld_wi1"]
                        if dd >= 16:
                            w.append("%s_be_%d" % (name, dd // 8 - 2))
                            w.append("%s_b0_%d" % (name, dd // 8 - 2))
                    emit("pe", mk(), w,
                         "%s_i1_%d" % (name, dd // 8)
                         if (dd % 8 == 7 and riout == 1 and riin == 1) else None)
            if dd % 8 == 7:
                g = dd // 8
                def mke0(g=g, rot=(g % 2)):
                    def fn(eng):
                        dst = bass.AP(absb, g * 8, [[8192, 128], [2, 4], [128, 64]])
                        src = bass.AP(pa[rot], 0, [[512, 128], [64, 4], [1, 64]])
                        return eng.activation(dst, src, AF.Identity)
                    return fn
                def mke1(g=g, rot=(g % 2)):
                    def fn(eng):
                        dst = bass.AP(absb, g * 8 + 1, [[8192, 128], [2, 4], [128, 64]])
                        src = bass.AP(pyp[rot], 0, [[256, 128], [64, 4], [1, 64]])
                        if "i" not in EVD:
                            return eng.activation(dst, src, AF.Identity)
                        return eng.tensor_copy(dst, src)
                    return fn
                emit("act", mke0(), ["%s_i1_%d" % (name, g)], "%s_b0_%d" % (name, g))
                emit("dve" if "i" in EVD else "act", mke1(),
                     ["%s_i1_%d" % (name, g)], "%s_be_%d" % (name, g))
        for tpq in range(16):
            rot = tpq % 2
            for tpi in range(2):
                tp = tpq * 2 + tpi
                for member in range(2):
                    tum = tp + 32 * member
                    def mk(tpi=tpi, member=member, tum=tum, rot=rot):
                        def fn(eng):
                            return eng.matmul(
                                pyp[rot][member * 64:member * 64 + 64,
                                         tpi * 128:tpi * 128 + 128],
                                wi2s[:, tum * 64:tum * 64 + 64],
                                bass.AP(absb, tum * 128, [[8192, 128], [1, 128]]),
                                start=True, stop=True,
                                tile_position=(0, member * 64))
                        return fn
                    w = []
                    if tpi == 0 and member == 0:
                        if tpq == 0:
                            w = (["%s_be_%d" % (name, g) for g in (14, 15)]
                                 + ["%s_b0_%d" % (name, g) for g in (14, 15)]
                                 + ["ld_wi2"])
                        if tpq >= 2:
                            w.append("%s_ye_%d" % (name, tpq - 2))
                    emit("pe", mk(), w,
                         "%s_i2_%d" % (name, tpq)
                         if (tpi == 1 and member == 1) else None)
            def mkg(tpq=tpq, rot=rot, yt=ytile_next, gt=gtile):
                def fn(eng):
                    tp0 = tpq * 2
                    src = bass.AP(pyp[rot], 0, [[256, 128], [128, 2], [1, 128]])
                    dstv = bass.AP(yt, tp0, [[4096, 128], [1, 2], [32, 128]])
                    gv = bass.AP(gt, tp0, [[4096, 128], [1, 2], [32, 128]])
                    return eng.tensor_mul(dstv, src, gv)
                return fn
            gw = ["%s_i2_%d" % (name, tpq)]
            if tpq == 0 and gate_ready:
                gw += list(gate_ready)
            emit("dve", mkg(), gw, "%s_ye_%d" % (name, tpq))
        EV["%s_done" % name] = ev_now("dve")

    proj_last_mm = {}
    proj_last_ev = {}

    def emit_proj(c, sp, dst0, dst1, mm_waits, evac_waits):
        wbuf = (c * 2 + sp) % 2
        for kc in range(NKC):
            def mkd(c=c, sp=sp, kc=kc, wbuf=wbuf):
                def fn(eng):
                    cols = bass.AP(wp.tensor, (kc * 128) * 4096 + (sp * 2) * 1024 + c * 128,
                                   [[4096, 128], [1024, 2], [1, 128]])
                    dst = bass.AP(wps[wbuf], kc * 256, [[2048, 128], [128, 2], [1, 128]])
                    return eng.dma_start(dst, cols)
                return fn
            w = []
            if kc == 0:
                prev = proj_last_mm.get((c * 2 + sp) - 2)
                if prev:
                    w.append(prev)
            emit("pool", mkd(), w, "wpl_%d_%d" % (c, sp) if kc == NKC - 1 else None,
                 n=16, fam="wps%d" % wbuf)
        for q in range(16):
            rot = q % 2
            for si in range(2):
                u32 = q * 2 + si
                for kc in range(NKC):
                    def mk(u32=u32, kc=kc, si=si, rot=rot, wbuf=wbuf):
                        def fn(eng):
                            return eng.matmul(
                                pp[rot][:, si * 256:si * 256 + 256],
                                bass.AP(xt, kc * 4096 + u32 * 128,
                                        [[NKC * 4096, 128], [1, 128]]),
                                bass.AP(wps[wbuf], kc * 256, [[2048, 128], [1, 256]]),
                                start=(kc == 0), stop=False)
                        return fn
                    w = []
                    if kc == 0 and si == 0:
                        if q == 0:
                            w = list(mm_waits) + ["wpl_%d_%d" % (c, sp), "xt_all", "ld_bp"]
                            prev = proj_last_ev.get((c * 2 + sp) - 1)
                            if prev:
                                w += list(prev)
                            else:
                                w.append("fpe_63")
                        if q >= 2:
                            w.append("pq_%d_%d_%d_0" % (c, sp, q - 2))
                            w.append("pq_%d_%d_%d_1" % (c, sp, q - 2))
                    emit("pe", mk(), w, None)
                def mkb(si=si, rot=rot, c=c, sp=sp):
                    def fn(eng):
                        bcols = bass.AP(m128, 64 * 4096 + (sp * 2) * 1024 + c * 128,
                                        [[4096, 1], [1024, 2], [1, 128]])
                        return eng.matmul(pp[rot][:, si * 256:si * 256 + 256],
                                          onesb[64:65, 0:128], bcols,
                                          start=False, stop=True)
                    return fn
                emit("pe", mkb(), [], "pj_%d_%d_%d" % (c, sp, q) if si == 1 else None)
            for s_i, dstt in ((0, dst0), (1, dst1)):
                def mke(q=q, s_i=s_i, dstt=dstt, rot=rot, h=None):
                    def fn(eng):
                        if h is None:
                            dst = bass.AP(dstt, q * 2, [[4096, 128], [1, 2], [32, 128]])
                            src = bass.AP(pp[rot], s_i * 128,
                                          [[512, 128], [256, 2], [1, 128]])
                            return eng.activation(dst, src, AF.Identity)
                        dst = bass.AP(dstt, q * 2 + h, [[4096, 128], [32, 128]])
                        src = bass.AP(pp[rot], s_i * 128 + h * 256,
                                      [[512, 128], [1, 128]])
                        return eng.tensor_copy(dst, src)
                    return fn
                w = ["pj_%d_%d_%d" % (c, sp, q)]
                if q == 0:
                    w += list(evac_waits)
                if s_i == 0 or "p" not in EVD:
                    emit("act", mke(), w, "pq_%d_%d_%d_%d" % (c, sp, q, s_i))
                else:
                    emit("dve", mke(h=0), w, None)
                    emit("dve", mke(h=1), [], "pq_%d_%d_%d_%d" % (c, sp, q, s_i))
        proj_last_mm[c * 2 + sp] = "pj_%d_%d_15" % (c, sp)
        proj_last_ev[c * 2 + sp] = ("pq_%d_%d_15_0" % (c, sp),
                                    "pq_%d_%d_15_1" % (c, sp))
        EV["proj_%d_%d" % (c, sp)] = ev_now("act")
        EV["projd_%d_%d" % (c, sp)] = ev_now("dve")

    for s in range(3):
        l1w = ["fpe_63"]
        if s > 0:
            l1w.append("fsl_%d" % (s - 1))
        emit_fwd("f%d" % s, tile[s], l1w, True)
        dma("sp", f_slice[s * 128:(s + 1) * 128, :], wsb.ap(),
            ["f%d_zdone" % s, "f%d_zdoned" % s], "fsl_%d" % s, fam="fsl")
        emit("pool", (lambda s=s: (lambda eng: eng.collective_compute(
            "AllGather", ALU.bypass, replica_groups=[list(range(NCORE))],
            ins=[f_slice[s * 128:(s + 1) * 128, :]], outs=[f_full[s]])))(),
            ["fsl_%d" % s], "cc_done%d" % s, n=16)
    if "fsl" in dbg_t:
        dma("pool", dbg_t["fsl"], f_slice, ["cc_done%d" % 2], "d_fsl", fam="dbgfsl")

    for c in range(NC):
        evw = ["f2_l1_15"] if c == 0 else ["y3s_%d" % (c - 1)]
        emit_proj(c, 0, tile[0], tile[1], [], evw)
        ytiles = [tile[0], tile[0], tile[1], tile[0]]
        gtiles = [None, tile[1], tile[2], tile[3]]
        for s in range(3):
            nm = "m%d_%d" % (c, s)
            if s == 0:
                l1w = ["proj_%d_0" % c, "projd_%d_0" % c]
                if c == 0:
                    l1w.append("f2_zdone")
                    l1w.append("f2_zdoned")
                else:
                    l1w.append("m%d_2_done" % (c - 1))
                    l1w.append("m%d_2_b0_15" % (c - 1))
            else:
                l1w = ["m%d_%d_done" % (c, s - 1), "m%d_%d_b0_15" % (c, s - 1)]
            emit_fwd(nm, ytiles[s], l1w, False)
            fw0 = []
            if c == 0:
                fw0.append("cc_done%d" % s)
                if s == 0:
                    fw0.append("xt_all")
            emit_filter(nm, c, s, fw0)
            emit_inv(nm, ytiles[s + 1], gtiles[s + 1],
                     gate_ready=[("proj_%d_0" % c, "projd_%d_0" % c),
                                 ("proj_%d_1" % c, "projd_%d_1" % c)][0 if s == 0 else 1])
            if s == 0:
                ew = ["bc_1_7"] if c == 0 else [
                    "m%d_1_ye_15" % (c - 1), "m%d_2_ye_15" % (c - 1)]
                emit_proj(c, 1, tile[2], tile[3], ["m%d_0_ae_15" % c], ew)
        def mky3(c=c):
            def fn(eng):
                return eng.dma_start(y3_dram[c * 128:(c + 1) * 128, :], tile[0].ap())
            return fn
        emit("sp", mky3(), ["m%d_2_done" % c], "y3s_%d" % c, n=16, fam="y3s")
    if "y3" in dbg_t:
        dma("pool", dbg_t["y3"], tile[0].ap(), ["y3s_%d" % (NC - 1)], "d_y3", fam="dbgy3")

    FDN = "m%d_2_done" % (NC - 1)
    emit("dve", lambda eng: eng.memset(st12[:, :], 0), [FDN], "stz")
    for c in range(NC):
        w = ["proj_%d_1" % (NC - 1), "projd_%d_1" % (NC - 1)]
        if c == 0:
            w.append("y3s_%d" % (NC - 1))
        w.append(FDN if c == 0 else "stp_%d" % (c - 1))
        dma("sp", xv[0][:, :], x_til[c * 128:(c + 1) * 128, :], w, "ltx_%d" % c,
            fam="xa")
        def mkyl(c=c):
            def fn(eng):
                return eng.dma_start(bass.AP(wsb, 0, [[8192, 128], [1, 4096]]),
                                     y3_dram[c * 128:(c + 1) * 128, :])
            return fn
        wy = ["m%d_2_i1_15" % (NC - 1), "m%d_2_fi_7" % (NC - 1),
              "y3s_%d" % (NC - 1)] if c == 0 else ["x1_%d" % (c - 1)]
        emit("sp", mkyl(), wy, "lty_%d" % c, n=16, fam="y3l")
        def mkx1(c=c):
            def fn(eng):
                dst = bass.AP(xt, c * 4096, [[NKC * 4096, 128], [1, 4096]])
                return eng.tensor_add(dst, xv[0][:, :],
                                      bass.AP(wsb, 0, [[8192, 128], [1, 4096]]))
            return fn
        wx1 = ["ltx_%d" % c, "lty_%d" % c, "stz"]
        if c == 0 and "xtilde" in dbg_t:
            wx1.append("d_xt")
        emit("dve", mkx1(), wx1, "x1_%d" % c)
        emit("dve", (lambda c=c: (lambda eng: eng.tensor_reduce(
            m128[:, 0:32], bass.AP(xt, c * 4096, [[NKC * 4096, 128], [1, 32], [32, 128]]),
            AX.X, ALU.add)))(), [], None, serial=True)
        emit("dve", lambda eng: eng.tensor_add(st12[:, 0:32], st12[:, 0:32],
                                               m128[:, 0:32]), [], None, serial=True)
        emit("act", (lambda c=c: (lambda eng: eng.activation(
            r128[:, :], bass.AP(xt, c * 4096, [[NKC * 4096, 128], [1, 4096]]),
            AF.Square)))(), ["x1_%d" % c, "m%d_2_fi_7" % (NC - 1)], "sq2_%d" % c)
        emit("dve", (lambda: (lambda eng: eng.tensor_reduce(
            m128[:, 32:64], bass.AP(r128, 0, [[4096, 128], [1, 32], [32, 128]]),
            AX.X, ALU.add)))(), ["sq2_%d" % c], None, serial=True)
        emit("dve", (lambda: (lambda eng: eng.tensor_add(
            st12[:, 32:64], st12[:, 32:64], m128[:, 32:64])))(), [], "stp_%d" % c,
            serial=True)
    if "x1c0" in dbg_t:
        dma("pool", dbg_t["x1c0"], bass.AP(xt, 0, [[NKC * 4096, 128], [1, 4096]]),
            ["stp_%d" % (NC - 1)], "d_x1", fam="dbgx1")
    emit("dve", lambda eng: eng.tensor_scalar_mul(st12[:, 64:96], st12[:, 0:32],
                                                  -1.0 / 1024.0),
         ["stp_%d" % (NC - 1)], None, serial=True)
    emit("dve", lambda eng: eng.tensor_scalar_mul(st12[:, 96:128], st12[:, 32:64],
                                                  1.0 / 1024.0), [], None, serial=True)
    emit("dve", lambda eng: eng.tensor_mul(st12[:, 0:32], st12[:, 64:96],
                                           st12[:, 64:96]), [], None, serial=True)
    emit("dve", lambda eng: eng.tensor_sub(st12[:, 96:128], st12[:, 96:128],
                                           st12[:, 0:32]), [], "var2", serial=True)
    emit("act", lambda eng: eng.activation(st12[:, 96:128], st12[:, 96:128],
                                           AF.Sqrt, bias=epsc[:, 0:1]), ["var2"], "sd2")
    emit("dve", lambda eng: eng.reciprocal(st12[:, 96:128], st12[:, 96:128]),
         ["sd2"], "mr2")

    for c in range(NC):
        for u32 in range(32):
            def mkts(c=c, u32=u32):
                def fn(eng):
                    src = bass.AP(xt, c * 4096 + u32, [[NKC * 4096, 128], [32, 128]])
                    dst = bass.AP(wsb, 4096 + u32, [[8192, 128], [32, 128]])
                    return eng.tensor_scalar(dst, src,
                                             st12[:, 64 + u32:65 + u32],
                                             st12[:, 96 + u32:97 + u32],
                                             ALU.add, ALU.mult)
                return fn
            w = []
            if u32 == 0:
                w = ["mr2"] if c == 0 else ["grd_%d" % (c - 1)]
                if c == 0:
                    w.append("lty_%d" % (NC - 1))
            emit("dve", mkts(), w, None)
        emit("dve", lambda eng: eng.tensor_reduce(
            m128[:, 0:256],
            bass.AP(wsb, 4096, [[8192, 128], [32, 128], [16, 2], [1, 16]]),
            AX.X, ALU.add), [], None, serial=True)
        emit("dve", lambda eng: eng.tensor_scalar_mul(
            m128[:, 0:256], m128[:, 0:256], 1.0 / 16.0), [], "grd_%d" % c,
            serial=True)
        ltw = ["ltx_%d" % (NC - 1), "x1_%d" % (NC - 1)] if c == 0 \
            else ["gt_%d_0" % (c - 1), "gt_%d_1" % (c - 1)]
        emit("dve", (lambda c=c: (lambda eng: eng.tensor_copy(
            xv[0][:, 0:256], m128[:, 0:256])))(), ltw, "gcp_%d" % c, serial=True)
        for gh in range(2):
            def mkt(c=c, gh=gh):
                def fn(eng):
                    return eng.transpose(pp[gh][:, 0:128], xv[0][:, gh:256:2],
                                         idents[:, :])
                return fn
            w = ["gcp_%d" % c, "ld_id"]
            if c >= 1:
                w.append("ge_%d_%d" % (c - 1, gh))
            else:
                w.append("proj_%d_1" % (NC - 1))
                w.append("projd_%d_1" % (NC - 1))
            emit("pe", mkt(), w, "gt_%d_%d" % (c, gh))
            def mkte(c=c, gh=gh):
                def fn(eng):
                    dst = bass.AP(w2s, c * 256 + gh * 64,
                                  [[16384, 128], [32, 2], [128, 2], [1, 32]])
                    src = bass.AP(pp[gh], 0, [[512, 128], [1, 128]])
                    return eng.activation(dst, src, AF.Identity)
                return fn
            emit("act", mkte(), ["gt_%d_%d" % (c, gh)], "ge_%d_%d" % (c, gh))
    EV["groups_done"] = EV["ge_%d_1" % (NC - 1)]
    if "groups" in dbg_t:
        dma("pool", dbg_t["groups"], bass.AP(w2s, 0, [[16384, 128], [1, 2048]]),
            ["groups_done"], "d_gr", fam="dbggr")

    wstream_prev = ["groups_done"]

    def stream_w_mms(tag, w_dram, psum_of, evac_name, rhs_fn, emit_evacs):
        for half in range(2):
            for kc in range(NKC):
                wbuf = kc % 2
                if kc >= 2:
                    wld = ["%s_mm_%d_%d" % (tag, half, kc - 2)]
                elif half > 0:
                    wld = ["%s_mm_%d_%d" % (tag, half - 1, NKC - 2 + kc)]
                else:
                    wld = list(wstream_prev)
                def mkd(kc=kc, half=half, wbuf=wbuf, w_dram=w_dram):
                    def fn(eng):
                        return eng.dma_start(
                            bass.AP(wps[wbuf], 0, [[2048, 128], [1, 512]]),
                            w_dram[kc * 128:(kc + 1) * 128, half * 512:(half + 1) * 512])
                    return fn
                emit("pool", mkd(), wld, "%s_ld_%d_%d" % (tag, half, kc), n=16,
                     fam="wps%d" % wbuf)
                for dc4 in range(4):
                    def mkm(kc=kc, half=half, dc4=dc4, wbuf=wbuf):
                        def fn(eng):
                            return eng.matmul(
                                psum_of(half, dc4),
                                bass.AP(wps[wbuf], dc4 * 128, [[2048, 128], [1, 128]]),
                                rhs_fn(kc),
                                start=(kc == 0), stop=(kc == NKC - 1))
                        return fn
                    w = []
                    if dc4 == 0:
                        w = ["%s_ld_%d_%d" % (tag, half, kc)]
                        if kc == 0 and half > 0:
                            w += [evac_name(half - 1, j) for j in range(4)]
                    emit("pe", mkm(), w,
                         "%s_mm_%d_%d" % (tag, half, kc) if dc4 == 3 else None)
            emit_evacs(half)
        wstream_prev[:] = [evac_name(1, j) for j in range(4)]

    PS4 = [pa[0], pa[1], px[0], px[1]]
    qkoff = {0: 2048, 1: 4096}
    for wi, wdr in ((0, wq_d), (1, wk_d)):
        def psof(half, dc4):
            return PS4[dc4][:, 0:256]
        evn = (lambda wi: (lambda half, dc4: "qk%d_ev_%d_%d" % (wi, half, dc4)))(wi)
        def emit_evacs(half, wi=wi):
            for dc4 in range(4):
                dc = half * 4 + dc4
                def mke(wi=wi, dc=dc, dc4=dc4):
                    def fn(eng):
                        sc = 0.125 if wi == 0 else 1.0
                        dst = bass.AP(w2s, qkoff[wi] + dc * 256, [[16384, 128], [1, 256]])
                        return eng.activation(
                            dst, PS4[dc4][:, 0:256],
                            AF.Identity, scale=sc)
                    return fn
                emit("act", mke(), ["qk%d_mm_%d_%d" % (wi, half, NKC - 1)],
                     "qk%d_ev_%d_%d" % (wi, half, dc4))
        stream_w_mms("qk%d" % wi, wdr, psof, evn,
                     lambda kc: bass.AP(w2s, kc * 256, [[16384, 128], [1, 256]]),
                     emit_evacs)
    for sb_ in range(2):
        for half in range(2):
            for kc in range(NKC):
                wbuf = kc % 2
                if kc >= 2:
                    wld = ["v_mm_%d_%d_%d" % (sb_, half, kc - 2)]
                elif (sb_, half) != (0, 0):
                    ph = (sb_, half - 1) if half > 0 else (sb_ - 1, 1)
                    wld = ["v_mm_%d_%d_%d" % (ph[0], ph[1], NKC - 2 + kc)]
                else:
                    wld = ["qk1_mm_1_%d" % (NKC - 2 + kc)]
                def mkd(kc=kc, half=half, wbuf=wbuf):
                    def fn(eng):
                        return eng.dma_start(
                            bass.AP(wps[wbuf], 0, [[2048, 128], [1, 512]]),
                            wv_d[kc * 128:(kc + 1) * 128, half * 512:(half + 1) * 512])
                    return fn
                emit("pool", mkd(), wld, "v_ld_%d_%d_%d" % (sb_, half, kc), n=16,
                     fam="wps%d" % wbuf)
                def mkm(kc=kc, sb_=sb_, half=half, wbuf=wbuf):
                    def fn(eng):
                        return eng.matmul(
                            px[half][:, 0:512],
                            bass.AP(w2s, kc * 256 + sb_ * 128, [[16384, 128], [1, 128]]),
                            bass.AP(wps[wbuf], 0, [[2048, 128], [1, 512]]),
                            start=(kc == 0), stop=(kc == NKC - 1))
                    return fn
                vw = ["v_ld_%d_%d_%d" % (sb_, half, kc)]
                if kc == 0 and sb_ > 0:
                    vw.append("v_ev_%d_%d" % (sb_ - 1, half))
                if kc == 0 and (sb_, half) == (0, 0):
                    vw += ["qk1_ev_1_%d" % j for j in range(4)]
                emit("pe", mkm(), vw, "v_mm_%d_%d_%d" % (sb_, half, kc))
            def mkev(sb_=sb_, half=half):
                def fn(eng):
                    dst = bass.AP(w2s, 6144 + sb_ * 1024 + half * 512,
                                  [[16384, 128], [1, 512]])
                    return eng.activation(dst, px[half][:, 0:512], AF.Identity)
                return fn
            emit("act", mkev(), ["v_mm_%d_%d_%d" % (sb_, half, NKC - 1)],
                 "v_ev_%d_%d" % (sb_, half))
    for kc in range(NKC):
        wbuf = kc % 2
        wld = ["v_mm_1_1_%d" % (NKC - 2 + kc)] if kc < 2 else ["h_mm_%d" % (kc - 2)]
        def mkd(kc=kc, wbuf=wbuf):
            def fn(eng):
                return eng.dma_start(bass.AP(wps[wbuf], 0, [[2048, 128], [1, 256]]),
                                     ws1_d[kc * 128:(kc + 1) * 128, :])
            return fn
        emit("pool", mkd(), wld, "h_ld_%d" % kc, n=16, fam="wps%d" % wbuf)
        for hf in range(2):
            def mkm(kc=kc, hf=hf, wbuf=wbuf):
                def fn(eng):
                    return eng.matmul(
                        pyp[hf][:, 0:256],
                        bass.AP(wps[wbuf], hf * 128, [[2048, 128], [1, 128]]),
                        bass.AP(w2s, kc * 256, [[16384, 128], [1, 256]]),
                        start=(kc == 0), stop=False)
                return fn
            w = ["h_ld_%d" % kc]
            if kc == 0 and hf == 0:
                w += ["m%d_2_ye_15" % (NC - 1)]
            emit("pe", mkm(), w if hf == 0 else [],
                 "h_mm_%d" % kc if hf == 1 else None)
    dma("pool", rowf[0:1, 0:256], bs1_d, ["fpe_63"], "ld_bs1", fam="bs1")
    dma("pool", rowf[0:1, 256:257], bs2_d, [], "ld_bs2", fam="bs2")
    for hf in range(2):
        def mkhb(hf=hf):
            def fn(eng):
                return eng.matmul(pyp[hf][:, 0:256],
                                  rowf[0:1, hf * 128:hf * 128 + 128],
                                  onesb[0:1, 0:256], start=False, stop=True)
            return fn
        emit("pe", mkhb(), ["ld_bs1", "h_mm_%d" % (NKC - 1)] if hf == 0 else [],
             "hb_%d" % hf)
        def mkhe(hf=hf):
            def fn(eng):
                return eng.activation(bass.AP(wi2s, hf * 256, [[4096, 128], [1, 256]]),
                                      pyp[hf][:, 0:256], AF.Relu)
            return fn
        emit("act", mkhe(), ["hb_%d" % hf], "hev_%d" % hf)
    for kc in range(2):
        def mks(kc=kc):
            def fn(eng):
                return eng.matmul(pp[0][0:1, 0:256], ws2s[:, kc:kc + 1],
                                  bass.AP(wi2s, kc * 256, [[4096, 128], [1, 256]]),
                                  start=(kc == 0), stop=(kc == 1))
            return fn
        emit("pe", mks(), ["hev_0", "hev_1", "ld_ws2"] if kc == 0 else [],
             "smm" if kc == 1 else None)
    scrow = lambda a, n: xv[0][0:1, 1024 + a:1024 + a + n]
    emit("act", lambda eng: eng.activation(scrow(0, 256), pp[0][0:1, 0:256],
                                           AF.Identity, bias=rowf[0:1, 256:257]),
         ["smm", "ld_bs2", "gcp_%d" % (NC - 1)], "sc_done")
    if "scores" in dbg_t:
        dma("pool", dbg_t["scores"], scrow(0, 256), ["sc_done"], "d_sc", fam="dbgsc")

    XA = lambda a, n=1: xv[0][:, 2048 + a:2048 + a + n]
    for b in range(2):
        def mksc(b=b):
            def fn(eng):
                return eng.matmul(pp[1][:, 0:1], scrow(b * 128, 128),
                                  ones32[0:1, 0:1], start=True, stop=True)
            return fn
        emit("pe", mksc(), ["sc_done"], "scol_%d" % b)
        emit("act", (lambda b=b: (lambda eng: eng.activation(
            XA(8 + b), pp[1][:, 0:1], AF.Identity)))(), ["scol_%d" % b], "scv_%d" % b)
        def mksr(b=b):
            def fn(eng):
                return eng.matmul(pp[1][:, 128:256], ones32[0:1, 0:128],
                                  scrow(b * 128, 128), start=True, stop=True)
            return fn
        emit("pe", mksr(), ["scv_%d" % b], "srow_%d" % b)
        emit("act", (lambda b=b: (lambda eng: eng.activation(
            XA(256, 128), pp[1][:, 128:256], AF.Identity)))(),
            ["srow_%d" % b], "srv_%d" % b)
        emit("dve", (lambda b=b: (lambda eng: eng.tensor_tensor(
            XA(384, 128), XA(256, 128),
            XA(8 + b, 1).broadcast_to([128, 128]), ALU.is_gt)))(),
            ["srv_%d" % b], None)
        emit("dve", (lambda b=b: (lambda eng: eng.tensor_reduce(
            XA(10 + b), XA(384, 128), AX.X, ALU.add)))(), [], None, serial=True)
        emit("dve", (lambda b=b: (lambda eng: eng.tensor_single_scalar(
            XA(0 + b), XA(10 + b), 63.5, ALU.is_lt)))(), [], "msk_%d" % b, serial=True)
        emit("dve", (lambda b=b: (lambda eng: eng.tensor_scalar(
            XA(2 + b), XA(0 + b), -1.0, 1e9, ALU.add, ALU.mult)))(), [], "pen_%d" % b,
            serial=True)
        def mkpt(b=b):
            def fn(eng):
                return eng.transpose(pp[1][0:1, 256:384], XA(2 + b), idents[:, :])
            return fn
        emit("pe", mkpt(), ["pen_%d" % b], "pt_%d" % b)
        emit("act", (lambda b=b: (lambda eng: eng.activation(
            scrow(256 + b * 128, 128), pp[1][0:1, 256:384], AF.Identity)))(),
            ["pt_%d" % b], "penr_%d" % b)
    if "mask" in dbg_t:
        dma("pool", dbg_t["mask"], xv[0][:, 2048:2050], ["msk_0", "msk_1"], "d_mk", fam="dbgmk")

    for b in range(2):
        for h in range(NH):
            hc, hb = h // 2, (h % 2) * 64
            nm = "at_%d_%d" % (b, h)
            qcol = 2048 + hc * 256 + b * 128
            kcol = 4096 + hc * 256 + b * 128
            if b == 0 and h == 0:
                w = ["v_ev_1_1", "penr_0", "penr_1", "qk0_ev_1_3", "qk1_ev_1_3"]
            else:
                pb, ph_ = (b, h - 1) if h > 0 else (b - 1, NH - 1)
                w = ["at_%d_%d_ctx" % (pb, ph_)]
            def mks(hb=hb, qcol=qcol, kcol=kcol):
                def fn(eng):
                    return eng.matmul(
                        px[0][:, 0:128],
                        bass.AP(w2s, hb * 16384 + qcol, [[16384, 64], [1, 128]]),
                        bass.AP(w2s, hb * 16384 + kcol, [[16384, 64], [1, 128]]),
                        start=True, stop=False, tile_position=(hb, 0))
                return fn
            emit("pe", mks(), w, None)
            def mkp(b=b):
                def fn(eng):
                    return eng.matmul(px[0][:, 0:128], ones32[0:1, 0:128],
                                      scrow(256 + b * 128, 128), start=False, stop=True)
                return fn
            emit("pe", mkp(), [], "%s_s" % nm)
            emit("dve", (lambda nm=nm: (lambda eng: eng.tensor_reduce(
                XA(4), px[0][:, 0:128], AX.X, ALU.max, negate=True)))(),
                ["%s_s" % nm], "%s_nm" % nm)
            def mknt(nm=nm):
                def fn(eng):
                    return eng.transpose(pp[1][0:1, 384:512], XA(4), idents[:, :])
                return fn
            emit("pe", mknt(), ["%s_nm" % nm], "%s_nt" % nm)
            emit("act", (lambda nm=nm: (lambda eng: eng.activation(
                scrow(640, 128), pp[1][0:1, 384:512], AF.Identity)))(),
                ["%s_nt" % nm], "%s_nr" % nm)
            def mkst(hb=hb, qcol=qcol, kcol=kcol):
                def fn(eng):
                    return eng.matmul(
                        px[1][:, 0:128],
                        bass.AP(w2s, hb * 16384 + kcol, [[16384, 64], [1, 128]]),
                        bass.AP(w2s, hb * 16384 + qcol, [[16384, 64], [1, 128]]),
                        start=True, stop=False, tile_position=(hb, 0))
                return fn
            emit("pe", mkst(), ["%s_nr" % nm], None)
            def mkst2(nm=nm):
                def fn(eng):
                    return eng.matmul(px[1][:, 0:128], ones32[0:1, 0:128],
                                      scrow(640, 128), start=False, stop=True)
                return fn
            emit("pe", mkst2(), [], "%s_st" % nm)
            def mkexp(nm=nm, b=b):
                def fn(eng):
                    return eng.activation(
                        bass.AP(wi2s, 512, [[4096, 128], [1, 128]]),
                        px[1][:, 0:128], AF.Exp, bias=XA(2 + b))
                return fn
            emit("act", mkexp(), ["%s_st" % nm], "%s_e" % nm)
            def mksum(nm=nm):
                def fn(eng):
                    return eng.matmul(pp[1][:, 0:1],
                                      bass.AP(wi2s, 512, [[4096, 128], [1, 128]]),
                                      onesb[:, 0:1], start=True, stop=True)
                return fn
            emit("pe", mksum(), ["%s_e" % nm], "%s_sum" % nm)
            emit("dve", (lambda: (lambda eng: eng.reciprocal(
                XA(5), pp[1][:, 0:1])))(), ["%s_sum" % nm], None)
            emit("dve", (lambda b=b: (lambda eng: eng.tensor_mul(
                XA(6), XA(5), XA(0 + b))))(), [], "%s_rs" % nm, serial=True)
            def mkc(b=b, h=h):
                def fn(eng):
                    return eng.matmul(
                        pp[0][:, 256:320],
                        bass.AP(wi2s, 512, [[4096, 128], [1, 128]]),
                        bass.AP(w2s, 6144 + b * 1024 + h * 64, [[16384, 128], [1, 64]]),
                        start=True, stop=True)
                return fn
            emit("pe", mkc(), ["%s_rs" % nm], "%s_cm" % nm)
            def mkce(b=b, h=h, nm=nm):
                def fn(eng):
                    return eng.activation(
                        xv[1][:, b * 1024 + h * 64:b * 1024 + h * 64 + 64],
                        pp[0][:, 256:320], AF.Identity, scale=XA(6))
                return fn
            cw = ["%s_cm" % nm]
            if b == 0 and h == 0:
                cw.append("grd_%d" % (NC - 1))
            emit("act", mkce(), cw, "%s_ctx" % nm)
    for b in range(2):
        for dc in range(8):
            def mkt(b=b, dc=dc):
                def fn(eng):
                    return eng.transpose(
                        pp[dc % 2][:, 128:256],
                        xv[1][:, b * 1024 + dc * 128:b * 1024 + dc * 128 + 128],
                        idents[:, :])
                return fn
            w = ["at_%d_%d_ctx" % (b, NH - 1)] if dc == 0 else []
            if dc >= 2:
                w.append("ctf_%d_%d" % (b, dc - 2))
            elif b == 1:
                w.append("ctf_0_%d" % (6 + dc))
            emit("pe", mkt(), w, "ctt_%d_%d" % (b, dc))
            def mkte(b=b, dc=dc):
                def fn(eng):
                    dst = bass.AP(w2s, 8192 + dc * 256 + b * 128, [[16384, 128], [1, 128]])
                    return eng.activation(dst, pp[dc % 2][:, 128:256], AF.Identity)
                return fn
            emit("act", mkte(), ["ctt_%d_%d" % (b, dc)], "ctf_%d_%d" % (b, dc))
    EV["ctxfm_done"] = ev_now("act")
    def wo_psum(half, dc4):
        return PS4[dc4][:, 0:256]
    def wo_emit_evacs(half):
        for dc4 in range(4):
            dc = half * 4 + dc4
            def fn_(eng, dc=dc, dc4=dc4):
                dst = bass.AP(w2s, 10240 + dc * 256, [[16384, 128], [1, 256]])
                return eng.activation(dst, wo_psum(0, dc4), AF.Identity)
            emit("act", fn_, ["wo_mm_%d_%d" % (half, NKC - 1)],
                 "wo_ev_%d_%d" % (half, dc4))
    wstream_prev[:] = ["ctxfm_done"]
    stream_w_mms("wo", wo_d, wo_psum, (lambda half, dc4: "wo_ev_%d_%d" % (half, dc4)),
                 lambda kc: bass.AP(w2s, 8192 + kc * 256, [[16384, 128], [1, 256]]),
                 wo_emit_evacs)
    if "outg" in dbg_t:
        dma("pool", dbg_t["outg"], bass.AP(w2s, 10240, [[16384, 128], [1, 2048]]),
            ["wo_ev_1_3"], "d_og", fam="dbgog")
    for c in range(NC):
        for gh in range(2):
            def mkcp(c=c, gh=gh):
                def fn(eng):
                    src = bass.AP(w2s, 10240 + c * 256 + gh * 64,
                                  [[16384, 128], [32, 2], [128, 2], [1, 32]])
                    return eng.tensor_copy(xv[1][:, 2048 + gh * 128:2048 + gh * 128 + 128],
                                           src)
                return fn
            w = ["wo_ev_1_3"] if (c == 0 and gh == 0) else []
            if c >= 1 and gh == 0:
                w = ["og_%d_1" % (c - 1)]
            emit("dve", mkcp(), w, "ocp_%d_%d" % (c, gh))
            def mkt(c=c, gh=gh):
                def fn(eng):
                    return eng.transpose(pp[gh][:, 128:256],
                                         xv[1][:, 2048 + gh * 128:2048 + gh * 128 + 128],
                                         idents[:, :])
                return fn
            w = ["ocp_%d_%d" % (c, gh)]
            if c >= 1:
                w.append("og_%d_%d" % (c - 1, gh))
            emit("pe", mkt(), w, "ogt_%d_%d" % (c, gh))
            def mkte(c=c, gh=gh):
                def fn(eng):
                    dst = bass.AP(w2s, 12288 + c * 256 + gh, [[16384, 128], [2, 128]])
                    return eng.activation(dst, pp[gh][:, 128:256], AF.Identity)
                return fn
            emit("act", mkte(), ["ogt_%d_%d" % (c, gh)], "og_%d_%d" % (c, gh))
    ovw = xv[0].rearrange("p (dd gh u) -> p dd gh u", gh=2, u=16)
    for c in range(NC):
        for u16 in range(16):
            def mko(c=c, u16=u16):
                def fn(eng):
                    x1v = bass.AP(xt, c * 4096 + u16, [[NKC * 4096, 128], [32, 128], [16, 2]])
                    ogv = bass.AP(w2s, 12288 + c * 256, [[16384, 128], [2, 128], [1, 2]])
                    dst = ovw[:, :, :, u16]
                    return eng.tensor_add(dst, x1v, ogv)
                return fn
            w = []
            if u16 == 0:
                w = ["og_%d_0" % c, "og_%d_1" % c]
                if c == 0:
                    w += ["msk_0", "msk_1", "at_1_%d_ctx" % (NH - 1), "sc_done",
                          "penr_0", "penr_1"]
                    if "scores" in dbg_t:
                        w.append("d_sc")
                    if "mask" in dbg_t:
                        w.append("d_mk")
                if c >= 1:
                    w.append("ost_%d" % (c - 1))
            emit("dve", mko(), w, "oadd_%d" % c if u16 == 15 else None)
        def mkst(c=c):
            def fn(eng):
                return eng.dma_start(out_t[c * 128:(c + 1) * 128, :], xv[0][:, :])
            return fn
        emit("sp", mkst(), ["oadd_%d" % c], "ost_%d" % c, n=16, fam="out")
    drain = ["ost_%d" % (NC - 1)]
    for s in list(cnt):
        if s.startswith("d_dbg"):
            drain.append((s, cnt[s]))
    emit("sp", lambda eng: eng.nop(), drain, None)
    for e in ("pe", "act", "dve", "pool"):
        emit(e, lambda eng: eng.nop(), drain, None)

    def check_sync():
        ptr = {e: 0 for e in streams}
        eclk = {e: {} for e in streams}
        prod_clk = {}
        made = {}
        dma_info = []
        wait_reg = []
        progress = True
        nleft = sum(len(s) for s in streams.values())
        while nleft and progress:
            progress = False
            for e in streams:
                while ptr[e] < len(streams[e]):
                    fn, w, rec, inc = streams[e][ptr[e]]
                    ok = True
                    for (s, v) in w:
                        vs = made.get(s, [])
                        if not vs or vs[-1] < v:
                            ok = False
                            break
                    if not ok:
                        break
                    clk = eclk[e]
                    for (s, v) in w:
                        import bisect
                        vs = made[s]
                        pv = vs[bisect.bisect_left(vs, v)]
                        for k2, v2 in prod_clk[(s, pv)].items():
                            if clk.get(k2, 0) < v2:
                                clk[k2] = v2
                        if clk.get(s, 0) < v:
                            clk[s] = v
                        wait_reg.append((s, v, e, ptr[e]))
                    if rec is not None:
                        s, n_ = rec
                        if s.startswith("d_") or s.startswith("s_cc"):
                            dma_info.append((s, EV[inc][1] if inc else None,
                                             clk.get(s, 0), inc, e))
                        val = EV[inc][1] if inc else None
                        if val is None:
                            val = made[s][-1] + n_ if s in made else n_
                        cc = dict(clk)
                        cc[s] = max(cc.get(s, 0), val)
                        prod_clk[(s, val)] = cc
                        made.setdefault(s, []).append(val)
                        clk[s] = max(clk.get(s, 0), val) if not s.startswith("d_") \
                            else clk.get(s, 0)
                    ptr[e] += 1
                    nleft -= 1
                    progress = True
        if nleft:
            stuck = {e: (ptr[e], streams[e][ptr[e]][1] if ptr[e] < len(streams[e])
                         else None) for e in streams if ptr[e] < len(streams[e])}
            raise RuntimeError("check_sync DEADLOCK: %r (made=%r)"
                               % (stuck, {k: v[-3:] for k, v in made.items()}))
        waited = {}
        for (s, v, we, wi) in wait_reg:
            if s.startswith("d_"):
                waited.setdefault(s, set()).add(v)
        bad = []
        for (s, val, wmark, inc, e) in dma_info:
            if s.startswith("s_cc") or val is None:
                continue
            for v in waited.get(s, ()):
                if val > v and wmark < v:
                    bad.append((s, "waited", v, "later dma", inc, "at", val,
                                "issue watermark", wmark))
        if bad:
            raise RuntimeError("check_sync RACES:\n" +
                               "\n".join(str(b) for b in bad[:20]) +
                               "\n(%d total)" % len(bad))

    check_sync()

    semobjs = {}
    for s in cnt:
        semobjs[s] = nc.alloc_semaphore(name=s)

    import contextlib
    with nc.Block() as block, nc.allow_low_precision(reason="bf16 residual stream by design"):
        def mkbody(engname):
            def body(eng):
                for fn, waits, rec, _inc in streams[engname]:
                    for (s, v) in waits:
                        eng.wait_ge(semobjs[s], v)
                    ins = fn(eng)
                    if rec is not None and ins is not None:
                        ins.then_inc(semobjs[rec[0]], rec[1])
            return body
        block.tensor(mkbody("pe"))
        block.scalar(mkbody("act"))
        block.vector(mkbody("dve"))
        block.gpsimd(mkbody("pool"))
        block.sync(mkbody("sp"))

    for h in reversed(ex):
        h.__exit__(None, None, None)
    build_nc.last_cnt = dict(cnt)
    build_nc.last_EV = dict(EV)
    build_nc.last_waits = list(WAITLOG)
    return nc, dbg_t


def stage_inputs(inputs):
    X = np.asarray(inputs["L0_attn_X"])
    Wp = np.ascontiguousarray(inputs["L0_attn_Wp"], np.float32)
    bp = np.ascontiguousarray(inputs["L0_attn_bp"], np.float32).reshape(1, -1)
    Wf = np.ascontiguousarray(inputs["L0_attn_Wf"], np.float32)
    bf_ = np.ascontiguousarray(inputs["L0_attn_bf"], np.float32).reshape(1, -1)
    Wq = np.ascontiguousarray(inputs["L0_ffn_Wq"], np.float32)
    Wk = np.ascontiguousarray(inputs["L0_ffn_Wk"], np.float32)
    Wv = np.ascontiguousarray(inputs["L0_ffn_Wv"], np.float32)
    Wo = np.ascontiguousarray(inputs["L0_ffn_Wo"], np.float32)
    Ws1 = np.ascontiguousarray(inputs["L0_ffn_Ws1"], np.float32)
    bs1 = np.ascontiguousarray(inputs["L0_ffn_bs1"], np.float32).reshape(1, -1)
    Ws2 = np.ascontiguousarray(inputs["L0_ffn_Ws2"], np.float32)
    bs2 = np.ascontiguousarray(inputs["L0_ffn_bs2"], np.float32).reshape(1, 1)

    t = np.arange(T)
    v, u = t // 64, t % 64
    uh, u32 = u // 32, u % 32
    prow = uh * 64 + v
    w1 = build_w1()
    w2 = build_w2()
    wi1 = build_wi1()
    wi2 = build_wi2()
    pos_sc = build_pos_scrambled().astype(bf16)
    ident = np.eye(128, dtype=np.float32)
    ws2t = np.ascontiguousarray(Ws2.reshape(2, 128).T)

    in_maps = []
    cols = u32[:, None] + np.arange(128)[None, :] * 32
    for core in range(NCORE):
        Xc = np.ascontiguousarray(X[2 * core:2 * core + 2], np.float32)
        x_nat = np.ascontiguousarray(Xc.transpose(2, 0, 1).reshape(D, BPC * T))
        x_til = np.zeros((NC * 128, 4096), np.float32)
        for b in range(BPC):
            rows = prow + b * 32
            for c in range(NC):
                x_til[c * 128 + rows[:, None], cols] = Xc[b, :, c * 128:(c + 1) * 128]
        ccols = (np.arange(3)[:, None] * 1024 + core * 128 + np.arange(128)[None, :]).ravel()
        wf_slc = np.ascontiguousarray(Wf[:, ccols])
        bf_slc = np.ascontiguousarray(bf_[:, ccols])
        in_maps.append(dict(
            x_nat=x_nat, x_til=x_til, wp=Wp, bp_d=bp, wf_sl=wf_slc, bf_sl=bf_slc,
            pos_d=pos_sc, wq_d=Wq, wk_d=Wk, wv_d=Wv, wo_d=Wo, ws1_d=Ws1,
            bs1_d=bs1, ws2t_d=ws2t, bs2_d=bs2, w1c=w1, w2c=w2, wi1c=wi1,
            wi2c=wi2, identc=ident))
    return in_maps


def unscramble_out(res_list):
    t = np.arange(T)
    v, u = t // 64, t % 64
    uh, u32 = u // 32, u % 32
    prow = uh * 64 + v
    out = np.empty((2 * NCORE, T, D), np.float32)
    cols = u32[:, None] + np.arange(128)[None, :] * 32
    for core in range(NCORE):
        o = res_list[core]
        for b in range(BPC):
            rows = prow + b * 32
            for c in range(NC):
                out[2 * core + b, :, c * 128:(c + 1) * 128] = o[c * 128 + rows[:, None], cols]
    return out


def kernel(**inputs):
    dbg = tuple(x for x in os.environ.get("MK_DBG", "").split(",") if x)
    nc, dbg_t = build_nc(dbg)
    in_maps = stage_inputs(inputs)
    res = run_bass_kernel_spmd(nc, in_maps, core_ids=list(range(NCORE)))
    outs = [np.asarray(res.results[i]["out_t"]) for i in range(NCORE)]
    if dbg:
        kernel.dbg_results = res.results
    return unscramble_out(outs)
